# version 01
# baseline (speedup 1.0000x reference)
from contextlib import ExitStack

import numpy as np
import concourse.bass as bass
import concourse.mybir as mybir
from concourse.bass_utils import run_bass_kernel_spmd

F32 = mybir.dt.float32
F32R = mybir.dt.float32r
BF16 = mybir.dt.bfloat16
AF = mybir.ActivationFunctionType
ALU = mybir.AluOpType
AXX = mybir.AxisListType.X

NCORES = 8
SEQS = 2
T = 2048
D = 2048
HID = 8192
HEADS = 16
DC = D // 128
HC = HID // 128
KC = T // 128
TQ = 512
NT = T // TQ
SCALE = 1.0 / float(np.sqrt(128))
STAB = 12.0


class Evt:
    __slots__ = ("sem", "val", "eng")

    def __init__(self, sem, val, eng):
        self.sem, self.val, self.eng = sem, val, eng


class Plan:

    ENGS = ("sync", "gpsimd", "scalar", "vector", "tensor")
    DMАS = ()
    SEMS = ENGS + ("cst", "p0a", "p0b", "p0sa", "p0sb", "xin",
                   "wA", "wB", "wC", "wD", "wE", "wF",
                   "qs", "qlA", "qlB", "x1s", "outs", "mxs")

    def __init__(self, nc):
        self.nc = nc
        self.streams = {e: [] for e in self.ENGS}
        self.sems = {s: nc.alloc_semaphore("sem_" + s) for s in self.SEMS}
        self.counts = {s: 0 for s in self.SEMS}
        self.marks = {e: {} for e in self.ENGS}

    def _wait(self, eng, evt, force=False):
        if evt is None:
            return
        if evt.eng == eng and not force:
            return
        m = self.marks[eng]
        if m.get(evt.sem, 0) >= evt.val:
            return
        m[evt.sem] = evt.val
        sem_h, val = self.sems[evt.sem], evt.val
        self.streams[eng].append(lambda E, s=sem_h, v=val: E.wait_ge(s, v))

    def emit(self, eng, fn, waits=(), inc=0, sem=None):
        is_dma = sem is not None and sem != eng
        for w in waits:
            self._wait(eng, w, force=is_dma)
        if inc:
            key = sem or eng
            self.counts[key] += inc
            sem_h, val = self.sems[key], self.counts[key]

            def run(E, fn=fn, s=sem_h, k=inc):
                fn(E).then_inc(s, k)
            self.streams[eng].append(run)
            return Evt(key, val, key if key != eng else eng)
        self.streams[eng].append(fn)
        return None

    def fence(self):
        return [Evt(s, self.counts[s], s if s.endswith("_dma") else s)
                for s in self.SEMS if self.counts[s] > 0]

    def wait_fence(self, fevts):
        for e in self.ENGS:
            for ev in fevts:
                self._wait(e, ev)

    def flush(self, final=False):
        if not final:
            return
        used = [e for e in self.ENGS if self.streams[e]]
        if not used:
            return
        with self.nc.Block() as block:
            for e in used:
                fns = self.streams[e]

                def body(E, fns=fns):
                    for f in fns:
                        f(E)
                getattr(block, e)(body)
        self.streams = {e: [] for e in self.ENGS}


def dma(P, qeng, ch, out_ap, in_ap, waits=()):
    return P.emit(qeng, lambda E: E.dma_start(out=out_ap, in_=in_ap),
                  waits=waits, inc=16, sem=ch)


def mm(P, out_ap, lhsT, rhs, start, stop, waits=(), inc=0):
    return P.emit("tensor",
                  lambda E: E.matmul(out_ap, lhsT, rhs, start=start, stop=stop),
                  waits=waits, inc=inc)


def tr(P, out_ap, in_ap, ident, waits=()):
    return P.emit("tensor", lambda E: E.transpose(out_ap, in_ap, ident),
                  waits=waits, inc=1)


def act(P, out_ap, in_ap, func, bias=0.0, scale=1.0, waits=()):
    return P.emit("scalar",
                  lambda E: E.activation(out_ap, in_ap, func, bias=bias,
                                         scale=scale),
                  waits=waits, inc=1)


def vop(P, fn, waits=()):
    return P.emit("vector", fn, waits=waits, inc=1)


def build_nc(stop_after="ffn", dbg=False, paranoid=False):
    nc = bass.Bass("TRN2", target_bir_lowering=False,
                   dynamic_dma_scratch_size=8192,
                   detect_race_conditions=False)

    x_in = nc.dram_tensor("x_fm", [SEQS, D, T], F32, kind="ExternalInput").ap()
    qT_in = nc.dram_tensor("qT", [DC, 128, D], F32, kind="ExternalInput").ap()
    kT_in = nc.dram_tensor("kT", [1, 128, D], F32, kind="ExternalInput").ap()
    vT_in = nc.dram_tensor("vT", [1, 128, D], F32, kind="ExternalInput").ap()
    oT_in = nc.dram_tensor("oT", [DC, 128, D], F32, kind="ExternalInput").ap()
    gT_in = nc.dram_tensor("gT", [HC, 128, D], F32, kind="ExternalInput").ap()
    uT_in = nc.dram_tensor("uT", [HC, 128, D], F32, kind="ExternalInput").ap()
    dT_in = nc.dram_tensor("dT", [DC, 128, HID], F32, kind="ExternalInput").ap()
    out_hbm = nc.dram_tensor("out_fm", [SEQS, D, T], F32,
                             kind="ExternalOutput").ap()

    _dk = "ExternalOutput" if dbg else "Internal"
    dbg_rows = nc.dram_tensor("dbg_rows", [12, TQ], F32, kind=_dk).ap() if dbg \
        else None
    qTr_hi = nc.dram_tensor("qTr_hi", [DC, 128, D], F32R, kind=_dk).ap()
    qTr_lo = nc.dram_tensor("qTr_lo", [DC, 128, D], F32R, kind=_dk).ap()
    oTr_hi = nc.dram_tensor("oTr_hi", [DC, 128, D], F32R).ap()
    oTr_lo = nc.dram_tensor("oTr_lo", [DC, 128, D], F32R).ap()
    kTr_hi = nc.dram_tensor("kTr_hi", [1, 128, D], F32R).ap()
    kTr_lo = nc.dram_tensor("kTr_lo", [1, 128, D], F32R).ap()
    vTr_hi = nc.dram_tensor("vTr_hi", [1, 128, D], F32R).ap()
    vTr_lo = nc.dram_tensor("vTr_lo", [1, 128, D], F32R).ap()
    gTs = nc.dram_tensor("gTs", [HC, 128, D], BF16).ap()
    uTs = nc.dram_tensor("uTs", [HC, 128, D], BF16).ap()
    dTs = nc.dram_tensor("dTs", [DC, 128, HID], BF16).ap()
    dbg_kind = "ExternalOutput" if dbg else "Internal"
    q_s = nc.dram_tensor("q_s", [SEQS, D, T], F32, kind=dbg_kind).ap()
    x1_s = nc.dram_tensor("x1_s", [SEQS, D, T], F32, kind=dbg_kind).ap()

    ones128_h = nc.inline_tensor(np.ones((128, 1), np.float32), "ones128")
    onescol_h = nc.inline_tensor(np.ones((1, 128), np.float32), "onescol")
    ident_h = nc.inline_tensor(np.eye(128, dtype=np.float32), "ident128")
    stab_h = nc.inline_tensor(np.full((128, 1), -STAB, np.float32), "stab")
    scl_h = nc.inline_tensor(np.full((128, 1), SCALE, np.float32), "scl")

    P = Plan(nc)

    cst = nc.alloc_sbuf_tensor("cst_ones128", [128, 1], F32)
    cst_col = nc.alloc_sbuf_tensor("cst_onescol", [1, 128], F32)
    cst_id = nc.alloc_sbuf_tensor("cst_ident", [128, 128], F32)
    cst_stab = nc.alloc_sbuf_tensor("cst_stab", [128, 1], F32)
    cst_scl = nc.alloc_sbuf_tensor("cst_scl", [128, 1], F32)
    cst_col_r = nc.alloc_sbuf_tensor("cst_colr", [1, 128], F32R)
    cst_ones_r = nc.alloc_sbuf_tensor("cst_onesr", [128, 1], F32R)
    e_ones = dma(P, "sync", "cst", cst[:], ones128_h.ap())
    e_col = dma(P, "sync", "cst", cst_col[:], onescol_h.ap())
    e_id = dma(P, "sync", "cst", cst_id[:], ident_h.ap())
    e_stab = dma(P, "sync", "cst", cst_stab[:], stab_h.ap())
    e_scl = dma(P, "sync", "cst", cst_scl[:], scl_h.ap())
    e_colr = act(P, cst_col_r[:], cst_col[:], AF.Copy, waits=(e_col,))
    e_onesr = act(P, cst_ones_r[:], cst[:], AF.Copy, waits=(e_ones,))

    def bcast_r(ps_ap, row_ap, hi_ap, lo_ap, waits=()):
        eh = act(P, hi_ap, row_ap, AF.Copy, waits=waits)
        el = vop(P, lambda E, o=lo_ap, a=row_ap, b=hi_ap: E.tensor_sub(o, a, b),
                 waits=(eh,))
        mm(P, ps_ap, cst_col_r[:], hi_ap, start=True, stop=False,
           waits=(eh, e_colr))
        return mm(P, ps_ap, cst_col_r[:], lo_ap, start=False, stop=True,
                  waits=(el,), inc=1)

    def ln_rows_emit(P, x_tiles, x_evts, lb, eps):
        sq = (lb["sq0"], lb["sq1"])
        xh = (lb["xh0"], lb["xh1"])
        xl = (lb["xl0"], lb["xl1"])
        prev_rows = lb.get("prev_rows") or ()
        mmev = lb["_smm"]
        e_m = None
        for c in range(DC):
            w = [x_evts[c]]
            if ("m", c - 2) in mmev:
                w.append(mmev[("m", c - 2)])
            eh = act(P, xh[c % 2][:], x_tiles[c], AF.Copy, waits=tuple(w))
            el = vop(P, lambda E, o=xl[c % 2][:], a=x_tiles[c], b=xh[c % 2][:]:
                     E.tensor_sub(o, a, b), waits=(eh,))
            mm(P, lb["mean_ps"][0:1, :], cst_ones_r[:], xh[c % 2][:],
               start=(c == 0), stop=False,
               waits=(eh, e_onesr) + (tuple(prev_rows) if c == 0 else ()))
            e_m = mm(P, lb["mean_ps"][0:1, :], cst_ones_r[:], xl[c % 2][:],
                     start=False, stop=(c == DC - 1), waits=(el,), inc=1)
            mmev[("m", c)] = e_m
        e_s = None
        for c in range(DC):
            w = [x_evts[c]]
            if ("s", c - 2) in mmev:
                w.append(mmev[("s", c - 2)])
            elif c < 2:
                w.append(mmev[("m", DC - 2 + c)])
            e_sq = act(P, sq[c % 2][:], x_tiles[c], AF.Square, waits=tuple(w))
            eh = act(P, xh[c % 2][:], sq[c % 2][:], AF.Copy, waits=(e_sq,))
            el = vop(P, lambda E, o=xl[c % 2][:], a=sq[c % 2][:],
                     b=xh[c % 2][:]: E.tensor_sub(o, a, b), waits=(eh,))
            mm(P, lb["sq_ps"][0:1, :], cst_ones_r[:], xh[c % 2][:],
               start=(c == 0), stop=False,
               waits=(eh,) + (tuple(prev_rows) if c == 0 else ()))
            e_s = mm(P, lb["sq_ps"][0:1, :], cst_ones_r[:], xl[c % 2][:],
                     start=False, stop=(c == DC - 1), waits=(el,), inc=1)
            mmev[("s", c)] = e_s
        m_row, v_row, mm_row = lb["m_row"], lb["v_row"], lb["mm_row"]
        em = vop(P, lambda E: E.tensor_scalar_mul(m_row[:], lb["mean_ps"][0:1, :],
                                                  1.0 / D), waits=(e_m,))
        es = vop(P, lambda E: E.tensor_scalar_mul(v_row[:], lb["sq_ps"][0:1, :],
                                                  1.0 / D), waits=(e_s,))
        e1 = vop(P, lambda E: E.tensor_mul(mm_row[:], m_row[:], m_row[:]),
                 waits=(em,))
        e2 = vop(P, lambda E: E.tensor_sub(v_row[:], v_row[:], mm_row[:]),
                 waits=(es, e1))
        e2b = vop(P, lambda E: E.tensor_scalar_add(v_row[:], v_row[:],
                                                   float(eps)), waits=(e2,))
        e2c = vop(P, lambda E: E.reciprocal(v_row[:], v_row[:]), waits=(e2b,))
        er = act(P, v_row[:], v_row[:], AF.Sqrt, waits=(e2c,))
        emb = bcast_r(lb["mb_ps"][:], m_row[:], lb["mh"][:], lb["ml"][:],
                      waits=(em,))
        erb = bcast_r(lb["rb_ps"][:], v_row[:], lb["vh"][:], lb["vl"][:],
                      waits=(er,))
        lb["prev_rows"] = (em, es)
        return emb, erb

    def mk_ln_bufs(stk, tag):
        return {
            "sq0": stk.enter_context(nc.sbuf_tensor(tag + "_sq0", [128, TQ], F32)),
            "sq1": stk.enter_context(nc.sbuf_tensor(tag + "_sq1", [128, TQ], F32)),
            "m_row": stk.enter_context(nc.sbuf_tensor(tag + "_m", [1, TQ], F32)),
            "v_row": stk.enter_context(nc.sbuf_tensor(tag + "_v", [1, TQ], F32)),
            "mm_row": stk.enter_context(nc.sbuf_tensor(tag + "_mm", [1, TQ], F32)),
            "xh0": stk.enter_context(nc.sbuf_tensor(tag + "_xh0", [128, TQ], F32R)),
            "xh1": stk.enter_context(nc.sbuf_tensor(tag + "_xh1", [128, TQ], F32R)),
            "xl0": stk.enter_context(nc.sbuf_tensor(tag + "_xl0", [128, TQ], F32R)),
            "xl1": stk.enter_context(nc.sbuf_tensor(tag + "_xl1", [128, TQ], F32R)),
            "mh": stk.enter_context(nc.sbuf_tensor(tag + "_mh", [1, TQ], F32R)),
            "ml": stk.enter_context(nc.sbuf_tensor(tag + "_ml", [1, TQ], F32R)),
            "vh": stk.enter_context(nc.sbuf_tensor(tag + "_vh", [1, TQ], F32R)),
            "vl": stk.enter_context(nc.sbuf_tensor(tag + "_vl", [1, TQ], F32R)),
            "mean_ps": stk.enter_context(nc.psum_tensor(tag + "_mps", [128, TQ], F32)),
            "sq_ps": stk.enter_context(nc.psum_tensor(tag + "_sps", [128, TQ], F32)),
            "mb_ps": stk.enter_context(nc.psum_tensor(tag + "_mb", [128, TQ], F32)),
            "rb_ps": stk.enter_context(nc.psum_tensor(tag + "_rb", [128, TQ], F32)),
            "_smm": {},
        }

    with (
        nc.sbuf_tensor("p0_in", [128, 2 * HID], F32) as p0_in,
        nc.sbuf_tensor("p0_hi", [128, 2 * D], F32R) as p0_hi,
        nc.sbuf_tensor("p0_lo", [128, 2 * D], F32R) as p0_lo,
        nc.sbuf_tensor("p0_sg", [128, 2 * HID], BF16) as p0_sg,
    ):
        attn_jobs = []
        for oc in range(DC):
            attn_jobs.append((qT_in[oc], qTr_hi[oc], qTr_lo[oc], D))
            attn_jobs.append((oT_in[oc], oTr_hi[oc], oTr_lo[oc], D))
        attn_jobs.append((kT_in[0], kTr_hi[0], kTr_lo[0], D))
        attn_jobs.append((vT_in[0], vTr_hi[0], vTr_lo[0], D))
        if stop_after == "p0s":
            attn_jobs = attn_jobs[:4]
        elif stop_after.startswith(("p0f", "p0d")):
            attn_jobs = []
        slot_ev = [None, None]
        for j, (src, dst_hi, dst_lo, w) in enumerate(attn_jobs):
            sl = j % 2
            off = sl * HID
            offd = sl * D
            iv, hv, lv = (p0_in[:, off:off + w], p0_hi[:, offd:offd + w],
                          p0_lo[:, offd:offd + w])
            ldch = "p0a" if sl == 0 else "p0b"
            stch = "p0sa" if sl == 0 else "p0sb"
            eld = dma(P, "sync", ldch, iv, src,
                      waits=(slot_ev[sl],) if slot_ev[sl] else ())
            ehi = act(P, hv, iv, AF.Copy, waits=(eld,))
            elo = vop(P, lambda E, o=lv, a=iv, b=hv: E.tensor_sub(o, a, b),
                      waits=(ehi,))
            dma(P, "scalar", stch, dst_hi, hv, waits=(ehi,))
            e2 = dma(P, "scalar", stch, dst_lo, lv, waits=(elo,))
            slot_ev[sl] = e2
        ffn_jobs = []
        for hc in range(HC):
            ffn_jobs.append((gT_in[hc], gTs[hc], D))
            ffn_jobs.append((uT_in[hc], uTs[hc], D))
        for oc in range(DC):
            ffn_jobs.append((dT_in[oc], dTs[oc], HID))
        if stop_after == "p0s":
            ffn_jobs = ffn_jobs[:4]
        elif stop_after == "p0a":
            ffn_jobs = []
        elif stop_after == "p0d":
            ffn_jobs = ffn_jobs[-DC:]
        elif stop_after == "p0f":
            ffn_jobs = ffn_jobs[:-DC]
        elif stop_after.startswith("p0f"):
            ffn_jobs = ffn_jobs[:int(stop_after[3:])]
        slot_ev2 = [None, None]
        for j, (src, dst, w) in enumerate(ffn_jobs):
            sl = j % 2
            off = sl * HID
            iv, sv = p0_in[:, off:off + w], p0_sg[:, off:off + w]
            ldch = "p0a" if sl == 0 else "p0b"
            stch = "p0sa" if sl == 0 else "p0sb"
            base = (slot_ev2[sl], slot_ev[sl]) if slot_ev2[sl] else \
                (slot_ev[sl],) if slot_ev[sl] else ()
            eld = dma(P, "sync", ldch, iv, src, waits=tuple(e for e in base if e))
            esg = act(P, sv, iv, AF.Sign, waits=(eld,))
            est = dma(P, "scalar", stch, dst, sv, waits=(esg,))
            slot_ev2[sl] = est
        p0_done_a = Evt("p0sa", P.counts["p0sa"], "p0sa")
        p0_done_b = Evt("p0sb", P.counts["p0sb"], "p0sb")
        P.flush()

    with (
        nc.sbuf_tensor("K_hi", [128, T], F32R) as K_hi,
        nc.sbuf_tensor("K_lo", [128, T], F32R) as K_lo,
        nc.sbuf_tensor("V_hi", [128, T], F32R) as V_hi,
        nc.sbuf_tensor("V_lo", [128, T], F32R) as V_lo,
    ):
        a1_q_done = {}
        x1_store_last = None

        for seq in ((1,) if stop_after == "a1x" else range(SEQS if not stop_after.startswith("p0") else 0)):
            with ExitStack() as stk:
                xa = [stk.enter_context(
                    nc.sbuf_tensor(f"a1x{seq}_{c}", [128, TQ], F32))
                    for c in range(DC)]
                lnh = stk.enter_context(
                    nc.sbuf_tensor(f"a1lnh{seq}", [128, DC * TQ], F32R))
                lnl = stk.enter_context(
                    nc.sbuf_tensor(f"a1lnl{seq}", [128, DC * TQ], F32R))
                lnt = [stk.enter_context(
                    nc.sbuf_tensor(f"a1lnt{seq}_{i}", [128, TQ], F32))
                    for i in range(2)]
                lb = mk_ln_bufs(stk, f"a1{seq}")
                wq = [stk.enter_context(
                    nc.sbuf_tensor(f"a1wq{seq}_{i}", [128, D], F32R))
                    for i in range(4)]
                ev_out = stk.enter_context(
                    nc.sbuf_tensor(f"a1ev{seq}", [128, TQ], F32))
                V_fm = stk.enter_context(
                    nc.sbuf_tensor(f"a1vfm{seq}", [128, T], F32))
                proj_ps = [stk.enter_context(
                    nc.psum_tensor(f"a1p{seq}_{i}", [128, TQ], F32))
                    for i in range(2)]
                tr_ps = stk.enter_context(
                    nc.psum_tensor(f"a1tr{seq}", [128, TQ], F32))

                wslot_ev = [None, None]
                psum_ev = [None, None]
                ev_store_ev = None
                kv_done = []
                for it in range(NT):
                    P.wait_fence(P.fence())
                    lb["_smm"] = {}
                    t0 = it * TQ
                    e_x = [dma(P, "scalar", "xin", xa[c][:],
                               x_in[seq, c * 128:(c + 1) * 128, t0:t0 + TQ])
                           for c in range(DC)]
                    e_x = [e_x[-1]] * DC
                    emb, erb = ln_rows_emit(
                        P, [xa[c][:] for c in range(DC)], e_x, lb, 1e-5)
                    if paranoid:
                        P.wait_fence(P.fence())
                    e_lh = [None] * DC
                    e_ll = [None] * DC
                    for c in range(DC):
                        hv = lnh[:, c * TQ:(c + 1) * TQ]
                        lv = lnl[:, c * TQ:(c + 1) * TQ]
                        w1 = [e_x[c], emb]
                        if c >= 2:
                            w1.append(e_lh[c - 2])
                        e1 = vop(P, lambda E, o=lnt[c % 2][:], a=xa[c][:],
                                 b=lb["mb_ps"][:]: E.tensor_sub(o, a, b),
                                 waits=tuple(w1))
                        e2 = vop(P, lambda E, o=lnt[c % 2][:], a=lnt[c % 2][:],
                                 b=lb["rb_ps"][:]: E.tensor_mul(o, a, b),
                                 waits=(erb,))
                        e_lh[c] = act(P, hv, lnt[c % 2][:], AF.Copy,
                                      waits=(e2,))
                        e_ll[c] = vop(P, lambda E, o=lv, a=lnt[c % 2][:], b=hv:
                                      E.tensor_sub(o, a, b), waits=(e_lh[c],))
                    if paranoid:
                        P.wait_fence(P.fence())
                    if dbg and it == 0:
                        base = seq * 6
                        dma(P, "scalar", "mxs", dbg_rows[base + 0:base + 1, :],
                            lb["m_row"][:], waits=(emb,))
                        dma(P, "scalar", "mxs", dbg_rows[base + 1:base + 2, :],
                            lb["v_row"][:], waits=(erb,))
                        dma(P, "scalar", "mxs", dbg_rows[base + 2:base + 3, :].bitcast(F32R),
                            lnh[0:1, 0:TQ], waits=(e_lh[0],))
                        dma(P, "scalar", "mxs", dbg_rows[base + 3:base + 4, :].bitcast(F32R),
                            lnl[0:1, 0:TQ], waits=(e_ll[0],))
                        e_d1 = vop(P, lambda E, o=lnt[0][:], i=lb["mb_ps"][:]:
                                   E.tensor_copy(o, i), waits=(emb,))
                        dma(P, "scalar", "mxs", dbg_rows[base + 4:base + 5, :],
                            lnt[0][0:1, :], waits=(e_d1,))
                        e_d2 = vop(P, lambda E, o=lnt[0][:], i=lb["rb_ps"][:]:
                                   E.tensor_copy(o, i), waits=(erb,))
                        dma(P, "scalar", "mxs", dbg_rows[base + 5:base + 6, :],
                            lnt[0][0:1, :], waits=(e_d2,))
                    jobs = [("q", oc) for oc in range(DC)] + [("k", 0), ("v", 0)]
                    for jn, (kind, oc) in enumerate(jobs):
                        if paranoid:
                            P.wait_fence(P.fence())
                        wsl = jn % 2
                        psl = jn % 2
                        if kind == "q":
                            src_h, src_l = qTr_hi[oc], qTr_lo[oc]
                        elif kind == "k":
                            src_h, src_l = kTr_hi[0], kTr_lo[0]
                        else:
                            src_h, src_l = vTr_hi[0], vTr_lo[0]
                        wh, wl = wq[2 * wsl], wq[2 * wsl + 1]
                        wch = "wA" if wsl == 0 else "wB"
                        ww = (wslot_ev[wsl], p0_done_a, p0_done_b) if wslot_ev[wsl] \
                            else (p0_done_a, p0_done_b)
                        e_wh = dma(P, "sync", wch, wh[:], src_h, waits=ww)
                        e_wl = dma(P, "sync", wch, wl[:], src_l, waits=ww)
                        ps = proj_ps[psl]
                        pw = (psum_ev[psl],) if psum_ev[psl] else ()
                        e_mm = None
                        for c in range(DC):
                            whc = wh[:, c * 128:(c + 1) * 128]
                            wlc = wl[:, c * 128:(c + 1) * 128]
                            rh = lnh[:, c * TQ:(c + 1) * TQ]
                            rl = lnl[:, c * TQ:(c + 1) * TQ]
                            first, last = (c == 0), (c == DC - 1)
                            mm(P, ps[:], whc, rh, start=first, stop=False,
                               waits=(e_wh, e_lh[c]) + (pw if first else ()))
                            mm(P, ps[:], whc, rl, start=False, stop=False,
                               waits=(e_ll[c],))
                            e_mm = mm(P, ps[:], wlc, rh, start=False, stop=last,
                                      waits=(e_wl,), inc=1 if last else 0)
                        wslot_ev[wsl] = e_mm
                        if kind == "q":
                            ee = vop(P, lambda E, o=ev_out[:], i=ps[:]:
                                     E.tensor_copy(o, i),
                                     waits=(e_mm,) + ((ev_store_ev,)
                                                      if ev_store_ev else ()))
                            ev_store_ev = dma(
                                P, "scalar", "qs",
                                q_s[seq, oc * 128:(oc + 1) * 128, t0:t0 + TQ],
                                ev_out[:], waits=(ee,))
                            psum_ev[psl] = ee
                        elif kind == "k":
                            eh = act(P, K_hi[:, t0:t0 + TQ], ps[:], AF.Copy,
                                     waits=(e_mm,))
                            ee = vop(P, lambda E, o=K_lo[:, t0:t0 + TQ],
                                     a=ps[:], b=K_hi[:, t0:t0 + TQ]:
                                     E.tensor_sub(o, a, b), waits=(eh,))
                            psum_ev[psl] = ee
                            kv_done.append(ee)
                        else:
                            ee = vop(P, lambda E, o=V_fm[:, t0:t0 + TQ], i=ps[:]:
                                     E.tensor_copy(o, i), waits=(e_mm,))
                            psum_ev[psl] = ee
                            kv_done.append(ee)
                vt_done = []
                e_prev = None
                for kc in range(KC):
                    e_t = tr(P, tr_ps[:, 0:128], V_fm[:, kc * 128:(kc + 1) * 128],
                             cst_id[:],
                             waits=(kv_done[-1], e_id) +
                                   ((e_prev,) if e_prev else ()))
                    eh = act(P, V_hi[:, kc * 128:(kc + 1) * 128],
                             tr_ps[:, 0:128], AF.Copy, waits=(e_t,))
                    e_c = vop(P, lambda E, o=V_lo[:, kc * 128:(kc + 1) * 128],
                              a=tr_ps[:, 0:128],
                              b=V_hi[:, kc * 128:(kc + 1) * 128]:
                              E.tensor_sub(o, a, b), waits=(eh,))
                    e_prev = e_c
                    vt_done.append(e_c)
                a1_q_done[seq] = Evt("qs", P.counts["qs"], "qs")
                P.flush()

            deps_kv = tuple(kv_done[-1:]) + tuple(vt_done[-1:]) + \
                tuple(vt_done[:1])
            for qt in range(NT if not (stop_after.startswith("p0") or stop_after in ("a1", "a1x")) else 0):
                t0 = qt * TQ
                with ExitStack() as stk:
                    P.wait_fence(P.fence())
                    tag = f"a2{seq}_{qt}"
                    qh = [stk.enter_context(
                        nc.sbuf_tensor(f"{tag}q{i}", [128, TQ], F32))
                        for i in range(2)]
                    qhh = [stk.enter_context(
                        nc.sbuf_tensor(f"{tag}qh{i}", [128, TQ], F32R))
                        for i in range(2)]
                    qhl = [stk.enter_context(
                        nc.sbuf_tensor(f"{tag}ql{i}", [128, TQ], F32R))
                        for i in range(2)]
                    expb = [stk.enter_context(
                        nc.sbuf_tensor(f"{tag}e{i}", [128, TQ], F32))
                        for i in range(2)]
                    exph = [stk.enter_context(
                        nc.sbuf_tensor(f"{tag}eh{i}", [128, TQ], F32R))
                        for i in range(2)]
                    expl = [stk.enter_context(
                        nc.sbuf_tensor(f"{tag}el{i}", [128, TQ], F32R))
                        for i in range(2)]
                    sacc = stk.enter_context(
                        nc.sbuf_tensor(f"{tag}sacc", [128, TQ], F32))
                    srow = stk.enter_context(
                        nc.sbuf_tensor(f"{tag}srow", [1, TQ], F32))
                    srh = stk.enter_context(
                        nc.sbuf_tensor(f"{tag}srh", [1, TQ], F32R))
                    srl = stk.enter_context(
                        nc.sbuf_tensor(f"{tag}srl", [1, TQ], F32R))
                    isb = stk.enter_context(
                        nc.sbuf_tensor(f"{tag}isb", [128, TQ], F32))
                    att_t = stk.enter_context(
                        nc.sbuf_tensor(f"{tag}att", [128, TQ], F32))
                    atth = stk.enter_context(
                        nc.sbuf_tensor(f"{tag}ah", [128, DC * TQ], F32R))
                    attl = stk.enter_context(
                        nc.sbuf_tensor(f"{tag}al", [128, DC * TQ], F32R))
                    wo = [stk.enter_context(
                        nc.sbuf_tensor(f"{tag}wo{i}", [128, D // 2], F32R))
                        for i in range(4)]
                    osb = stk.enter_context(
                        nc.sbuf_tensor(f"{tag}osb", [128, DC * TQ], F32))
                    ao_ps = [stk.enter_context(
                        nc.psum_tensor(f"{tag}ao{i}", [128, TQ], F32))
                        for i in range(2)]

                    att_split_ev = [None] * DC
                    ao_slot_ev = [None, None]
                    with ExitStack() as s1:
                        sc_ps = [s1.enter_context(
                            nc.psum_tensor(f"{tag}sc{i}", [128, TQ], F32))
                            for i in range(2)]
                        s_ps = s1.enter_context(
                            nc.psum_tensor(f"{tag}sps", [128, TQ], F32))
                        is_ps = s1.enter_context(
                            nc.psum_tensor(f"{tag}isps", [128, TQ], F32))
                        q_slot_ev = [None, None]
                        exp_slot_ev = [None, None]
                        sc_slot_ev = [None, None]
                        att_tmp_ev = None
                        e_s_prev = None
                        e_ib_prev = None
                        e_ic_prev = None
                        for h in range(HEADS):
                            qsl = h % 2
                            qw = (q_slot_ev[qsl],) if q_slot_ev[qsl] else ()
                            e_q = dma(P, "sync",
                                      "qlA" if qsl == 0 else "qlB",
                                      qh[qsl][:],
                                      q_s[seq, h * 128:(h + 1) * 128,
                                          t0:t0 + TQ],
                                      waits=qw + (a1_q_done[seq],))
                            e_qh = act(P, qhh[qsl][:], qh[qsl][:], AF.Copy,
                                       waits=(e_q,) + qw)
                            e_ql = vop(P, lambda E, o=qhl[qsl][:],
                                       a=qh[qsl][:], b=qhh[qsl][:]:
                                       E.tensor_sub(o, a, b), waits=(e_qh,))
                            e_sacc = None
                            e_ao_last = None
                            for kc in range(KC):
                                ssl = kc % 2
                                scw = (sc_slot_ev[ssl],) if sc_slot_ev[ssl] \
                                    else ()
                                kh = K_hi[:, kc * 128:(kc + 1) * 128]
                                kl = K_lo[:, kc * 128:(kc + 1) * 128]
                                mm(P, sc_ps[ssl][:], kh, qhh[qsl][:],
                                   start=True, stop=False,
                                   waits=(e_qh,) + deps_kv + scw)
                                mm(P, sc_ps[ssl][:], kh, qhl[qsl][:],
                                   start=False, stop=False, waits=(e_ql,))
                                e_sc = mm(P, sc_ps[ssl][:], kl, qhh[qsl][:],
                                          start=False, stop=True, inc=1)
                                ew = exp_slot_ev[ssl] or ()
                                e_e = act(P, expb[ssl][:], sc_ps[ssl][:],
                                          AF.Exp, bias=cst_stab[:],
                                          scale=cst_scl[:],
                                          waits=(e_sc, e_stab, e_scl) +
                                                tuple(ew))
                                sc_slot_ev[ssl] = e_e
                                e_eh = act(P, exph[ssl][:], expb[ssl][:],
                                           AF.Copy, waits=(e_e,))
                                e_el = vop(P, lambda E, o=expl[ssl][:],
                                           a=expb[ssl][:], b=exph[ssl][:]:
                                           E.tensor_sub(o, a, b),
                                           waits=(e_eh,))
                                if kc == 0:
                                    sw = (e_e,) + ((e_s_prev,) if e_s_prev
                                                   else ())
                                    e_sacc = vop(P, lambda E, o=sacc[:],
                                                 i=expb[ssl][:]:
                                                 E.tensor_copy(o, i), waits=sw)
                                else:
                                    e_sacc = vop(P, lambda E, o=sacc[:],
                                                 a=sacc[:], b=expb[ssl][:]:
                                                 E.tensor_add(o, a, b),
                                                 waits=(e_e,))
                                vh = V_hi[:, kc * 128:(kc + 1) * 128]
                                vl = V_lo[:, kc * 128:(kc + 1) * 128]
                                aw = [e_eh]
                                if kc == 0 and ao_slot_ev[h % 2] is not None:
                                    aw.append(ao_slot_ev[h % 2])
                                mm(P, ao_ps[h % 2][:], vh, exph[ssl][:],
                                   start=(kc == 0), stop=False,
                                   waits=tuple(aw))
                                mm(P, ao_ps[h % 2][:], vh, expl[ssl][:],
                                   start=False, stop=False, waits=(e_el,))
                                e_ao_last = mm(P, ao_ps[h % 2][:], vl,
                                               exph[ssl][:], start=False,
                                               stop=(kc == KC - 1),
                                               inc=1 if kc == KC - 1 else 0)
                                exp_slot_ev[ssl] = (e_ao_last, e_sacc)
                            q_slot_ev[qsl] = e_ao_last
                            ssp = (e_s_prev,) if e_s_prev else ()
                            e_kh = act(P, qhh[qsl][:], sacc[:], AF.Copy,
                                       waits=(e_sacc, e_ao_last) + ssp)
                            e_kl = vop(P, lambda E, o=qhl[qsl][:], a=sacc[:],
                                       b=qhh[qsl][:]: E.tensor_sub(o, a, b),
                                       waits=(e_kh,))
                            mm(P, s_ps[0:1, :], cst_ones_r[:], qhh[qsl][:],
                               start=True, stop=False, waits=(e_kh, e_onesr) + ssp)
                            e_s = mm(P, s_ps[0:1, :], cst_ones_r[:],
                                     qhl[qsl][:], start=False, stop=True,
                                     waits=(e_kl,), inc=1)
                            e_s_prev = e_s
                            e_r = vop(P, lambda E, o=srow[:], i=s_ps[0:1, :]:
                                      E.reciprocal(o, i),
                                      waits=(e_s,) + ((e_ib_prev,)
                                                      if e_ib_prev else ()))
                            e_ib = bcast_r(is_ps[:], srow[:], srh[:], srl[:],
                                           waits=(e_r,) + ((e_ic_prev,)
                                                           if e_ic_prev
                                                           else ()))
                            e_ib_prev = e_ib
                            e_ic = vop(P, lambda E, o=isb[:], i=is_ps[:]:
                                       E.tensor_copy(o, i), waits=(e_ib,))
                            e_ic_prev = e_ic
                            e_at = vop(P, lambda E, o=att_t[:],
                                       a=ao_ps[h % 2][:], b=isb[:]:
                                       E.tensor_mul(o, a, b),
                                       waits=(e_ao_last, e_ic) +
                                             ((att_tmp_ev,) if att_tmp_ev
                                              else ()))
                            ao_slot_ev[h % 2] = e_at
                            hv = atth[:, h * TQ:(h + 1) * TQ]
                            lv = attl[:, h * TQ:(h + 1) * TQ]
                            e_h = act(P, hv, att_t[:], AF.Copy, waits=(e_at,))
                            e_l = vop(P, lambda E, o=lv, a=att_t[:], b=hv:
                                      E.tensor_sub(o, a, b), waits=(e_h,))
                            att_tmp_ev = e_l
                            att_split_ev[h] = e_l

                    with ExitStack() as s2:
                        lb2 = mk_ln_bufs(s2, tag + "ln")
                        xa = [s2.enter_context(
                            nc.sbuf_tensor(f"{tag}x{i}", [128, TQ], F32))
                            for i in range(2)]
                        xguard = tuple(P.fence())
                        wslot_ev = [None, None]
                        oproj_ev = [None] * DC
                        for oc in range(DC):
                            ps = ao_ps[oc % 2]
                            pw = (ao_slot_ev[oc % 2],) \
                                if ao_slot_ev[oc % 2] else ()
                            e_mm = None
                            for half in range(2):
                                wh, wl = wo[2 * half], wo[2 * half + 1]
                                wch = "wA" if half == 0 else "wB"
                                ww = ((wslot_ev[half],) if wslot_ev[half]
                                      else ()) + (p0_done_a, p0_done_b)
                                e_wh = dma(P, "sync", wch, wh[:],
                                           oTr_hi[oc, :, half * (D // 2):
                                                  (half + 1) * (D // 2)],
                                           waits=ww)
                                e_wl = dma(P, "sync", wch, wl[:],
                                           oTr_lo[oc, :, half * (D // 2):
                                                  (half + 1) * (D // 2)],
                                           waits=ww)
                                for k in range(DC // 2):
                                    c = half * (DC // 2) + k
                                    whc = wh[:, k * 128:(k + 1) * 128]
                                    wlc = wl[:, k * 128:(k + 1) * 128]
                                    rh = atth[:, c * TQ:(c + 1) * TQ]
                                    rl = attl[:, c * TQ:(c + 1) * TQ]
                                    first = (c == 0)
                                    last = (c == DC - 1)
                                    mm(P, ps[:], whc, rh, start=first,
                                       stop=False,
                                       waits=(e_wh, att_split_ev[c]) +
                                             (pw if first else ()))
                                    mm(P, ps[:], whc, rl, start=False,
                                       stop=False)
                                    e_mm = mm(P, ps[:], wlc, rh, start=False,
                                              stop=last, waits=(e_wl,),
                                              inc=1 if last or k == DC // 2 - 1
                                              else 0)
                                wslot_ev[half] = e_mm
                            ee = act(P, osb[:, oc * TQ:(oc + 1) * TQ], ps[:],
                                     AF.Copy, waits=(e_mm,))
                            ao_slot_ev[oc % 2] = ee
                            oproj_ev[oc] = ee
                        o_tiles = [osb[:, c * TQ:(c + 1) * TQ]
                                   for c in range(DC)]
                        emb, erb = ln_rows_emit(P, o_tiles, oproj_ev, lb2, 1e-5)
                        xa_ev = [None, None]
                        for c in range(DC):
                            xw = list(xguard)
                            if xa_ev[c % 2]:
                                xw.append(xa_ev[c % 2])
                            e_x = dma(P, "scalar", "xin", xa[c % 2][:],
                                      x_in[seq, c * 128:(c + 1) * 128,
                                           t0:t0 + TQ], waits=tuple(xw))
                            vop(P, lambda E, o=o_tiles[c], a=o_tiles[c],
                                b=lb2["mb_ps"][:]: E.tensor_sub(o, a, b),
                                waits=(oproj_ev[c], emb, erb))
                            vop(P, lambda E, o=o_tiles[c], a=o_tiles[c],
                                b=lb2["rb_ps"][:]: E.tensor_mul(o, a, b))
                            e3 = vop(P, lambda E, o=o_tiles[c], a=o_tiles[c],
                                     b=xa[c % 2][:]: E.tensor_add(o, a, b),
                                     waits=(e_x,))
                            xa_ev[c % 2] = e3
                            x1_store_last = dma(
                                P, "scalar", "x1s",
                                x1_s[seq, c * 128:(c + 1) * 128, t0:t0 + TQ],
                                o_tiles[c], waits=(e3,))
                    P.flush()

    out_store_last = None
    for tt in range(SEQS * T // TQ if stop_after == "ffn" else 0):
        seq, t0 = divmod(tt * TQ, T)
        with ExitStack() as stk:
            P.wait_fence(P.fence())
            tag = f"f{tt}"
            ph = stk.enter_context(
                nc.sbuf_tensor(f"{tag}ph", [128, HC * TQ], BF16))
            pl = stk.enter_context(
                nc.sbuf_tensor(f"{tag}pl", [128, HC * TQ], BF16))
            lb = mk_ln_bufs(stk, tag + "ln")
            s23 = stk.enter_context(ExitStack())
            x1 = [s23.enter_context(
                nc.sbuf_tensor(f"{tag}x{c}", [128, TQ], F32))
                for c in range(DC)]
            xi = s23.enter_context(
                nc.sbuf_tensor(f"{tag}xi", [128, DC * TQ], BF16))
            sb_b = s23.enter_context(
                nc.sbuf_tensor(f"{tag}sb", [128, TQ], F32))
            isv_b = s23.enter_context(
                nc.sbuf_tensor(f"{tag}isv", [128, TQ], F32))

            with ExitStack() as s2:
                _l0 = s2.enter_context(
                    nc.sbuf_tensor(f"{tag}l0", [128, TQ], F32))
                ln3t = [_l0, _l0]
                tmax = s2.enter_context(
                    nc.sbuf_tensor(f"{tag}tm", [128, TQ], F32))
                t_sx = lb["sq0"]
                t_c = lb["sq1"]
                t_r = lb["xh0"]
                t_sg = lb["xl0"]
                mx4 = s2.enter_context(
                    nc.sbuf_tensor(f"{tag}mx4", [128, 4], F32))
                amax_r = s2.enter_context(
                    nc.sbuf_tensor(f"{tag}am", [1, TQ], F32))
                srow2 = s2.enter_context(
                    nc.sbuf_tensor(f"{tag}sr", [1, TQ], F32))
                fbh = lb["mh"]
                fbl = lb["ml"]
                tr_ps = s2.enter_context(
                    nc.psum_tensor(f"{tag}trp", [128, TQ], F32))

                sb_ps = s2.enter_context(
                    nc.psum_tensor(f"{tag}sbp", [128, TQ], F32))
                iv_ps = s2.enter_context(
                    nc.psum_tensor(f"{tag}ivp", [128, TQ], F32))

                xw = (x1_store_last,) if x1_store_last else ()
                e_x = [dma(P, "scalar", "xin", x1[c][:],
                           x1_s[seq, c * 128:(c + 1) * 128, t0:t0 + TQ],
                           waits=xw)
                       for c in range(DC)]
                e_x = [e_x[-1]] * DC
                emb, erb = ln_rows_emit(P, [x1[c][:] for c in range(DC)],
                                        e_x, lb, 1e-8)
                e_mx = None
                ln_last = [None, None]
                for c in range(DC):
                    lt = ln3t[c % 2]
                    w1 = [e_x[c], emb, erb]
                    if ln_last[c % 2]:
                        w1.append(ln_last[c % 2])
                    vop(P, lambda E, o=lt[:], a=x1[c][:],
                        b=lb["mb_ps"][:]: E.tensor_sub(o, a, b),
                        waits=tuple(w1))
                    vop(P, lambda E, o=lt[:], a=lt[:],
                        b=lb["rb_ps"][:]: E.tensor_mul(o, a, b))
                    if c == 0:
                        e_mx = vop(P, lambda E, o=tmax[:], a=lt[:],
                                   b=lt[:]: E.scalar_tensor_tensor(
                                       o, a, -1.0, b, ALU.mult, ALU.max))
                    else:
                        e_a = vop(P, lambda E, o=t_sx[:], a=lt[:],
                                  b=lt[:]: E.scalar_tensor_tensor(
                                      o, a, -1.0, b, ALU.mult, ALU.max))
                        e_mx = vop(P, lambda E, o=tmax[:], a=tmax[:],
                                   b=t_sx[:]: E.tensor_max(o, a, b))
                    ln_last[c % 2] = e_mx
                e_red = None
                for j in range(4):
                    e_t = tr(P, tr_ps[:, 0:128], tmax[:, j * 128:(j + 1) * 128],
                             cst_id[:], waits=(e_mx, e_id) +
                             ((e_red,) if e_red else ()))
                    e_red = vop(P, lambda E, o=mx4[:, j:j + 1], i=tr_ps[:, 0:128]:
                                E.reduce_max(o, i, AXX), waits=(e_t,))
                e_t4 = tr(P, tr_ps[0:4, 0:128], mx4[:], cst_id[:], waits=(e_red,))
                e_cp = vop(P, lambda E, o=tmax[:, 0:128], i=tr_ps[:, 0:128]:
                           E.tensor_copy(o, i), waits=(e_t4,))
                e_am = None
                for j in range(4):
                    e_am = dma(P, "sync", "mxs", amax_r[:, j * 128:(j + 1) * 128],
                               tmax[j:j + 1, 0:128], waits=(e_cp,))
                e_rec = vop(P, lambda E, o=srow2[:], i=amax_r[:]:
                            E.reciprocal(o, i), waits=(e_am,))
                e_sr = vop(P, lambda E, o=srow2[:]: E.tensor_scalar_mul(
                    o, o, 127.0), waits=(e_rec,))
                e_ir = vop(P, lambda E, o=amax_r[:]: E.tensor_scalar_mul(
                    o, o, 1.0 / 127.0), waits=(e_rec,))
                e_sb = bcast_r(sb_ps[:], srow2[:], fbh[:], fbl[:],
                               waits=(e_sr,))
                e_sbc = vop(P, lambda E, o=sb_b[:], i=sb_ps[:]:
                            E.tensor_copy(o, i), waits=(e_sb,))
                e_iv = bcast_r(iv_ps[:], amax_r[:], fbh[:], fbl[:],
                               waits=(e_ir, e_sbc))
                e_ivc = vop(P, lambda E, o=isv_b[:], i=iv_ps[:]:
                            E.tensor_copy(o, i), waits=(e_iv,))
                e_xi = [None] * DC
                e_sg_prev = None
                for c in range(DC):
                    xv = xi[:, c * TQ:(c + 1) * TQ]
                    lt = ln3t[c % 2]
                    vop(P, lambda E, o=lt[:], a=x1[c][:], b=lb["mb_ps"][:]:
                        E.tensor_sub(o, a, b))
                    vop(P, lambda E, o=lt[:], a=lt[:], b=lb["rb_ps"][:]:
                        E.tensor_mul(o, a, b))
                    e1 = vop(P, lambda E, o=t_sx[:], a=lt[:], b=sb_b[:]:
                             E.tensor_mul(o, a, b), waits=(e_sbc,))
                    e_sg = act(P, t_sg[:], t_sx[:], AF.Sign,
                               waits=(e1,) + ((e_sg_prev,)
                                              if e_sg_prev else ()))
                    e2 = vop(P, lambda E, o=t_c[:], a=t_sx[:], b=t_sx[:]:
                             E.scalar_tensor_tensor(o, a, -1.0, b, ALU.mult,
                                                    ALU.max))
                    e3 = vop(P, lambda E, o=t_r[:], i=t_c[:]:
                             E.tensor_single_scalar(o, i, 63.5, ALU.is_ge))
                    e3 = vop(P, lambda E, o=t_r[:], i=t_r[:]:
                             E.tensor_scalar_mul(o, i, 64.0))
                    for b in (32, 16, 8, 4, 2, 1):
                        e3 = vop(P, lambda E, o=t_sx[:], a=t_r[:],
                                 bb=float(b), c2=t_c[:]:
                                 E.scalar_tensor_tensor(
                                     o, a, bb - 0.5, c2, ALU.add, ALU.is_le))
                        e3 = vop(P, lambda E, o=t_r[:], a=t_sx[:],
                                 bb=float(b), c2=t_r[:]:
                                 E.scalar_tensor_tensor(
                                     o, a, bb, c2, ALU.mult, ALU.add))
                    e_xi[c] = vop(P, lambda E, o=xv, a=t_r[:], b=t_sg[:]:
                                  E.tensor_mul(o, a, b), waits=(e3, e_sg))
                    e_sg_prev = e_xi[c]

            with ExitStack() as s3:
                wg_b = [s3.enter_context(
                    nc.sbuf_tensor(f"{tag}wg{i}", [128, D // 2], BF16))
                    for i in range(2)]
                wu_b = [s3.enter_context(
                    nc.sbuf_tensor(f"{tag}wu{i}", [128, D // 2], BF16))
                    for i in range(2)]
                gt = lb["sq0"]
                ut = lb["sq1"]
                pt = gt
                g_ps = [s3.enter_context(
                    nc.psum_tensor(f"{tag}gp{i}", [128, TQ], F32))
                    for i in range(2)]
                u_ps = [s3.enter_context(
                    nc.psum_tensor(f"{tag}up{i}", [128, TQ], F32))
                    for i in range(2)]
                wg_ev = [None, None]
                wu_ev = [None, None]
                gp_ev = [None, None]
                up_ev = [None, None]
                tmp_ev = None
                p_evts = []
                s3_guard = tuple(P.fence())
                HD2 = D // 2
                for hc in range(HC):
                    e_g = e_u = None
                    for half in range(2):
                        sl = half
                        ww = ((wg_ev[sl],) if wg_ev[sl] else ()) + \
                            (p0_done_a, p0_done_b) + \
                            (s3_guard if wg_ev[sl] is None else ())
                        e_wg = dma(P, "sync", "wA" if sl == 0 else "wB",
                                   wg_b[sl][:],
                                   gTs[hc, :, half * HD2:(half + 1) * HD2],
                                   waits=ww)
                        ww = ((wu_ev[sl],) if wu_ev[sl] else ()) + \
                            (p0_done_a, p0_done_b) + \
                            (s3_guard if wu_ev[sl] is None else ())
                        e_wu = dma(P, "sync", "wC" if sl == 0 else "wD",
                                   wu_b[sl][:],
                                   uTs[hc, :, half * HD2:(half + 1) * HD2],
                                   waits=ww)
                        for k in range(DC // 2):
                            c = half * (DC // 2) + k
                            first, last = (c == 0), (c == DC - 1)
                            gw = [e_wg, e_xi[c]]
                            if first and gp_ev[0]:
                                gw.append(gp_ev[0])
                            e_g = mm(P, g_ps[hc % 2][:],
                                     wg_b[sl][:, k * 128:(k + 1) * 128],
                                     xi[:, c * TQ:(c + 1) * TQ], start=first,
                                     stop=last, waits=tuple(gw),
                                     inc=1 if last or k == DC // 2 - 1 else 0)
                        wg_ev[sl] = e_g
                        for k in range(DC // 2):
                            c = half * (DC // 2) + k
                            first, last = (c == 0), (c == DC - 1)
                            uw = [e_wu, e_xi[c]]
                            if first and up_ev[0]:
                                uw.append(up_ev[0])
                            e_u = mm(P, u_ps[hc % 2][:],
                                     wu_b[sl][:, k * 128:(k + 1) * 128],
                                     xi[:, c * TQ:(c + 1) * TQ], start=first,
                                     stop=last, waits=tuple(uw),
                                     inc=1 if last or k == DC // 2 - 1 else 0)
                        wu_ev[sl] = e_u
                    psl = hc % 2
                    e1 = vop(P, lambda E, o=gt[:], a=g_ps[psl][:], b=isv_b[:]:
                             E.tensor_mul(o, a, b),
                             waits=(e_g, e_ivc) + ((tmp_ev,) if tmp_ev
                                                   else ()))
                    gp_ev[0] = e1
                    e2 = act(P, gt[:], gt[:], AF.Silu, waits=(e1,))
                    e3 = vop(P, lambda E, o=ut[:], a=u_ps[psl][:], b=isv_b[:]:
                             E.tensor_mul(o, a, b), waits=(e_u,))
                    up_ev[0] = e3
                    e4 = vop(P, lambda E, o=pt[:], a=gt[:], b=ut[:]:
                             E.tensor_mul(o, a, b), waits=(e2, e3))
                    hv = ph[:, hc * TQ:(hc + 1) * TQ]
                    lv = pl[:, hc * TQ:(hc + 1) * TQ]
                    e5 = act(P, hv, pt[:], AF.Copy, waits=(e4,))
                    e6 = vop(P, lambda E, o=lv, a=pt[:], b=hv:
                             E.tensor_sub(o, a, b), waits=(e5,))
                    tmp_ev = e6
                    p_evts.append(e6)

            s23.close()
            with ExitStack() as s4:
                QW = HID // 4
                wd_b = [s4.enter_context(
                    nc.sbuf_tensor(f"{tag}wd{i}", [128, QW], BF16))
                    for i in range(2)]
                x1b = [s4.enter_context(
                    nc.sbuf_tensor(f"{tag}xb{i}", [128, TQ], F32))
                    for i in range(2)]
                dsb = s4.enter_context(
                    nc.sbuf_tensor(f"{tag}dsb", [128, DC * TQ], F32))
                d_ps = [s4.enter_context(
                    nc.psum_tensor(f"{tag}dp{i}", [128, TQ], F32))
                    for i in range(2)]
                s4_guard = tuple(P.fence())
                wd_ev = [None, None]
                dp_ev = [None, None]
                d_ev = [None] * DC
                for oc in range(DC):
                    e_d = None
                    for qr in range(4):
                        sl = qr % 2
                        ww = [p0_done_a, p0_done_b]
                        ww += list(s4_guard)
                        if wd_ev[sl]:
                            ww.append(wd_ev[sl])
                        e_wd = dma(P, "sync", "wE" if sl == 0 else "wF",
                                   wd_b[sl][:],
                                   dTs[oc, :, qr * QW:(qr + 1) * QW],
                                   waits=tuple(ww))
                        for k in range(HC // 4):
                            hc = qr * (HC // 4) + k
                            first = (hc == 0)
                            last = (hc == HC - 1)
                            dw = [e_wd]
                            if first:
                                dw += p_evts
                                if dp_ev[oc % 2]:
                                    dw.append(dp_ev[oc % 2])
                            mm(P, d_ps[oc % 2][:],
                               wd_b[sl][:, k * 128:(k + 1) * 128],
                               ph[:, hc * TQ:(hc + 1) * TQ], start=first,
                               stop=False, waits=tuple(dw))
                            e_d = mm(P, d_ps[oc % 2][:],
                                     wd_b[sl][:, k * 128:(k + 1) * 128],
                                     pl[:, hc * TQ:(hc + 1) * TQ],
                                     start=False, stop=last,
                                     inc=1 if last or k == HC // 4 - 1 else 0)
                        wd_ev[sl] = e_d
                    ee = act(P, dsb[:, oc * TQ:(oc + 1) * TQ], d_ps[oc % 2][:],
                             AF.Copy, waits=(e_d,))
                    dp_ev[oc % 2] = ee
                    d_ev[oc] = ee
                d_tiles = [dsb[:, c * TQ:(c + 1) * TQ] for c in range(DC)]
                lb["_smm"] = {}
                emb, erb = ln_rows_emit(P, d_tiles, d_ev, lb, 1e-5)
                xb_ev = [None, None]
                for c in range(DC):
                    xw = [s for s in s4_guard]
                    if xb_ev[c % 2]:
                        xw.append(xb_ev[c % 2])
                    e_xb = dma(P, "scalar", "xin", x1b[c % 2][:],
                               x1_s[seq, c * 128:(c + 1) * 128, t0:t0 + TQ],
                               waits=tuple(xw))
                    vop(P, lambda E, o=d_tiles[c], a=d_tiles[c],
                        b=lb["mb_ps"][:]: E.tensor_sub(o, a, b),
                        waits=(d_ev[c], emb, erb))
                    vop(P, lambda E, o=d_tiles[c], a=d_tiles[c],
                        b=lb["rb_ps"][:]: E.tensor_mul(o, a, b))
                    e3 = vop(P, lambda E, o=d_tiles[c], a=d_tiles[c],
                             b=x1b[c % 2][:]: E.tensor_add(o, a, b),
                             waits=(e_xb,))
                    xb_ev[c % 2] = e3
                    out_store_last = dma(
                        P, "scalar", "outs",
                        out_hbm[seq, c * 128:(c + 1) * 128, t0:t0 + TQ],
                        d_tiles[c], waits=(e3,))
            P.flush()

    for key in ("outs", "x1s", "qs", "p0sa", "p0sb"):
        if P.counts[key]:
            P.emit("scalar",
                   lambda E, s=P.sems[key], v=P.counts[key]: E.wait_ge(s, v))
    P.flush(final=True)
    return nc


def _layout_inputs(x, q_w, k_w, v_w, o_w, W_g, W_u, W_d):
    def slabify(W, ocs, ics):
        a = W.reshape(ocs, 128, ics, 128)
        a = a.transpose(0, 3, 2, 1)
        return np.ascontiguousarray(a.reshape(ocs, 128, ics * 128))

    qT = slabify(q_w, DC, DC)
    kT = slabify(k_w, 1, DC)
    vT = slabify(v_w, 1, DC)
    oT = slabify(o_w, DC, DC)
    gT = slabify(W_g, HC, DC)
    uT = slabify(W_u, HC, DC)
    dT = slabify(W_d, DC, HC)
    in_maps = []
    for core in range(NCORES):
        xs = x[core * SEQS:(core + 1) * SEQS]
        x_fm = np.ascontiguousarray(xs.transpose(0, 2, 1))
        in_maps.append({"x_fm": x_fm, "qT": qT, "kT": kT, "vT": vT, "oT": oT,
                        "gT": gT, "uT": uT, "dT": dT})
    return in_maps


def kernel(L0_attn_x, L0_attn_q_w, L0_attn_k_w, L0_attn_v_w, L0_attn_o_w,
           L0_ffn_W_g, L0_ffn_W_u, L0_ffn_W_d):
    in_maps = _layout_inputs(
        np.asarray(L0_attn_x, dtype=np.float32),
        np.asarray(L0_attn_q_w, dtype=np.float32),
        np.asarray(L0_attn_k_w, dtype=np.float32),
        np.asarray(L0_attn_v_w, dtype=np.float32),
        np.asarray(L0_attn_o_w, dtype=np.float32),
        np.asarray(L0_ffn_W_g, dtype=np.float32),
        np.asarray(L0_ffn_W_u, dtype=np.float32),
        np.asarray(L0_ffn_W_d, dtype=np.float32))
    nc = build_nc()
    res = run_bass_kernel_spmd(nc, in_maps, core_ids=list(range(NCORES)))
    out = np.empty((NCORES * SEQS, T, D), dtype=np.float32)
    for core in range(NCORES):
        o = res.results[core]["out_fm"]
        out[core * SEQS:(core + 1) * SEQS] = o.transpose(0, 2, 1)
    return out


# version 02
# speedup vs baseline: 1.1828x; 1.1828x over previous
from contextlib import ExitStack

import numpy as np
import concourse.bass as bass
import concourse.mybir as mybir
from concourse.bass_utils import run_bass_kernel_spmd

F32 = mybir.dt.float32
F32R = mybir.dt.float32r
BF16 = mybir.dt.bfloat16
AF = mybir.ActivationFunctionType
ALU = mybir.AluOpType
AXX = mybir.AxisListType.X

NCORES = 8
SEQS = 2
T = 2048
D = 2048
HID = 8192
HEADS = 16
DC = D // 128
HC = HID // 128
KC = T // 128
TQ = 512
NT = T // TQ
SCALE = 1.0 / float(np.sqrt(128))
STAB = 12.0


class Evt:
    __slots__ = ("sem", "val", "eng")

    def __init__(self, sem, val, eng):
        self.sem, self.val, self.eng = sem, val, eng


class Plan:

    ENGS = ("sync", "gpsimd", "scalar", "vector", "tensor")
    DMАS = ()
    SEMS = ENGS + ("cst", "p0a", "p0b", "p0sa", "p0sb", "xin",
                   "wA", "wB", "wC", "wD", "wE", "wF",
                   "qs", "qlA", "qlB", "x1s", "outs", "mxs")

    def __init__(self, nc):
        self.nc = nc
        self.streams = {e: [] for e in self.ENGS}
        self.sems = {s: nc.alloc_semaphore("sem_" + s) for s in self.SEMS}
        self.counts = {s: 0 for s in self.SEMS}
        self.marks = {e: {} for e in self.ENGS}

    def _wait(self, eng, evt, force=False):
        if evt is None:
            return
        if evt.eng == eng and not force:
            return
        m = self.marks[eng]
        if m.get(evt.sem, 0) >= evt.val:
            return
        m[evt.sem] = evt.val
        sem_h, val = self.sems[evt.sem], evt.val
        self.streams[eng].append(lambda E, s=sem_h, v=val: E.wait_ge(s, v))

    def emit(self, eng, fn, waits=(), inc=0, sem=None):
        is_dma = sem is not None and sem != eng
        for w in waits:
            self._wait(eng, w, force=is_dma)
        if inc:
            key = sem or eng
            self.counts[key] += inc
            sem_h, val = self.sems[key], self.counts[key]

            def run(E, fn=fn, s=sem_h, k=inc):
                fn(E).then_inc(s, k)
            self.streams[eng].append(run)
            return Evt(key, val, key if key != eng else eng)
        self.streams[eng].append(fn)
        return None

    def fence(self):
        return [Evt(s, self.counts[s], s if s.endswith("_dma") else s)
                for s in self.SEMS if self.counts[s] > 0]

    def wait_fence(self, fevts, exclude=("tensor",)):
        for e in self.ENGS:
            if e in exclude:
                continue
            for ev in fevts:
                self._wait(e, ev)

    def flush(self, final=False):
        if not final:
            return
        used = [e for e in self.ENGS if self.streams[e]]
        if not used:
            return
        with self.nc.Block() as block:
            for e in used:
                fns = self.streams[e]

                def body(E, fns=fns):
                    for f in fns:
                        f(E)
                getattr(block, e)(body)
        self.streams = {e: [] for e in self.ENGS}


def dma(P, qeng, ch, out_ap, in_ap, waits=()):
    return P.emit(qeng, lambda E: E.dma_start(out=out_ap, in_=in_ap),
                  waits=waits, inc=16, sem=ch)


def mm(P, out_ap, lhsT, rhs, start, stop, waits=(), inc=0):
    return P.emit("tensor",
                  lambda E: E.matmul(out_ap, lhsT, rhs, start=start, stop=stop),
                  waits=waits, inc=inc)


def tr(P, out_ap, in_ap, ident, waits=()):
    return P.emit("tensor", lambda E: E.transpose(out_ap, in_ap, ident),
                  waits=waits, inc=1)


def act(P, out_ap, in_ap, func, bias=0.0, scale=1.0, waits=()):
    return P.emit("scalar",
                  lambda E: E.activation(out_ap, in_ap, func, bias=bias,
                                         scale=scale),
                  waits=waits, inc=1)


def vop(P, fn, waits=()):
    return P.emit("vector", fn, waits=waits, inc=1)


def build_nc(stop_after="ffn", dbg=False, paranoid=False):
    nc = bass.Bass("TRN2", target_bir_lowering=False,
                   dynamic_dma_scratch_size=8192,
                   detect_race_conditions=False)

    x_in = nc.dram_tensor("x_fm", [SEQS, D, T], F32, kind="ExternalInput").ap()
    qT_in = nc.dram_tensor("qT", [DC, 128, D], F32, kind="ExternalInput").ap()
    kT_in = nc.dram_tensor("kT", [1, 128, D], F32, kind="ExternalInput").ap()
    vT_in = nc.dram_tensor("vT", [1, 128, D], F32, kind="ExternalInput").ap()
    oT_in = nc.dram_tensor("oT", [DC, 128, D], F32, kind="ExternalInput").ap()
    gT_in = nc.dram_tensor("gT", [HC, 128, D], F32, kind="ExternalInput").ap()
    uT_in = nc.dram_tensor("uT", [HC, 128, D], F32, kind="ExternalInput").ap()
    dT_in = nc.dram_tensor("dT", [DC, 128, HID], F32, kind="ExternalInput").ap()
    out_hbm = nc.dram_tensor("out_fm", [SEQS, D, T], F32,
                             kind="ExternalOutput").ap()

    _dk = "ExternalOutput" if dbg else "Internal"
    dbg_rows = nc.dram_tensor("dbg_rows", [12, TQ], F32, kind=_dk).ap() if dbg \
        else None
    qTr_hi = nc.dram_tensor("qTr_hi", [DC, 128, D], F32R, kind=_dk).ap()
    qTr_lo = nc.dram_tensor("qTr_lo", [DC, 128, D], F32R, kind=_dk).ap()
    oTr_hi = nc.dram_tensor("oTr_hi", [DC, 128, D], F32R).ap()
    oTr_lo = nc.dram_tensor("oTr_lo", [DC, 128, D], F32R).ap()
    kTr_hi = nc.dram_tensor("kTr_hi", [1, 128, D], F32R).ap()
    kTr_lo = nc.dram_tensor("kTr_lo", [1, 128, D], F32R).ap()
    vTr_hi = nc.dram_tensor("vTr_hi", [1, 128, D], F32R).ap()
    vTr_lo = nc.dram_tensor("vTr_lo", [1, 128, D], F32R).ap()
    gTs = nc.dram_tensor("gTs", [HC, 128, D], BF16).ap()
    uTs = nc.dram_tensor("uTs", [HC, 128, D], BF16).ap()
    dTs = nc.dram_tensor("dTs", [DC, 128, HID], BF16).ap()
    dbg_kind = "ExternalOutput" if dbg else "Internal"
    q_s = nc.dram_tensor("q_s", [SEQS, D, T], F32, kind=dbg_kind).ap()
    x1_s = nc.dram_tensor("x1_s", [SEQS, D, T], F32, kind=dbg_kind).ap()

    ones128_h = nc.inline_tensor(np.ones((128, 1), np.float32), "ones128")
    onescol_h = nc.inline_tensor(np.ones((1, 128), np.float32), "onescol")
    ident_h = nc.inline_tensor(np.eye(128, dtype=np.float32), "ident128")
    stab_h = nc.inline_tensor(np.full((128, 1), -STAB, np.float32), "stab")
    scl_h = nc.inline_tensor(np.full((128, 1), SCALE, np.float32), "scl")

    P = Plan(nc)

    cst = nc.alloc_sbuf_tensor("cst_ones128", [128, 1], F32)
    cst_col = nc.alloc_sbuf_tensor("cst_onescol", [1, 128], F32)
    cst_id = nc.alloc_sbuf_tensor("cst_ident", [128, 128], F32)
    cst_stab = nc.alloc_sbuf_tensor("cst_stab", [128, 1], F32)
    cst_scl = nc.alloc_sbuf_tensor("cst_scl", [128, 1], F32)
    cst_col_r = nc.alloc_sbuf_tensor("cst_colr", [1, 128], F32R)
    cst_ones_r = nc.alloc_sbuf_tensor("cst_onesr", [128, 1], F32R)
    e_ones = dma(P, "sync", "cst", cst[:], ones128_h.ap())
    e_col = dma(P, "sync", "cst", cst_col[:], onescol_h.ap())
    e_id = dma(P, "sync", "cst", cst_id[:], ident_h.ap())
    e_stab = dma(P, "sync", "cst", cst_stab[:], stab_h.ap())
    e_scl = dma(P, "sync", "cst", cst_scl[:], scl_h.ap())
    e_colr = act(P, cst_col_r[:], cst_col[:], AF.Copy, waits=(e_col,))
    e_onesr = act(P, cst_ones_r[:], cst[:], AF.Copy, waits=(e_ones,))

    def bcast_r(ps_ap, row_ap, hi_ap, lo_ap, waits=()):
        eh = act(P, hi_ap, row_ap, AF.Copy, waits=waits)
        el = vop(P, lambda E, o=lo_ap, a=row_ap, b=hi_ap: E.tensor_sub(o, a, b),
                 waits=(eh,))
        mm(P, ps_ap, cst_col_r[:], hi_ap, start=True, stop=False,
           waits=(eh, e_colr))
        return mm(P, ps_ap, cst_col_r[:], lo_ap, start=False, stop=True,
                  waits=(el,), inc=1)

    def ln_rows_emit(P, x_tiles, x_evts, lb, eps):
        sq = (lb["sq0"], lb["sq1"])
        xh = (lb["xh0"], lb["xh1"])
        xl = (lb["xl0"], lb["xl1"])
        prev_rows = lb.get("prev_rows") or ()
        mmev = lb["_smm"]
        e_m = None
        for c in range(DC):
            w = [x_evts[c]]
            if ("m", c - 2) in mmev:
                w.append(mmev[("m", c - 2)])
            elif c < 2:
                w.extend(prev_rows)
            eh = act(P, xh[c % 2][:], x_tiles[c], AF.Copy, waits=tuple(w))
            el = vop(P, lambda E, o=xl[c % 2][:], a=x_tiles[c], b=xh[c % 2][:]:
                     E.tensor_sub(o, a, b), waits=(eh,))
            mm(P, lb["mean_ps"][0:1, :], cst_ones_r[:], xh[c % 2][:],
               start=(c == 0), stop=False,
               waits=(eh, e_onesr) + (tuple(prev_rows) if c == 0 else ()))
            e_m = mm(P, lb["mean_ps"][0:1, :], cst_ones_r[:], xl[c % 2][:],
                     start=False, stop=(c == DC - 1), waits=(el,), inc=1)
            mmev[("m", c)] = e_m
        e_s = None
        for c in range(DC):
            w = [x_evts[c]]
            if ("s", c - 2) in mmev:
                w.append(mmev[("s", c - 2)])
            elif c < 2:
                w.append(mmev[("m", DC - 2 + c)])
            e_sq = act(P, sq[c % 2][:], x_tiles[c], AF.Square, waits=tuple(w))
            eh = act(P, xh[c % 2][:], sq[c % 2][:], AF.Copy, waits=(e_sq,))
            el = vop(P, lambda E, o=xl[c % 2][:], a=sq[c % 2][:],
                     b=xh[c % 2][:]: E.tensor_sub(o, a, b), waits=(eh,))
            mm(P, lb["sq_ps"][0:1, :], cst_ones_r[:], xh[c % 2][:],
               start=(c == 0), stop=False,
               waits=(eh,) + (tuple(prev_rows) if c == 0 else ()))
            e_s = mm(P, lb["sq_ps"][0:1, :], cst_ones_r[:], xl[c % 2][:],
                     start=False, stop=(c == DC - 1), waits=(el,), inc=1)
            mmev[("s", c)] = e_s
        m_row, v_row, mm_row = lb["m_row"], lb["v_row"], lb["mm_row"]
        em = vop(P, lambda E: E.tensor_scalar_mul(m_row[:], lb["mean_ps"][0:1, :],
                                                  1.0 / D), waits=(e_m,))
        es = vop(P, lambda E: E.tensor_scalar_mul(v_row[:], lb["sq_ps"][0:1, :],
                                                  1.0 / D), waits=(e_s,))
        e1 = vop(P, lambda E: E.tensor_mul(mm_row[:], m_row[:], m_row[:]),
                 waits=(em,))
        e2 = vop(P, lambda E: E.tensor_sub(v_row[:], v_row[:], mm_row[:]),
                 waits=(es, e1))
        e2b = vop(P, lambda E: E.tensor_scalar_add(v_row[:], v_row[:],
                                                   float(eps)), waits=(e2,))
        e2c = vop(P, lambda E: E.reciprocal(v_row[:], v_row[:]), waits=(e2b,))
        er = act(P, v_row[:], v_row[:], AF.Sqrt, waits=(e2c,))
        emb = bcast_r(lb["mb_ps"][:], m_row[:], lb["mh"][:], lb["ml"][:],
                      waits=(em,))
        erb = bcast_r(lb["rb_ps"][:], v_row[:], lb["vh"][:], lb["vl"][:],
                      waits=(er,))
        lb["prev_rows"] = (em, es)
        return emb, erb

    def mk_ln_bufs(stk, tag):
        return {
            "sq0": stk.enter_context(nc.sbuf_tensor(tag + "_sq0", [128, TQ], F32)),
            "sq1": stk.enter_context(nc.sbuf_tensor(tag + "_sq1", [128, TQ], F32)),
            "m_row": stk.enter_context(nc.sbuf_tensor(tag + "_m", [1, TQ], F32)),
            "v_row": stk.enter_context(nc.sbuf_tensor(tag + "_v", [1, TQ], F32)),
            "mm_row": stk.enter_context(nc.sbuf_tensor(tag + "_mm", [1, TQ], F32)),
            "xh0": stk.enter_context(nc.sbuf_tensor(tag + "_xh0", [128, TQ], F32R)),
            "xh1": stk.enter_context(nc.sbuf_tensor(tag + "_xh1", [128, TQ], F32R)),
            "xl0": stk.enter_context(nc.sbuf_tensor(tag + "_xl0", [128, TQ], F32R)),
            "xl1": stk.enter_context(nc.sbuf_tensor(tag + "_xl1", [128, TQ], F32R)),
            "mh": stk.enter_context(nc.sbuf_tensor(tag + "_mh", [1, TQ], F32R)),
            "ml": stk.enter_context(nc.sbuf_tensor(tag + "_ml", [1, TQ], F32R)),
            "vh": stk.enter_context(nc.sbuf_tensor(tag + "_vh", [1, TQ], F32R)),
            "vl": stk.enter_context(nc.sbuf_tensor(tag + "_vl", [1, TQ], F32R)),
            "mean_ps": stk.enter_context(nc.psum_tensor(tag + "_mps", [128, TQ], F32)),
            "sq_ps": stk.enter_context(nc.psum_tensor(tag + "_sps", [128, TQ], F32)),
            "mb_ps": stk.enter_context(nc.psum_tensor(tag + "_mb", [128, TQ], F32)),
            "rb_ps": stk.enter_context(nc.psum_tensor(tag + "_rb", [128, TQ], F32)),
            "_smm": {},
        }

    with (
        nc.sbuf_tensor("p0_in", [128, 2 * HID], F32) as p0_in,
        nc.sbuf_tensor("p0_hi", [128, 2 * D], F32R) as p0_hi,
        nc.sbuf_tensor("p0_lo", [128, 2 * D], F32R) as p0_lo,
        nc.sbuf_tensor("p0_sg", [128, 2 * HID], BF16) as p0_sg,
    ):
        attn_jobs = []
        for oc in range(DC):
            attn_jobs.append((qT_in[oc], qTr_hi[oc], qTr_lo[oc], D))
            attn_jobs.append((oT_in[oc], oTr_hi[oc], oTr_lo[oc], D))
        attn_jobs.append((kT_in[0], kTr_hi[0], kTr_lo[0], D))
        attn_jobs.append((vT_in[0], vTr_hi[0], vTr_lo[0], D))
        if stop_after == "p0s":
            attn_jobs = attn_jobs[:4]
        elif stop_after.startswith(("p0f", "p0d")):
            attn_jobs = []
        slot_ev = [None, None]
        for j, (src, dst_hi, dst_lo, w) in enumerate(attn_jobs):
            sl = j % 2
            off = sl * HID
            offd = sl * D
            iv, hv, lv = (p0_in[:, off:off + w], p0_hi[:, offd:offd + w],
                          p0_lo[:, offd:offd + w])
            ldch = "p0a" if sl == 0 else "p0b"
            stch = "p0sa" if sl == 0 else "p0sb"
            eld = dma(P, "sync", ldch, iv, src,
                      waits=(slot_ev[sl],) if slot_ev[sl] else ())
            ehi = act(P, hv, iv, AF.Copy, waits=(eld,))
            elo = vop(P, lambda E, o=lv, a=iv, b=hv: E.tensor_sub(o, a, b),
                      waits=(ehi,))
            dma(P, "sync", stch, dst_hi, hv, waits=(ehi,))
            e2 = dma(P, "sync", stch, dst_lo, lv, waits=(elo,))
            slot_ev[sl] = e2
        ffn_jobs = []
        for hc in range(HC):
            ffn_jobs.append((gT_in[hc], gTs[hc], D))
            ffn_jobs.append((uT_in[hc], uTs[hc], D))
        for oc in range(DC):
            ffn_jobs.append((dT_in[oc], dTs[oc], HID))
        if stop_after == "p0s":
            ffn_jobs = ffn_jobs[:4]
        elif stop_after == "p0a":
            ffn_jobs = []
        elif stop_after == "p0d":
            ffn_jobs = ffn_jobs[-DC:]
        elif stop_after == "p0f":
            ffn_jobs = ffn_jobs[:-DC]
        elif stop_after.startswith("p0f"):
            ffn_jobs = ffn_jobs[:int(stop_after[3:])]
        slot_ev2 = [None, None]
        for j, (src, dst, w) in enumerate(ffn_jobs):
            sl = j % 2
            off = sl * HID
            iv, sv = p0_in[:, off:off + w], p0_sg[:, off:off + w]
            ldch = "p0a" if sl == 0 else "p0b"
            stch = "p0sa" if sl == 0 else "p0sb"
            base = (slot_ev2[sl], slot_ev[sl]) if slot_ev2[sl] else \
                (slot_ev[sl],) if slot_ev[sl] else ()
            eld = dma(P, "sync", ldch, iv, src, waits=tuple(e for e in base if e))
            esg = act(P, sv, iv, AF.Sign, waits=(eld,))
            est = dma(P, "sync", stch, dst, sv, waits=(esg,))
            slot_ev2[sl] = est
        p0_done_a = Evt("p0sa", P.counts["p0sa"], "p0sa")
        p0_done_b = Evt("p0sb", P.counts["p0sb"], "p0sb")
        P.flush()

    with (
        nc.sbuf_tensor("K_hi", [128, T], F32R) as K_hi,
        nc.sbuf_tensor("K_lo", [128, T], F32R) as K_lo,
        nc.sbuf_tensor("V_hi", [128, T], F32R) as V_hi,
        nc.sbuf_tensor("V_lo", [128, T], F32R) as V_lo,
    ):
        a1_q_done = {}
        x1_store_last = None

        for seq in ((1,) if stop_after == "a1x" else range(SEQS if not stop_after.startswith("p0") else 0)):
            with ExitStack() as stk:
                xa = [stk.enter_context(
                    nc.sbuf_tensor(f"a1x{seq}_{c}", [128, TQ], F32))
                    for c in range(DC)]
                lnh = stk.enter_context(
                    nc.sbuf_tensor(f"a1lnh{seq}", [128, DC * TQ], F32R))
                lnl = stk.enter_context(
                    nc.sbuf_tensor(f"a1lnl{seq}", [128, DC * TQ], F32R))
                lnt = [stk.enter_context(
                    nc.sbuf_tensor(f"a1lnt{seq}_{i}", [128, TQ], F32))
                    for i in range(2)]
                lb = mk_ln_bufs(stk, f"a1{seq}")
                wq = [stk.enter_context(
                    nc.sbuf_tensor(f"a1wq{seq}_{i}", [128, D], F32R))
                    for i in range(4)]
                ev_out = stk.enter_context(
                    nc.sbuf_tensor(f"a1ev{seq}", [128, TQ], F32))
                V_fm = stk.enter_context(
                    nc.sbuf_tensor(f"a1vfm{seq}", [128, T], F32))
                proj_ps = [stk.enter_context(
                    nc.psum_tensor(f"a1p{seq}_{i}", [128, TQ], F32))
                    for i in range(2)]
                tr_ps = stk.enter_context(
                    nc.psum_tensor(f"a1tr{seq}", [128, TQ], F32))

                wslot_ev = [None, None]
                psum_ev = [None, None]
                ev_store_ev = None
                kv_done = []
                for it in range(NT):
                    P.wait_fence(P.fence())
                    lb["_smm"] = {}
                    t0 = it * TQ
                    e_x = [dma(P, "scalar", "xin", xa[c][:],
                               x_in[seq, c * 128:(c + 1) * 128, t0:t0 + TQ])
                           for c in range(DC)]
                    e_x = [e_x[-1]] * DC
                    emb, erb = ln_rows_emit(
                        P, [xa[c][:] for c in range(DC)], e_x, lb, 1e-5)
                    if paranoid:
                        P.wait_fence(P.fence())
                    e_lh = [None] * DC
                    e_ll = [None] * DC
                    for c in range(DC):
                        hv = lnh[:, c * TQ:(c + 1) * TQ]
                        lv = lnl[:, c * TQ:(c + 1) * TQ]
                        w1 = [e_x[c], emb]
                        if c >= 2:
                            w1.append(e_lh[c - 2])
                        e1 = vop(P, lambda E, o=lnt[c % 2][:], a=xa[c][:],
                                 b=lb["mb_ps"][:]: E.tensor_sub(o, a, b),
                                 waits=tuple(w1))
                        e2 = vop(P, lambda E, o=lnt[c % 2][:], a=lnt[c % 2][:],
                                 b=lb["rb_ps"][:]: E.tensor_mul(o, a, b),
                                 waits=(erb,))
                        e_lh[c] = act(P, hv, lnt[c % 2][:], AF.Copy,
                                      waits=(e2,))
                        e_ll[c] = vop(P, lambda E, o=lv, a=lnt[c % 2][:], b=hv:
                                      E.tensor_sub(o, a, b), waits=(e_lh[c],))
                    if paranoid:
                        P.wait_fence(P.fence())
                    if dbg and it == 0:
                        base = seq * 6
                        dma(P, "scalar", "mxs", dbg_rows[base + 0:base + 1, :],
                            lb["m_row"][:], waits=(emb,))
                        dma(P, "scalar", "mxs", dbg_rows[base + 1:base + 2, :],
                            lb["v_row"][:], waits=(erb,))
                        dma(P, "scalar", "mxs", dbg_rows[base + 2:base + 3, :].bitcast(F32R),
                            lnh[0:1, 0:TQ], waits=(e_lh[0],))
                        dma(P, "scalar", "mxs", dbg_rows[base + 3:base + 4, :].bitcast(F32R),
                            lnl[0:1, 0:TQ], waits=(e_ll[0],))
                        e_d1 = vop(P, lambda E, o=lnt[0][:], i=lb["mb_ps"][:]:
                                   E.tensor_copy(o, i), waits=(emb,))
                        dma(P, "scalar", "mxs", dbg_rows[base + 4:base + 5, :],
                            lnt[0][0:1, :], waits=(e_d1,))
                        e_d2 = vop(P, lambda E, o=lnt[0][:], i=lb["rb_ps"][:]:
                                   E.tensor_copy(o, i), waits=(erb,))
                        dma(P, "scalar", "mxs", dbg_rows[base + 5:base + 6, :],
                            lnt[0][0:1, :], waits=(e_d2,))
                    jobs = [("q", oc) for oc in range(DC)] + [("k", 0), ("v", 0)]
                    for jn, (kind, oc) in enumerate(jobs):
                        if paranoid:
                            P.wait_fence(P.fence())
                        wsl = jn % 2
                        psl = jn % 2
                        if kind == "q":
                            src_h, src_l = qTr_hi[oc], qTr_lo[oc]
                        elif kind == "k":
                            src_h, src_l = kTr_hi[0], kTr_lo[0]
                        else:
                            src_h, src_l = vTr_hi[0], vTr_lo[0]
                        wh, wl = wq[2 * wsl], wq[2 * wsl + 1]
                        wch = "wA" if wsl == 0 else "wB"
                        ww = (wslot_ev[wsl], p0_done_a, p0_done_b) if wslot_ev[wsl] \
                            else (p0_done_a, p0_done_b)
                        e_wh = dma(P, "sync", wch, wh[:], src_h, waits=ww)
                        e_wl = dma(P, "sync", wch, wl[:], src_l, waits=ww)
                        ps = proj_ps[psl]
                        pw = (psum_ev[psl],) if psum_ev[psl] else ()
                        e_mm = None
                        for c in range(DC):
                            whc = wh[:, c * 128:(c + 1) * 128]
                            wlc = wl[:, c * 128:(c + 1) * 128]
                            rh = lnh[:, c * TQ:(c + 1) * TQ]
                            rl = lnl[:, c * TQ:(c + 1) * TQ]
                            first, last = (c == 0), (c == DC - 1)
                            mm(P, ps[:], whc, rh, start=first, stop=False,
                               waits=(e_wh, e_lh[c]) + (pw if first else ()))
                            mm(P, ps[:], whc, rl, start=False, stop=False,
                               waits=(e_ll[c],))
                            e_mm = mm(P, ps[:], wlc, rh, start=False, stop=last,
                                      waits=(e_wl,), inc=1 if last else 0)
                        wslot_ev[wsl] = e_mm
                        if kind == "q":
                            ee = vop(P, lambda E, o=ev_out[:], i=ps[:]:
                                     E.tensor_copy(o, i),
                                     waits=(e_mm,) + ((ev_store_ev,)
                                                      if ev_store_ev else ()))
                            ev_store_ev = dma(
                                P, "scalar", "qs",
                                q_s[seq, oc * 128:(oc + 1) * 128, t0:t0 + TQ],
                                ev_out[:], waits=(ee,))
                            psum_ev[psl] = ee
                        elif kind == "k":
                            eh = act(P, K_hi[:, t0:t0 + TQ], ps[:], AF.Copy,
                                     waits=(e_mm,))
                            ee = vop(P, lambda E, o=K_lo[:, t0:t0 + TQ],
                                     a=ps[:], b=K_hi[:, t0:t0 + TQ]:
                                     E.tensor_sub(o, a, b), waits=(eh,))
                            psum_ev[psl] = ee
                            kv_done.append(ee)
                        else:
                            ee = vop(P, lambda E, o=V_fm[:, t0:t0 + TQ], i=ps[:]:
                                     E.tensor_copy(o, i), waits=(e_mm,))
                            psum_ev[psl] = ee
                            kv_done.append(ee)
                vt_done = []
                e_prev = None
                for kc in range(KC):
                    e_t = tr(P, tr_ps[:, 0:128], V_fm[:, kc * 128:(kc + 1) * 128],
                             cst_id[:],
                             waits=(kv_done[-1], e_id) +
                                   ((e_prev,) if e_prev else ()))
                    eh = act(P, V_hi[:, kc * 128:(kc + 1) * 128],
                             tr_ps[:, 0:128], AF.Copy, waits=(e_t,))
                    e_c = vop(P, lambda E, o=V_lo[:, kc * 128:(kc + 1) * 128],
                              a=tr_ps[:, 0:128],
                              b=V_hi[:, kc * 128:(kc + 1) * 128]:
                              E.tensor_sub(o, a, b), waits=(eh,))
                    e_prev = e_c
                    vt_done.append(e_c)
                a1_q_done[seq] = Evt("qs", P.counts["qs"], "qs")
                P.flush()

            deps_kv = tuple(kv_done[-1:]) + tuple(vt_done[-1:]) + \
                tuple(vt_done[:1])
            a2c = {"q": [None, None], "exp": [None, None, None],
                   "sc": [None, None, None],
                   "ao": [None, None], "att": None, "es": None, "eib": None,
                   "eic": None, "wo": [None, None], "xa": [None, None],
                   "rows": None}
            for qt in range(NT if not (stop_after.startswith("p0") or stop_after in ("a1", "a1x")) else 0):
                t0 = qt * TQ
                with ExitStack() as stk:
                    P.wait_fence(P.fence())
                    tag = f"a2{seq}_{qt}"
                    atth = stk.enter_context(
                        nc.sbuf_tensor(f"{tag}ah", [128, DC * TQ], F32R))
                    attl = stk.enter_context(
                        nc.sbuf_tensor(f"{tag}al", [128, DC * TQ], F32R))
                    wo = [stk.enter_context(
                        nc.sbuf_tensor(f"{tag}wo{i}", [128, D // 2], F32R))
                        for i in range(4)]
                    osb = stk.enter_context(
                        nc.sbuf_tensor(f"{tag}osb", [128, DC * TQ], F32))
                    ao_ps = [stk.enter_context(
                        nc.psum_tensor(f"{tag}ao{i}", [128, TQ], F32))
                        for i in range(2)]

                    att_split_ev = [None] * DC
                    ao_slot_ev = a2c["ao"]
                    with ExitStack() as s1:
                        qh = [s1.enter_context(
                            nc.sbuf_tensor(f"{tag}q{i}", [128, TQ], F32))
                            for i in range(2)]
                        qhh = [s1.enter_context(
                            nc.sbuf_tensor(f"{tag}qh{i}", [128, TQ], F32R))
                            for i in range(2)]
                        qhl = [s1.enter_context(
                            nc.sbuf_tensor(f"{tag}ql{i}", [128, TQ], F32R))
                            for i in range(2)]
                        expb = [s1.enter_context(
                            nc.sbuf_tensor(f"{tag}e{i}", [128, TQ], F32))
                            for i in range(3)]
                        exph = [s1.enter_context(
                            nc.sbuf_tensor(f"{tag}eh{i}", [128, TQ], F32R))
                            for i in range(3)]
                        expl = [s1.enter_context(
                            nc.sbuf_tensor(f"{tag}el{i}", [128, TQ], F32R))
                            for i in range(3)]
                        sacc = s1.enter_context(
                            nc.sbuf_tensor(f"{tag}sacc", [128, TQ], F32))
                        srow = s1.enter_context(
                            nc.sbuf_tensor(f"{tag}srow", [1, TQ], F32))
                        srh = s1.enter_context(
                            nc.sbuf_tensor(f"{tag}srh", [1, TQ], F32R))
                        srl = s1.enter_context(
                            nc.sbuf_tensor(f"{tag}srl", [1, TQ], F32R))
                        isb = s1.enter_context(
                            nc.sbuf_tensor(f"{tag}isb", [128, TQ], F32))
                        att_t = s1.enter_context(
                            nc.sbuf_tensor(f"{tag}att", [128, TQ], F32))
                        sc_ps = [s1.enter_context(
                            nc.psum_tensor(f"{tag}sc{i}", [128, TQ], F32))
                            for i in range(3)]
                        s_ps = s1.enter_context(
                            nc.psum_tensor(f"{tag}sps", [128, TQ], F32))
                        is_ps = s1.enter_context(
                            nc.psum_tensor(f"{tag}isps", [128, TQ], F32))
                        q_slot_ev = a2c["q"]
                        exp_slot_ev = a2c["exp"]
                        sc_slot_ev = a2c["sc"]
                        att_tmp_ev = a2c["att"]
                        e_s_prev = a2c["es"]
                        e_ib_prev = a2c["eib"]
                        e_ic_prev = a2c["eic"]
                        for h in range(HEADS):
                            qsl = h % 2
                            qw = (q_slot_ev[qsl],) if q_slot_ev[qsl] else ()
                            e_q = dma(P, "sync",
                                      "qlA" if qsl == 0 else "qlB",
                                      qh[qsl][:],
                                      q_s[seq, h * 128:(h + 1) * 128,
                                          t0:t0 + TQ],
                                      waits=qw + (a1_q_done[seq],))
                            e_qh = act(P, qhh[qsl][:], qh[qsl][:], AF.Copy,
                                       waits=(e_q,) + qw)
                            e_ql = vop(P, lambda E, o=qhl[qsl][:],
                                       a=qh[qsl][:], b=qhh[qsl][:]:
                                       E.tensor_sub(o, a, b), waits=(e_qh,))
                            e_sacc = None
                            e_ao_last = None
                            split_ev = [None] * KC
                            for kc in range(KC + 1):
                                if kc < KC:
                                    ssl = kc % 3
                                    scw = (sc_slot_ev[ssl],) \
                                        if sc_slot_ev[ssl] else ()
                                    kh = K_hi[:, kc * 128:(kc + 1) * 128]
                                    kl = K_lo[:, kc * 128:(kc + 1) * 128]
                                    mm(P, sc_ps[ssl][:], kh, qhh[qsl][:],
                                       start=True, stop=False,
                                       waits=(e_qh,) + deps_kv + scw)
                                    mm(P, sc_ps[ssl][:], kh, qhl[qsl][:],
                                       start=False, stop=False, waits=(e_ql,))
                                    e_sc = mm(P, sc_ps[ssl][:], kl,
                                              qhh[qsl][:], start=False,
                                              stop=True, inc=1)
                                    ew = exp_slot_ev[ssl] or ()
                                    e_e = act(P, expb[ssl][:], sc_ps[ssl][:],
                                              AF.Exp, bias=cst_stab[:],
                                              scale=cst_scl[:],
                                              waits=(e_sc, e_stab, e_scl) +
                                                    tuple(ew))
                                    sc_slot_ev[ssl] = e_e
                                    e_eh = act(P, exph[ssl][:], expb[ssl][:],
                                               AF.Copy, waits=(e_e,))
                                    e_el = vop(P, lambda E, o=expl[ssl][:],
                                               a=expb[ssl][:], b=exph[ssl][:]:
                                               E.tensor_sub(o, a, b),
                                               waits=(e_eh,))
                                    split_ev[kc] = (e_eh, e_el)
                                    if kc == 0:
                                        sw = (e_e,) + ((e_s_prev,)
                                                       if e_s_prev else ())
                                        e_sacc = vop(P, lambda E, o=sacc[:],
                                                     i=expb[ssl][:]:
                                                     E.tensor_copy(o, i),
                                                     waits=sw)
                                    else:
                                        e_sacc = vop(P, lambda E, o=sacc[:],
                                                     a=sacc[:], b=expb[ssl][:]:
                                                     E.tensor_add(o, a, b),
                                                     waits=(e_e,))
                                if kc >= 1:
                                    kv = kc - 1
                                    vsl = kv % 3
                                    vh = V_hi[:, kv * 128:(kv + 1) * 128]
                                    vl = V_lo[:, kv * 128:(kv + 1) * 128]
                                    e_eh2, e_el2 = split_ev[kv]
                                    aw = [e_eh2]
                                    if kv == 0 and ao_slot_ev[h % 2] is not None:
                                        aw.append(ao_slot_ev[h % 2])
                                    mm(P, ao_ps[h % 2][:], vh, exph[vsl][:],
                                       start=(kv == 0), stop=False,
                                       waits=tuple(aw))
                                    mm(P, ao_ps[h % 2][:], vh, expl[vsl][:],
                                       start=False, stop=False,
                                       waits=(e_el2,))
                                    e_ao_last = mm(P, ao_ps[h % 2][:], vl,
                                                   exph[vsl][:], start=False,
                                                   stop=(kv == KC - 1),
                                                   inc=1 if kv == KC - 1
                                                   else 0)
                                    exp_slot_ev[vsl] = (e_ao_last, e_sacc)
                            q_slot_ev[qsl] = e_ao_last
                            ssp = (e_s_prev,) if e_s_prev else ()
                            e_kh = act(P, qhh[qsl][:], sacc[:], AF.Copy,
                                       waits=(e_sacc, e_ao_last) + ssp)
                            e_kl = vop(P, lambda E, o=qhl[qsl][:], a=sacc[:],
                                       b=qhh[qsl][:]: E.tensor_sub(o, a, b),
                                       waits=(e_kh,))
                            mm(P, s_ps[0:1, :], cst_ones_r[:], qhh[qsl][:],
                               start=True, stop=False, waits=(e_kh, e_onesr) + ssp)
                            e_s = mm(P, s_ps[0:1, :], cst_ones_r[:],
                                     qhl[qsl][:], start=False, stop=True,
                                     waits=(e_kl,), inc=1)
                            e_s_prev = e_s
                            e_r = vop(P, lambda E, o=srow[:], i=s_ps[0:1, :]:
                                      E.reciprocal(o, i),
                                      waits=(e_s,) + ((e_ib_prev,)
                                                      if e_ib_prev else ()))
                            e_ib = bcast_r(is_ps[:], srow[:], srh[:], srl[:],
                                           waits=(e_r,) + ((e_ic_prev,)
                                                           if e_ic_prev
                                                           else ()))
                            e_ib_prev = e_ib
                            e_ic = vop(P, lambda E, o=isb[:], i=is_ps[:]:
                                       E.tensor_copy(o, i), waits=(e_ib,))
                            e_ic_prev = e_ic
                            e_at = vop(P, lambda E, o=att_t[:],
                                       a=ao_ps[h % 2][:], b=isb[:]:
                                       E.tensor_mul(o, a, b),
                                       waits=(e_ao_last, e_ic) +
                                             ((att_tmp_ev,) if att_tmp_ev
                                              else ()))
                            ao_slot_ev[h % 2] = e_at
                            hv = atth[:, h * TQ:(h + 1) * TQ]
                            lv = attl[:, h * TQ:(h + 1) * TQ]
                            hw_ = (e_at,)
                            if h < 2:
                                hw_ = hw_ + tuple(e for e in a2c["wo"] if e)
                            e_h = act(P, hv, att_t[:], AF.Copy, waits=hw_)
                            e_l = vop(P, lambda E, o=lv, a=att_t[:], b=hv:
                                      E.tensor_sub(o, a, b), waits=(e_h,))
                            att_tmp_ev = e_l
                            att_split_ev[h] = e_l

                    a2c["att"] = att_tmp_ev
                    a2c["es"] = e_s_prev
                    a2c["eib"] = e_ib_prev
                    a2c["eic"] = e_ic_prev
                    with ExitStack() as s2:
                        lb2 = mk_ln_bufs(s2, tag + "ln")
                        lb2["prev_rows"] = a2c["rows"] or ()
                        xa = [s2.enter_context(
                            nc.sbuf_tensor(f"{tag}x{i}", [128, TQ], F32))
                            for i in range(2)]
                        xguard = tuple(P.fence())
                        wslot_ev = a2c["wo"]
                        oproj_ev = [None] * DC
                        for oc in range(DC):
                            ps = ao_ps[oc % 2]
                            pw = (ao_slot_ev[oc % 2],) \
                                if ao_slot_ev[oc % 2] else ()
                            e_mm = None
                            for half in range(2):
                                wh, wl = wo[2 * half], wo[2 * half + 1]
                                wch = "wA" if half == 0 else "wB"
                                ww = ((wslot_ev[half],) if wslot_ev[half]
                                      else ()) + (p0_done_a, p0_done_b)
                                e_wh = dma(P, "sync", wch, wh[:],
                                           oTr_hi[oc, :, half * (D // 2):
                                                  (half + 1) * (D // 2)],
                                           waits=ww)
                                e_wl = dma(P, "sync", wch, wl[:],
                                           oTr_lo[oc, :, half * (D // 2):
                                                  (half + 1) * (D // 2)],
                                           waits=ww)
                                for k in range(DC // 2):
                                    c = half * (DC // 2) + k
                                    whc = wh[:, k * 128:(k + 1) * 128]
                                    wlc = wl[:, k * 128:(k + 1) * 128]
                                    rh = atth[:, c * TQ:(c + 1) * TQ]
                                    rl = attl[:, c * TQ:(c + 1) * TQ]
                                    first = (c == 0)
                                    last = (c == DC - 1)
                                    mm(P, ps[:], whc, rh, start=first,
                                       stop=False,
                                       waits=(e_wh, att_split_ev[c]) +
                                             (pw if first else ()))
                                    mm(P, ps[:], whc, rl, start=False,
                                       stop=False)
                                    e_mm = mm(P, ps[:], wlc, rh, start=False,
                                              stop=last, waits=(e_wl,),
                                              inc=1 if last or k == DC // 2 - 1
                                              else 0)
                                wslot_ev[half] = e_mm
                            oevw = (e_mm,)
                            if oc < 2 and a2c["xa"][oc % 2]:
                                oevw = (e_mm, a2c["xa"][oc % 2])
                            ee = act(P, osb[:, oc * TQ:(oc + 1) * TQ], ps[:],
                                     AF.Copy, waits=oevw)
                            ao_slot_ev[oc % 2] = ee
                            oproj_ev[oc] = ee
                        o_tiles = [osb[:, c * TQ:(c + 1) * TQ]
                                   for c in range(DC)]
                        emb, erb = ln_rows_emit(P, o_tiles, oproj_ev, lb2, 1e-5)
                        xa_ev = a2c["xa"]
                        for c in range(DC):
                            xw = list(xguard)
                            if xa_ev[c % 2]:
                                xw.append(xa_ev[c % 2])
                            e_x = dma(P, "scalar", "xin", xa[c % 2][:],
                                      x_in[seq, c * 128:(c + 1) * 128,
                                           t0:t0 + TQ], waits=tuple(xw))
                            vop(P, lambda E, o=o_tiles[c], a=o_tiles[c],
                                b=lb2["mb_ps"][:]: E.tensor_sub(o, a, b),
                                waits=(oproj_ev[c], emb, erb))
                            vop(P, lambda E, o=o_tiles[c], a=o_tiles[c],
                                b=lb2["rb_ps"][:]: E.tensor_mul(o, a, b))
                            e3 = vop(P, lambda E, o=o_tiles[c], a=o_tiles[c],
                                     b=xa[c % 2][:]: E.tensor_add(o, a, b),
                                     waits=(e_x,))
                            xa_ev[c % 2] = e3
                            x1_store_last = dma(
                                P, "scalar", "x1s",
                                x1_s[seq, c * 128:(c + 1) * 128, t0:t0 + TQ],
                                o_tiles[c], waits=(e3,))
                        a2c["rows"] = lb2.get("prev_rows")
                    P.flush()

    out_store_last = None
    for tt in range(SEQS * T // TQ if stop_after == "ffn" else 0):
        seq, t0 = divmod(tt * TQ, T)
        with ExitStack() as stk:
            P.wait_fence(P.fence())
            tag = f"f{tt}"
            ph = stk.enter_context(
                nc.sbuf_tensor(f"{tag}ph", [128, HC * TQ], BF16))
            pl = stk.enter_context(
                nc.sbuf_tensor(f"{tag}pl", [128, HC * TQ], BF16))
            lb = mk_ln_bufs(stk, tag + "ln")
            s23 = stk.enter_context(ExitStack())
            x1 = [s23.enter_context(
                nc.sbuf_tensor(f"{tag}x{c}", [128, TQ], F32))
                for c in range(DC)]
            xi = s23.enter_context(
                nc.sbuf_tensor(f"{tag}xi", [128, DC * TQ], BF16))
            sb_b = s23.enter_context(
                nc.sbuf_tensor(f"{tag}sb", [128, TQ], F32))
            isv_b = s23.enter_context(
                nc.sbuf_tensor(f"{tag}isv", [128, TQ], F32))

            with ExitStack() as s2:
                _l0 = s2.enter_context(
                    nc.sbuf_tensor(f"{tag}l0", [128, TQ], F32))
                ln3t = [_l0, _l0]
                tmax = s2.enter_context(
                    nc.sbuf_tensor(f"{tag}tm", [128, TQ], F32))
                t_sx = lb["sq0"]
                t_c = lb["sq1"]
                t_r = lb["xh0"]
                t_sg = lb["xl0"]
                mx4 = s2.enter_context(
                    nc.sbuf_tensor(f"{tag}mx4", [128, 4], F32))
                amax_r = s2.enter_context(
                    nc.sbuf_tensor(f"{tag}am", [1, TQ], F32))
                srow2 = s2.enter_context(
                    nc.sbuf_tensor(f"{tag}sr", [1, TQ], F32))
                fbh = lb["mh"]
                fbl = lb["ml"]
                tr_ps = s2.enter_context(
                    nc.psum_tensor(f"{tag}trp", [128, TQ], F32))

                sb_ps = s2.enter_context(
                    nc.psum_tensor(f"{tag}sbp", [128, TQ], F32))
                iv_ps = s2.enter_context(
                    nc.psum_tensor(f"{tag}ivp", [128, TQ], F32))

                xw = (x1_store_last,) if x1_store_last else ()
                e_x = [dma(P, "scalar", "xin", x1[c][:],
                           x1_s[seq, c * 128:(c + 1) * 128, t0:t0 + TQ],
                           waits=xw)
                       for c in range(DC)]
                e_x = [e_x[-1]] * DC
                emb, erb = ln_rows_emit(P, [x1[c][:] for c in range(DC)],
                                        e_x, lb, 1e-8)
                e_mx = None
                ln_last = [None, None]
                for c in range(DC):
                    lt = ln3t[c % 2]
                    w1 = [e_x[c], emb, erb]
                    if ln_last[c % 2]:
                        w1.append(ln_last[c % 2])
                    vop(P, lambda E, o=lt[:], a=x1[c][:],
                        b=lb["mb_ps"][:]: E.tensor_sub(o, a, b),
                        waits=tuple(w1))
                    vop(P, lambda E, o=lt[:], a=lt[:],
                        b=lb["rb_ps"][:]: E.tensor_mul(o, a, b))
                    if c == 0:
                        e_mx = vop(P, lambda E, o=tmax[:], a=lt[:],
                                   b=lt[:]: E.scalar_tensor_tensor(
                                       o, a, -1.0, b, ALU.mult, ALU.max))
                    else:
                        e_a = vop(P, lambda E, o=t_sx[:], a=lt[:],
                                  b=lt[:]: E.scalar_tensor_tensor(
                                      o, a, -1.0, b, ALU.mult, ALU.max))
                        e_mx = vop(P, lambda E, o=tmax[:], a=tmax[:],
                                   b=t_sx[:]: E.tensor_max(o, a, b))
                    ln_last[c % 2] = e_mx
                e_red = None
                for j in range(4):
                    e_t = tr(P, tr_ps[:, 0:128], tmax[:, j * 128:(j + 1) * 128],
                             cst_id[:], waits=(e_mx, e_id) +
                             ((e_red,) if e_red else ()))
                    e_red = vop(P, lambda E, o=mx4[:, j:j + 1], i=tr_ps[:, 0:128]:
                                E.reduce_max(o, i, AXX), waits=(e_t,))
                e_t4 = tr(P, tr_ps[0:4, 0:128], mx4[:], cst_id[:], waits=(e_red,))
                e_cp = vop(P, lambda E, o=tmax[:, 0:128], i=tr_ps[:, 0:128]:
                           E.tensor_copy(o, i), waits=(e_t4,))
                e_am = None
                for j in range(4):
                    e_am = dma(P, "sync", "mxs", amax_r[:, j * 128:(j + 1) * 128],
                               tmax[j:j + 1, 0:128], waits=(e_cp,))
                e_rec = vop(P, lambda E, o=srow2[:], i=amax_r[:]:
                            E.reciprocal(o, i), waits=(e_am,))
                e_sr = vop(P, lambda E, o=srow2[:]: E.tensor_scalar_mul(
                    o, o, 127.0), waits=(e_rec,))
                e_ir = vop(P, lambda E, o=amax_r[:]: E.tensor_scalar_mul(
                    o, o, 1.0 / 127.0), waits=(e_rec,))
                e_sb = bcast_r(sb_ps[:], srow2[:], fbh[:], fbl[:],
                               waits=(e_sr,))
                e_sbc = vop(P, lambda E, o=sb_b[:], i=sb_ps[:]:
                            E.tensor_copy(o, i), waits=(e_sb,))
                e_iv = bcast_r(iv_ps[:], amax_r[:], fbh[:], fbl[:],
                               waits=(e_ir, e_sbc))
                e_ivc = vop(P, lambda E, o=isv_b[:], i=iv_ps[:]:
                            E.tensor_copy(o, i), waits=(e_iv,))
                e_xi = [None] * DC
                e_sg_prev = None
                for c in range(DC):
                    xv = xi[:, c * TQ:(c + 1) * TQ]
                    lt = ln3t[c % 2]
                    vop(P, lambda E, o=lt[:], a=x1[c][:], b=lb["mb_ps"][:]:
                        E.tensor_sub(o, a, b))
                    vop(P, lambda E, o=lt[:], a=lt[:], b=lb["rb_ps"][:]:
                        E.tensor_mul(o, a, b))
                    e1 = vop(P, lambda E, o=t_sx[:], a=lt[:], b=sb_b[:]:
                             E.tensor_mul(o, a, b), waits=(e_sbc,))
                    e_sg = act(P, t_sg[:], t_sx[:], AF.Sign,
                               waits=(e1,) + ((e_sg_prev,)
                                              if e_sg_prev else ()))
                    e2 = vop(P, lambda E, o=t_c[:], a=t_sx[:], b=t_sx[:]:
                             E.scalar_tensor_tensor(o, a, -1.0, b, ALU.mult,
                                                    ALU.max))
                    e3 = vop(P, lambda E, o=t_r[:], i=t_c[:]:
                             E.tensor_single_scalar(o, i, 63.5, ALU.is_ge))
                    e3 = vop(P, lambda E, o=t_r[:], i=t_r[:]:
                             E.tensor_scalar_mul(o, i, 64.0))
                    for b in (32, 16, 8, 4, 2, 1):
                        e3 = vop(P, lambda E, o=t_sx[:], a=t_r[:],
                                 bb=float(b), c2=t_c[:]:
                                 E.scalar_tensor_tensor(
                                     o, a, bb - 0.5, c2, ALU.add, ALU.is_le))
                        e3 = vop(P, lambda E, o=t_r[:], a=t_sx[:],
                                 bb=float(b), c2=t_r[:]:
                                 E.scalar_tensor_tensor(
                                     o, a, bb, c2, ALU.mult, ALU.add))
                    e_xi[c] = vop(P, lambda E, o=xv, a=t_r[:], b=t_sg[:]:
                                  E.tensor_mul(o, a, b), waits=(e3, e_sg))
                    e_sg_prev = e_xi[c]

            with ExitStack() as s3:
                wg_b = [s3.enter_context(
                    nc.sbuf_tensor(f"{tag}wg{i}", [128, D // 2], BF16))
                    for i in range(2)]
                wu_b = [s3.enter_context(
                    nc.sbuf_tensor(f"{tag}wu{i}", [128, D // 2], BF16))
                    for i in range(2)]
                gt = lb["sq0"]
                ut = lb["sq1"]
                pt = gt
                g_ps = [s3.enter_context(
                    nc.psum_tensor(f"{tag}gp{i}", [128, TQ], F32))
                    for i in range(2)]
                u_ps = [s3.enter_context(
                    nc.psum_tensor(f"{tag}up{i}", [128, TQ], F32))
                    for i in range(2)]
                wg_ev = [None, None]
                wu_ev = [None, None]
                gp_ev = [None, None]
                up_ev = [None, None]
                tmp_ev = None
                p_evts = []
                s3_guard = tuple(P.fence())
                HD2 = D // 2
                for hc in range(HC):
                    e_g = e_u = None
                    for half in range(2):
                        sl = half
                        ww = ((wg_ev[sl],) if wg_ev[sl] else ()) + \
                            (p0_done_a, p0_done_b) + \
                            (s3_guard if wg_ev[sl] is None else ())
                        e_wg = dma(P, "sync", "wA" if sl == 0 else "wB",
                                   wg_b[sl][:],
                                   gTs[hc, :, half * HD2:(half + 1) * HD2],
                                   waits=ww)
                        ww = ((wu_ev[sl],) if wu_ev[sl] else ()) + \
                            (p0_done_a, p0_done_b) + \
                            (s3_guard if wu_ev[sl] is None else ())
                        e_wu = dma(P, "sync", "wC" if sl == 0 else "wD",
                                   wu_b[sl][:],
                                   uTs[hc, :, half * HD2:(half + 1) * HD2],
                                   waits=ww)
                        for k in range(DC // 2):
                            c = half * (DC // 2) + k
                            first, last = (c == 0), (c == DC - 1)
                            gw = [e_wg, e_xi[c]]
                            if first and gp_ev[hc % 2]:
                                gw.append(gp_ev[hc % 2])
                            e_g = mm(P, g_ps[hc % 2][:],
                                     wg_b[sl][:, k * 128:(k + 1) * 128],
                                     xi[:, c * TQ:(c + 1) * TQ], start=first,
                                     stop=last, waits=tuple(gw),
                                     inc=1 if last or k == DC // 2 - 1 else 0)
                        wg_ev[sl] = e_g
                        for k in range(DC // 2):
                            c = half * (DC // 2) + k
                            first, last = (c == 0), (c == DC - 1)
                            uw = [e_wu, e_xi[c]]
                            if first and up_ev[hc % 2]:
                                uw.append(up_ev[hc % 2])
                            e_u = mm(P, u_ps[hc % 2][:],
                                     wu_b[sl][:, k * 128:(k + 1) * 128],
                                     xi[:, c * TQ:(c + 1) * TQ], start=first,
                                     stop=last, waits=tuple(uw),
                                     inc=1 if last or k == DC // 2 - 1 else 0)
                        wu_ev[sl] = e_u
                    psl = hc % 2
                    e1 = vop(P, lambda E, o=gt[:], a=g_ps[psl][:], b=isv_b[:]:
                             E.tensor_mul(o, a, b),
                             waits=(e_g, e_ivc) + ((tmp_ev,) if tmp_ev
                                                   else ()))
                    gp_ev[psl] = e1
                    e2 = act(P, gt[:], gt[:], AF.Silu, waits=(e1,))
                    e3 = vop(P, lambda E, o=ut[:], a=u_ps[psl][:], b=isv_b[:]:
                             E.tensor_mul(o, a, b), waits=(e_u,))
                    up_ev[psl] = e3
                    e4 = vop(P, lambda E, o=pt[:], a=gt[:], b=ut[:]:
                             E.tensor_mul(o, a, b), waits=(e2, e3))
                    hv = ph[:, hc * TQ:(hc + 1) * TQ]
                    lv = pl[:, hc * TQ:(hc + 1) * TQ]
                    e5 = act(P, hv, pt[:], AF.Copy, waits=(e4,))
                    e6 = vop(P, lambda E, o=lv, a=pt[:], b=hv:
                             E.tensor_sub(o, a, b), waits=(e5,))
                    tmp_ev = e6
                    p_evts.append(e6)

            s23.close()
            with ExitStack() as s4:
                QW = HID // 4
                wd_b = [s4.enter_context(
                    nc.sbuf_tensor(f"{tag}wd{i}", [128, QW], BF16))
                    for i in range(2)]
                x1b = [s4.enter_context(
                    nc.sbuf_tensor(f"{tag}xb{i}", [128, TQ], F32))
                    for i in range(2)]
                dsb = s4.enter_context(
                    nc.sbuf_tensor(f"{tag}dsb", [128, DC * TQ], F32))
                d_ps = [s4.enter_context(
                    nc.psum_tensor(f"{tag}dp{i}", [128, TQ], F32))
                    for i in range(2)]
                s4_guard = tuple(P.fence())
                wd_ev = [None, None]
                dp_ev = [None, None]
                d_ev = [None] * DC
                for oc in range(DC):
                    e_d = None
                    for qr in range(4):
                        sl = qr % 2
                        ww = [p0_done_a, p0_done_b]
                        ww += list(s4_guard)
                        if wd_ev[sl]:
                            ww.append(wd_ev[sl])
                        e_wd = dma(P, "sync", "wE" if sl == 0 else "wF",
                                   wd_b[sl][:],
                                   dTs[oc, :, qr * QW:(qr + 1) * QW],
                                   waits=tuple(ww))
                        for k in range(HC // 4):
                            hc = qr * (HC // 4) + k
                            first = (hc == 0)
                            last = (hc == HC - 1)
                            dw = [e_wd]
                            if first:
                                dw += p_evts
                                if dp_ev[oc % 2]:
                                    dw.append(dp_ev[oc % 2])
                            mm(P, d_ps[oc % 2][:],
                               wd_b[sl][:, k * 128:(k + 1) * 128],
                               ph[:, hc * TQ:(hc + 1) * TQ], start=first,
                               stop=False, waits=tuple(dw))
                            e_d = mm(P, d_ps[oc % 2][:],
                                     wd_b[sl][:, k * 128:(k + 1) * 128],
                                     pl[:, hc * TQ:(hc + 1) * TQ],
                                     start=False, stop=last,
                                     inc=1 if last or k == HC // 4 - 1 else 0)
                        wd_ev[sl] = e_d
                    ee = act(P, dsb[:, oc * TQ:(oc + 1) * TQ], d_ps[oc % 2][:],
                             AF.Copy, waits=(e_d,))
                    dp_ev[oc % 2] = ee
                    d_ev[oc] = ee
                d_tiles = [dsb[:, c * TQ:(c + 1) * TQ] for c in range(DC)]
                lb["_smm"] = {}
                emb, erb = ln_rows_emit(P, d_tiles, d_ev, lb, 1e-5)
                xb_ev = [None, None]
                for c in range(DC):
                    xw = [s for s in s4_guard]
                    if xb_ev[c % 2]:
                        xw.append(xb_ev[c % 2])
                    e_xb = dma(P, "scalar", "xin", x1b[c % 2][:],
                               x1_s[seq, c * 128:(c + 1) * 128, t0:t0 + TQ],
                               waits=tuple(xw))
                    vop(P, lambda E, o=d_tiles[c], a=d_tiles[c],
                        b=lb["mb_ps"][:]: E.tensor_sub(o, a, b),
                        waits=(d_ev[c], emb, erb))
                    vop(P, lambda E, o=d_tiles[c], a=d_tiles[c],
                        b=lb["rb_ps"][:]: E.tensor_mul(o, a, b))
                    e3 = vop(P, lambda E, o=d_tiles[c], a=d_tiles[c],
                             b=x1b[c % 2][:]: E.tensor_add(o, a, b),
                             waits=(e_xb,))
                    xb_ev[c % 2] = e3
                    out_store_last = dma(
                        P, "scalar", "outs",
                        out_hbm[seq, c * 128:(c + 1) * 128, t0:t0 + TQ],
                        d_tiles[c], waits=(e3,))
            P.flush()

    for key in ("outs", "x1s", "qs", "p0sa", "p0sb"):
        if P.counts[key]:
            P.emit("scalar",
                   lambda E, s=P.sems[key], v=P.counts[key]: E.wait_ge(s, v))
    P.flush(final=True)
    return nc


def _layout_inputs(x, q_w, k_w, v_w, o_w, W_g, W_u, W_d):
    def slabify(W, ocs, ics):
        a = W.reshape(ocs, 128, ics, 128)
        a = a.transpose(0, 3, 2, 1)
        return np.ascontiguousarray(a.reshape(ocs, 128, ics * 128))

    qT = slabify(q_w, DC, DC)
    kT = slabify(k_w, 1, DC)
    vT = slabify(v_w, 1, DC)
    oT = slabify(o_w, DC, DC)
    gT = slabify(W_g, HC, DC)
    uT = slabify(W_u, HC, DC)
    dT = slabify(W_d, DC, HC)
    in_maps = []
    for core in range(NCORES):
        xs = x[core * SEQS:(core + 1) * SEQS]
        x_fm = np.ascontiguousarray(xs.transpose(0, 2, 1))
        in_maps.append({"x_fm": x_fm, "qT": qT, "kT": kT, "vT": vT, "oT": oT,
                        "gT": gT, "uT": uT, "dT": dT})
    return in_maps


def kernel(L0_attn_x, L0_attn_q_w, L0_attn_k_w, L0_attn_v_w, L0_attn_o_w,
           L0_ffn_W_g, L0_ffn_W_u, L0_ffn_W_d):
    in_maps = _layout_inputs(
        np.asarray(L0_attn_x, dtype=np.float32),
        np.asarray(L0_attn_q_w, dtype=np.float32),
        np.asarray(L0_attn_k_w, dtype=np.float32),
        np.asarray(L0_attn_v_w, dtype=np.float32),
        np.asarray(L0_attn_o_w, dtype=np.float32),
        np.asarray(L0_ffn_W_g, dtype=np.float32),
        np.asarray(L0_ffn_W_u, dtype=np.float32),
        np.asarray(L0_ffn_W_d, dtype=np.float32))
    nc = build_nc()
    res = run_bass_kernel_spmd(nc, in_maps, core_ids=list(range(NCORES)))
    out = np.empty((NCORES * SEQS, T, D), dtype=np.float32)
    for core in range(NCORES):
        o = res.results[core]["out_fm"]
        out[core * SEQS:(core + 1) * SEQS] = o.transpose(0, 2, 1)
    return out


# version 03
# speedup vs baseline: 1.1851x; 1.0020x over previous
from contextlib import ExitStack

import numpy as np
import concourse.bass as bass
import concourse.mybir as mybir
from concourse.bass_utils import run_bass_kernel_spmd

F32 = mybir.dt.float32
F32R = mybir.dt.float32r
BF16 = mybir.dt.bfloat16
AF = mybir.ActivationFunctionType
ALU = mybir.AluOpType
AXX = mybir.AxisListType.X

NCORES = 8
SEQS = 2
T = 2048
D = 2048
HID = 8192
HEADS = 16
DC = D // 128
HC = HID // 128
KC = T // 128
TQ = 512
NT = T // TQ
SCALE = 1.0 / float(np.sqrt(128))
STAB = 12.0


class Evt:
    __slots__ = ("sem", "val", "eng")

    def __init__(self, sem, val, eng):
        self.sem, self.val, self.eng = sem, val, eng


class Plan:

    ENGS = ("sync", "gpsimd", "scalar", "vector", "tensor")
    DMАS = ()
    SEMS = ENGS + ("cst", "p0a", "p0b", "p0c", "p0d", "p0e", "p0f",
                   "p0sa", "p0sb", "p0sc", "p0sd", "xin",
                   "wA", "wB", "wC", "wD", "wE", "wF",
                   "qs", "qlA", "qlB", "qlC", "x1s", "outs", "mxs")

    def __init__(self, nc):
        self.nc = nc
        self.streams = {e: [] for e in self.ENGS}
        self.sems = {s: nc.alloc_semaphore("sem_" + s) for s in self.SEMS}
        self.counts = {s: 0 for s in self.SEMS}
        self.marks = {e: {} for e in self.ENGS}

    def _wait(self, eng, evt, force=False):
        if evt is None:
            return
        if evt.eng == eng and not force:
            return
        m = self.marks[eng]
        if m.get(evt.sem, 0) >= evt.val:
            return
        m[evt.sem] = evt.val
        sem_h, val = self.sems[evt.sem], evt.val
        self.streams[eng].append(lambda E, s=sem_h, v=val: E.wait_ge(s, v))

    def emit(self, eng, fn, waits=(), inc=0, sem=None):
        is_dma = sem is not None and sem != eng
        for w in waits:
            self._wait(eng, w, force=is_dma)
        if inc:
            key = sem or eng
            self.counts[key] += inc
            sem_h, val = self.sems[key], self.counts[key]

            def run(E, fn=fn, s=sem_h, k=inc):
                fn(E).then_inc(s, k)
            self.streams[eng].append(run)
            return Evt(key, val, key if key != eng else eng)
        self.streams[eng].append(fn)
        return None

    def fence(self):
        return [Evt(s, self.counts[s], s if s.endswith("_dma") else s)
                for s in self.SEMS if self.counts[s] > 0]

    def wait_fence(self, fevts, exclude=("tensor",)):
        for e in self.ENGS:
            if e in exclude:
                continue
            for ev in fevts:
                self._wait(e, ev)

    def flush(self, final=False):
        if not final:
            return
        used = [e for e in self.ENGS if self.streams[e]]
        if not used:
            return
        with self.nc.Block() as block:
            for e in used:
                fns = self.streams[e]

                def body(E, fns=fns):
                    for f in fns:
                        f(E)
                getattr(block, e)(body)
        self.streams = {e: [] for e in self.ENGS}


def dma(P, qeng, ch, out_ap, in_ap, waits=()):
    return P.emit(qeng, lambda E: E.dma_start(out=out_ap, in_=in_ap),
                  waits=waits, inc=16, sem=ch)


def mm(P, out_ap, lhsT, rhs, start, stop, waits=(), inc=0):
    return P.emit("tensor",
                  lambda E: E.matmul(out_ap, lhsT, rhs, start=start, stop=stop),
                  waits=waits, inc=inc)


def tr(P, out_ap, in_ap, ident, waits=()):
    return P.emit("tensor", lambda E: E.transpose(out_ap, in_ap, ident),
                  waits=waits, inc=1)


def act(P, out_ap, in_ap, func, bias=0.0, scale=1.0, waits=()):
    return P.emit("scalar",
                  lambda E: E.activation(out_ap, in_ap, func, bias=bias,
                                         scale=scale),
                  waits=waits, inc=1)


def vop(P, fn, waits=()):
    return P.emit("vector", fn, waits=waits, inc=1)


def build_nc(stop_after="ffn", dbg=False, paranoid=False):
    nc = bass.Bass("TRN2", target_bir_lowering=False,
                   dynamic_dma_scratch_size=8192,
                   detect_race_conditions=False)

    x_in = nc.dram_tensor("x_fm", [SEQS, D, T], F32, kind="ExternalInput").ap()
    qT_in = nc.dram_tensor("qT", [DC, 128, D], F32, kind="ExternalInput").ap()
    kT_in = nc.dram_tensor("kT", [1, 128, D], F32, kind="ExternalInput").ap()
    vT_in = nc.dram_tensor("vT", [1, 128, D], F32, kind="ExternalInput").ap()
    oT_in = nc.dram_tensor("oT", [DC, 128, D], F32, kind="ExternalInput").ap()
    gT_in = nc.dram_tensor("gT", [HC, 128, D], F32, kind="ExternalInput").ap()
    uT_in = nc.dram_tensor("uT", [HC, 128, D], F32, kind="ExternalInput").ap()
    dT_in = nc.dram_tensor("dT", [DC, 128, HID], F32, kind="ExternalInput").ap()
    out_hbm = nc.dram_tensor("out_fm", [SEQS, D, T], F32,
                             kind="ExternalOutput").ap()

    _dk = "ExternalOutput" if dbg else "Internal"
    dbg_rows = nc.dram_tensor("dbg_rows", [12, TQ], F32, kind=_dk).ap() if dbg \
        else None
    qTr_hi = nc.dram_tensor("qTr_hi", [DC, 128, D], F32R, kind=_dk).ap()
    qTr_lo = nc.dram_tensor("qTr_lo", [DC, 128, D], F32R, kind=_dk).ap()
    oTr_hi = nc.dram_tensor("oTr_hi", [DC, 128, D], F32R).ap()
    oTr_lo = nc.dram_tensor("oTr_lo", [DC, 128, D], F32R).ap()
    kTr_hi = nc.dram_tensor("kTr_hi", [1, 128, D], F32R).ap()
    kTr_lo = nc.dram_tensor("kTr_lo", [1, 128, D], F32R).ap()
    vTr_hi = nc.dram_tensor("vTr_hi", [1, 128, D], F32R).ap()
    vTr_lo = nc.dram_tensor("vTr_lo", [1, 128, D], F32R).ap()
    gTs = nc.dram_tensor("gTs", [HC, 128, D], BF16).ap()
    uTs = nc.dram_tensor("uTs", [HC, 128, D], BF16).ap()
    dTs = nc.dram_tensor("dTs", [DC, 128, HID], BF16).ap()
    dbg_kind = "ExternalOutput" if dbg else "Internal"
    q_s = nc.dram_tensor("q_s", [SEQS, D, T], F32, kind=dbg_kind).ap()
    x1_s = nc.dram_tensor("x1_s", [SEQS, D, T], F32, kind=dbg_kind).ap()

    ones128_h = nc.inline_tensor(np.ones((128, 1), np.float32), "ones128")
    onescol_h = nc.inline_tensor(np.ones((1, 128), np.float32), "onescol")
    ident_h = nc.inline_tensor(np.eye(128, dtype=np.float32), "ident128")
    stab_h = nc.inline_tensor(np.full((128, 1), -STAB, np.float32), "stab")
    scl_h = nc.inline_tensor(np.full((128, 1), SCALE, np.float32), "scl")

    P = Plan(nc)

    cst = nc.alloc_sbuf_tensor("cst_ones128", [128, 1], F32)
    cst_col = nc.alloc_sbuf_tensor("cst_onescol", [1, 128], F32)
    cst_id = nc.alloc_sbuf_tensor("cst_ident", [128, 128], F32)
    cst_stab = nc.alloc_sbuf_tensor("cst_stab", [128, 1], F32)
    cst_scl = nc.alloc_sbuf_tensor("cst_scl", [128, 1], F32)
    cst_col_r = nc.alloc_sbuf_tensor("cst_colr", [1, 128], F32R)
    cst_ones_r = nc.alloc_sbuf_tensor("cst_onesr", [128, 1], F32R)
    e_ones = dma(P, "sync", "cst", cst[:], ones128_h.ap())
    e_col = dma(P, "sync", "cst", cst_col[:], onescol_h.ap())
    e_id = dma(P, "sync", "cst", cst_id[:], ident_h.ap())
    e_stab = dma(P, "sync", "cst", cst_stab[:], stab_h.ap())
    e_scl = dma(P, "sync", "cst", cst_scl[:], scl_h.ap())
    e_colr = act(P, cst_col_r[:], cst_col[:], AF.Copy, waits=(e_col,))
    e_onesr = act(P, cst_ones_r[:], cst[:], AF.Copy, waits=(e_ones,))

    def bcast_r(ps_ap, row_ap, hi_ap, lo_ap, waits=()):
        eh = act(P, hi_ap, row_ap, AF.Copy, waits=waits)
        el = vop(P, lambda E, o=lo_ap, a=row_ap, b=hi_ap: E.tensor_sub(o, a, b),
                 waits=(eh,))
        mm(P, ps_ap, cst_col_r[:], hi_ap, start=True, stop=False,
           waits=(eh, e_colr))
        return mm(P, ps_ap, cst_col_r[:], lo_ap, start=False, stop=True,
                  waits=(el,), inc=1)

    def ln_rows_emit(P, x_tiles, x_evts, lb, eps):
        sq = (lb["sq0"], lb["sq1"])
        xh = (lb["xh0"], lb["xh1"])
        xl = (lb["xl0"], lb["xl1"])
        prev_rows = lb.get("prev_rows") or ()
        mmev = lb["_smm"]
        e_m = None
        for c in range(DC):
            w = [x_evts[c]]
            if ("m", c - 2) in mmev:
                w.append(mmev[("m", c - 2)])
            elif c < 2:
                w.extend(prev_rows)
            eh = act(P, xh[c % 2][:], x_tiles[c], AF.Copy, waits=tuple(w))
            el = vop(P, lambda E, o=xl[c % 2][:], a=x_tiles[c], b=xh[c % 2][:]:
                     E.tensor_sub(o, a, b), waits=(eh,))
            mm(P, lb["mean_ps"][0:1, :], cst_ones_r[:], xh[c % 2][:],
               start=(c == 0), stop=False,
               waits=(eh, e_onesr) + (tuple(prev_rows) if c == 0 else ()))
            e_m = mm(P, lb["mean_ps"][0:1, :], cst_ones_r[:], xl[c % 2][:],
                     start=False, stop=(c == DC - 1), waits=(el,), inc=1)
            mmev[("m", c)] = e_m
        e_s = None
        for c in range(DC):
            w = [x_evts[c]]
            if ("s", c - 2) in mmev:
                w.append(mmev[("s", c - 2)])
            elif c < 2:
                w.append(mmev[("m", DC - 2 + c)])
            e_sq = act(P, sq[c % 2][:], x_tiles[c], AF.Square, waits=tuple(w))
            eh = act(P, xh[c % 2][:], sq[c % 2][:], AF.Copy, waits=(e_sq,))
            el = vop(P, lambda E, o=xl[c % 2][:], a=sq[c % 2][:],
                     b=xh[c % 2][:]: E.tensor_sub(o, a, b), waits=(eh,))
            mm(P, lb["sq_ps"][0:1, :], cst_ones_r[:], xh[c % 2][:],
               start=(c == 0), stop=False,
               waits=(eh,) + (tuple(prev_rows) if c == 0 else ()))
            e_s = mm(P, lb["sq_ps"][0:1, :], cst_ones_r[:], xl[c % 2][:],
                     start=False, stop=(c == DC - 1), waits=(el,), inc=1)
            mmev[("s", c)] = e_s
        m_row, v_row, mm_row = lb["m_row"], lb["v_row"], lb["mm_row"]
        em = vop(P, lambda E: E.tensor_scalar_mul(m_row[:], lb["mean_ps"][0:1, :],
                                                  1.0 / D), waits=(e_m,))
        es = vop(P, lambda E: E.tensor_scalar_mul(v_row[:], lb["sq_ps"][0:1, :],
                                                  1.0 / D), waits=(e_s,))
        e1 = vop(P, lambda E: E.tensor_mul(mm_row[:], m_row[:], m_row[:]),
                 waits=(em,))
        e2 = vop(P, lambda E: E.tensor_sub(v_row[:], v_row[:], mm_row[:]),
                 waits=(es, e1))
        e2b = vop(P, lambda E: E.tensor_scalar_add(v_row[:], v_row[:],
                                                   float(eps)), waits=(e2,))
        e2c = vop(P, lambda E: E.reciprocal(v_row[:], v_row[:]), waits=(e2b,))
        er = act(P, v_row[:], v_row[:], AF.Sqrt, waits=(e2c,))
        emb = bcast_r(lb["mb_ps"][:], m_row[:], lb["mh"][:], lb["ml"][:],
                      waits=(em,))
        erb = bcast_r(lb["rb_ps"][:], v_row[:], lb["vh"][:], lb["vl"][:],
                      waits=(er,))
        lb["prev_rows"] = (em, es)
        return emb, erb

    def mk_ln_bufs(stk, tag):
        return {
            "sq0": stk.enter_context(nc.sbuf_tensor(tag + "_sq0", [128, TQ], F32)),
            "sq1": stk.enter_context(nc.sbuf_tensor(tag + "_sq1", [128, TQ], F32)),
            "m_row": stk.enter_context(nc.sbuf_tensor(tag + "_m", [1, TQ], F32)),
            "v_row": stk.enter_context(nc.sbuf_tensor(tag + "_v", [1, TQ], F32)),
            "mm_row": stk.enter_context(nc.sbuf_tensor(tag + "_mm", [1, TQ], F32)),
            "xh0": stk.enter_context(nc.sbuf_tensor(tag + "_xh0", [128, TQ], F32R)),
            "xh1": stk.enter_context(nc.sbuf_tensor(tag + "_xh1", [128, TQ], F32R)),
            "xl0": stk.enter_context(nc.sbuf_tensor(tag + "_xl0", [128, TQ], F32R)),
            "xl1": stk.enter_context(nc.sbuf_tensor(tag + "_xl1", [128, TQ], F32R)),
            "mh": stk.enter_context(nc.sbuf_tensor(tag + "_mh", [1, TQ], F32R)),
            "ml": stk.enter_context(nc.sbuf_tensor(tag + "_ml", [1, TQ], F32R)),
            "vh": stk.enter_context(nc.sbuf_tensor(tag + "_vh", [1, TQ], F32R)),
            "vl": stk.enter_context(nc.sbuf_tensor(tag + "_vl", [1, TQ], F32R)),
            "mean_ps": stk.enter_context(nc.psum_tensor(tag + "_mps", [128, TQ], F32)),
            "sq_ps": stk.enter_context(nc.psum_tensor(tag + "_sps", [128, TQ], F32)),
            "mb_ps": stk.enter_context(nc.psum_tensor(tag + "_mb", [128, TQ], F32)),
            "rb_ps": stk.enter_context(nc.psum_tensor(tag + "_rb", [128, TQ], F32)),
            "_smm": {},
        }

    with (
        nc.sbuf_tensor("p0_in", [128, 4 * D], F32) as p0_in,
        nc.sbuf_tensor("p0_hi", [128, 4 * D], F32R) as p0_hi,
        nc.sbuf_tensor("p0_lo", [128, 4 * D], F32R) as p0_lo,
    ):
        attn_jobs = []
        for oc in range(DC):
            attn_jobs.append((qT_in[oc], qTr_hi[oc], qTr_lo[oc], D))
            attn_jobs.append((oT_in[oc], oTr_hi[oc], oTr_lo[oc], D))
        attn_jobs.append((kT_in[0], kTr_hi[0], kTr_lo[0], D))
        attn_jobs.append((vT_in[0], vTr_hi[0], vTr_lo[0], D))
        if stop_after == "p0s":
            attn_jobs = attn_jobs[:4]
        elif stop_after.startswith(("p0f", "p0d")):
            attn_jobs = []
        LOADQ = [("scalar", "p0a"), ("scalar", "p0b"),
                 ("sync", "p0c"), ("sync", "p0d")]
        STORQ = [("sync", "p0sa"), ("sync", "p0sb"),
                 ("scalar", "p0sc"), ("scalar", "p0sd")]
        slot_ev = [None] * 4
        for j, (src, dst_hi, dst_lo, w) in enumerate(attn_jobs):
            ln = j % 4
            off = ln * D
            iv, hv, lv = (p0_in[:, off:off + w], p0_hi[:, off:off + w],
                          p0_lo[:, off:off + w])
            lq, lch = LOADQ[ln]
            sq, sch = STORQ[ln]
            eld = dma(P, lq, lch, iv, src,
                      waits=(slot_ev[ln],) if slot_ev[ln] else ())
            ehi = act(P, hv, iv, AF.Copy, waits=(eld,))
            elo = vop(P, lambda E, o=lv, a=iv, b=hv: E.tensor_sub(o, a, b),
                      waits=(ehi,))
            dma(P, sq, sch, dst_hi, hv, waits=(ehi,))
            e2 = dma(P, sq, sch, dst_lo, lv, waits=(elo,))
            slot_ev[ln] = e2
        p0_done = tuple(Evt(ch, P.counts[ch], ch)
                        for ch in ("p0sa", "p0sb", "p0sc", "p0sd"))
        P.flush()

    ffn_jobs = []
    for hc in range(HC):
        ffn_jobs.append((gT_in[hc], gTs[hc], D))
        ffn_jobs.append((uT_in[hc], uTs[hc], D))
    for oc in range(DC):
        ffn_jobs.append((dT_in[oc], dTs[oc], HID))
    if stop_after == "p0s":
        ffn_jobs = ffn_jobs[:4]
    elif stop_after == "p0a":
        ffn_jobs = []
    elif stop_after == "p0d":
        ffn_jobs = ffn_jobs[-DC:]
    elif stop_after == "p0f":
        ffn_jobs = ffn_jobs[:-DC]
    elif stop_after.startswith("p0f"):
        ffn_jobs = ffn_jobs[:int(stop_after[3:])]
    CW, CSLOTS, CK = 512, 5, 4
    conv_jobs = []
    for (src, dst, w) in ffn_jobs:
        for o in range(0, w, CW):
            conv_jobs.append((src[:, o:o + CW], dst[:, o:o + CW]))
    cvb = {}
    cvs = {"jl": 0, "js": 0, "ld_ev": [None] * CSLOTS,
           "sg_ev": [None] * CSLOTS, "st_ev": [None, None],
           "N": len(conv_jobs)}
    CLOADQ = [("scalar", "p0e"), ("sync", "p0f")]
    CSTORQ = [("sync", "p0sc"), ("scalar", "p0sd")]

    def conv_tick():
        if cvs["jl"] < cvs["N"]:
            j = cvs["jl"]
            sl = j % CSLOTS
            lq, lch = CLOADQ[j % 2]
            iv = cvb["in"][:, sl * CW:(sl + 1) * CW]
            sg0 = cvs["sg_ev"][sl]
            w = sg0 if isinstance(sg0, tuple) else (sg0,) if sg0 else ()
            cvs["ld_ev"][sl] = dma(P, lq, lch, iv, conv_jobs[j][0], waits=w)
            cvs["jl"] += 1
        if cvs["js"] < cvs["N"] and \
                (cvs["js"] <= cvs["jl"] - CK or cvs["jl"] >= cvs["N"]):
            j = cvs["js"]
            sl = j % CSLOTS
            ln = j % 2
            iv = cvb["in"][:, sl * CW:(sl + 1) * CW]
            sgv = cvb["sg"][:, 0:CW]
            t1v = cvb["t1"][:, 0:CW]
            t2v = cvb["t2"][:, 0:CW]
            w0 = (cvs["ld_ev"][sl],) + ((cvs["st_ev"][0],)
                                        if cvs["st_ev"][0] else ())
            e3 = act(P, sgv, iv, AF.Sign, waits=w0)
            cvs["sg_ev"][sl] = e3
            sq, sch = CSTORQ[ln]
            cvs["st_ev"][0] = dma(P, sq, sch, conv_jobs[j][1], sgv,
                                  waits=(e3,))
            cvs["js"] += 1

    conv_deferred = (stop_after == "ffn")

    with (
        nc.sbuf_tensor("K_hi", [128, T], F32R) as K_hi,
        nc.sbuf_tensor("K_lo", [128, T], F32R) as K_lo,
        nc.sbuf_tensor("V_hi", [128, T], F32R) as V_hi,
        nc.sbuf_tensor("V_lo", [128, T], F32R) as V_lo,
        nc.sbuf_tensor("cv_in", [128, CSLOTS * CW], F32) as cv_in,
        nc.sbuf_tensor("cv_sg", [128, CW], BF16) as cv_sg,
        nc.sbuf_tensor("cv_t1", [128, CW], BF16) as cv_t1,
        nc.sbuf_tensor("cv_t2", [128, CW], BF16) as cv_t2,
    ):
        cvb["in"], cvb["sg"], cvb["t1"], cvb["t2"] = cv_in, cv_sg, cv_t1, cv_t2
        if not conv_deferred:
            cvs["sg_ev"] = [p0_done] * CSLOTS
            while cvs["js"] < cvs["N"]:
                conv_tick()
        a1_q_done = {}
        x1_store_last = None

        for seq in ((1,) if stop_after == "a1x" else range(SEQS if not stop_after.startswith("p0") else 0)):
            with ExitStack() as stk:
                xa = [stk.enter_context(
                    nc.sbuf_tensor(f"a1x{seq}_{c}", [128, TQ], F32))
                    for c in range(DC)]
                lnh = stk.enter_context(
                    nc.sbuf_tensor(f"a1lnh{seq}", [128, DC * TQ], F32R))
                lnl = stk.enter_context(
                    nc.sbuf_tensor(f"a1lnl{seq}", [128, DC * TQ], F32R))
                lnt = [stk.enter_context(
                    nc.sbuf_tensor(f"a1lnt{seq}_{i}", [128, TQ], F32))
                    for i in range(2)]
                lb = mk_ln_bufs(stk, f"a1{seq}")
                wq = [stk.enter_context(
                    nc.sbuf_tensor(f"a1wq{seq}_{i}", [128, D], F32R))
                    for i in range(4)]
                ev_out = stk.enter_context(
                    nc.sbuf_tensor(f"a1ev{seq}", [128, TQ], F32))
                V_fm = stk.enter_context(
                    nc.sbuf_tensor(f"a1vfm{seq}", [128, T], F32))
                proj_ps = [stk.enter_context(
                    nc.psum_tensor(f"a1p{seq}_{i}", [128, TQ], F32))
                    for i in range(2)]
                tr_ps = stk.enter_context(
                    nc.psum_tensor(f"a1tr{seq}", [128, TQ], F32))

                wslot_ev = [None, None]
                psum_ev = [None, None]
                ev_store_ev = None
                kv_done = []
                for it in range(NT):
                    P.wait_fence(P.fence())
                    lb["_smm"] = {}
                    t0 = it * TQ
                    e_x = [dma(P, "scalar", "xin", xa[c][:],
                               x_in[seq, c * 128:(c + 1) * 128, t0:t0 + TQ])
                           for c in range(DC)]
                    e_x = [e_x[-1]] * DC
                    emb, erb = ln_rows_emit(
                        P, [xa[c][:] for c in range(DC)], e_x, lb, 1e-5)
                    if paranoid:
                        P.wait_fence(P.fence())
                    e_lh = [None] * DC
                    e_ll = [None] * DC
                    for c in range(DC):
                        hv = lnh[:, c * TQ:(c + 1) * TQ]
                        lv = lnl[:, c * TQ:(c + 1) * TQ]
                        w1 = [e_x[c], emb]
                        if c >= 2:
                            w1.append(e_lh[c - 2])
                        e1 = vop(P, lambda E, o=lnt[c % 2][:], a=xa[c][:],
                                 b=lb["mb_ps"][:]: E.tensor_sub(o, a, b),
                                 waits=tuple(w1))
                        e2 = vop(P, lambda E, o=lnt[c % 2][:], a=lnt[c % 2][:],
                                 b=lb["rb_ps"][:]: E.tensor_mul(o, a, b),
                                 waits=(erb,))
                        e_lh[c] = act(P, hv, lnt[c % 2][:], AF.Copy,
                                      waits=(e2,))
                        e_ll[c] = vop(P, lambda E, o=lv, a=lnt[c % 2][:], b=hv:
                                      E.tensor_sub(o, a, b), waits=(e_lh[c],))
                    if paranoid:
                        P.wait_fence(P.fence())
                    if dbg and it == 0:
                        base = seq * 6
                        dma(P, "scalar", "mxs", dbg_rows[base + 0:base + 1, :],
                            lb["m_row"][:], waits=(emb,))
                        dma(P, "scalar", "mxs", dbg_rows[base + 1:base + 2, :],
                            lb["v_row"][:], waits=(erb,))
                        dma(P, "scalar", "mxs", dbg_rows[base + 2:base + 3, :].bitcast(F32R),
                            lnh[0:1, 0:TQ], waits=(e_lh[0],))
                        dma(P, "scalar", "mxs", dbg_rows[base + 3:base + 4, :].bitcast(F32R),
                            lnl[0:1, 0:TQ], waits=(e_ll[0],))
                        e_d1 = vop(P, lambda E, o=lnt[0][:], i=lb["mb_ps"][:]:
                                   E.tensor_copy(o, i), waits=(emb,))
                        dma(P, "scalar", "mxs", dbg_rows[base + 4:base + 5, :],
                            lnt[0][0:1, :], waits=(e_d1,))
                        e_d2 = vop(P, lambda E, o=lnt[0][:], i=lb["rb_ps"][:]:
                                   E.tensor_copy(o, i), waits=(erb,))
                        dma(P, "scalar", "mxs", dbg_rows[base + 5:base + 6, :],
                            lnt[0][0:1, :], waits=(e_d2,))
                    jobs = [("q", oc) for oc in range(DC)] + [("k", 0), ("v", 0)]
                    for jn, (kind, oc) in enumerate(jobs):
                        if paranoid:
                            P.wait_fence(P.fence())
                        wsl = jn % 2
                        psl = jn % 2
                        if kind == "q":
                            src_h, src_l = qTr_hi[oc], qTr_lo[oc]
                        elif kind == "k":
                            src_h, src_l = kTr_hi[0], kTr_lo[0]
                        else:
                            src_h, src_l = vTr_hi[0], vTr_lo[0]
                        wh, wl = wq[2 * wsl], wq[2 * wsl + 1]
                        wch = "wA" if wsl == 0 else "wB"
                        ww = ((wslot_ev[wsl],) + p0_done) if wslot_ev[wsl] \
                            else p0_done
                        e_wh = dma(P, "sync", wch, wh[:], src_h, waits=ww)
                        e_wl = dma(P, "sync", wch, wl[:], src_l, waits=ww)
                        ps = proj_ps[psl]
                        pw = (psum_ev[psl],) if psum_ev[psl] else ()
                        e_mm = None
                        for c in range(DC):
                            whc = wh[:, c * 128:(c + 1) * 128]
                            wlc = wl[:, c * 128:(c + 1) * 128]
                            rh = lnh[:, c * TQ:(c + 1) * TQ]
                            rl = lnl[:, c * TQ:(c + 1) * TQ]
                            first, last = (c == 0), (c == DC - 1)
                            mm(P, ps[:], whc, rh, start=first, stop=False,
                               waits=(e_wh, e_lh[c]) + (pw if first else ()))
                            mm(P, ps[:], whc, rl, start=False, stop=False,
                               waits=(e_ll[c],))
                            e_mm = mm(P, ps[:], wlc, rh, start=False, stop=last,
                                      waits=(e_wl,), inc=1 if last else 0)
                        wslot_ev[wsl] = e_mm
                        if kind == "q":
                            ee = vop(P, lambda E, o=ev_out[:], i=ps[:]:
                                     E.tensor_copy(o, i),
                                     waits=(e_mm,) + ((ev_store_ev,)
                                                      if ev_store_ev else ()))
                            ev_store_ev = dma(
                                P, "scalar", "qs",
                                q_s[seq, oc * 128:(oc + 1) * 128, t0:t0 + TQ],
                                ev_out[:], waits=(ee,))
                            psum_ev[psl] = ee
                        elif kind == "k":
                            eh = act(P, K_hi[:, t0:t0 + TQ], ps[:], AF.Copy,
                                     waits=(e_mm,))
                            ee = vop(P, lambda E, o=K_lo[:, t0:t0 + TQ],
                                     a=ps[:], b=K_hi[:, t0:t0 + TQ]:
                                     E.tensor_sub(o, a, b), waits=(eh,))
                            psum_ev[psl] = ee
                            kv_done.append(ee)
                        else:
                            ee = vop(P, lambda E, o=V_fm[:, t0:t0 + TQ], i=ps[:]:
                                     E.tensor_copy(o, i), waits=(e_mm,))
                            psum_ev[psl] = ee
                            kv_done.append(ee)
                vt_done = []
                e_prev = None
                for kc in range(KC):
                    e_t = tr(P, tr_ps[:, 0:128], V_fm[:, kc * 128:(kc + 1) * 128],
                             cst_id[:],
                             waits=(kv_done[-1], e_id) +
                                   ((e_prev,) if e_prev else ()))
                    eh = act(P, V_hi[:, kc * 128:(kc + 1) * 128],
                             tr_ps[:, 0:128], AF.Copy, waits=(e_t,))
                    e_c = vop(P, lambda E, o=V_lo[:, kc * 128:(kc + 1) * 128],
                              a=tr_ps[:, 0:128],
                              b=V_hi[:, kc * 128:(kc + 1) * 128]:
                              E.tensor_sub(o, a, b), waits=(eh,))
                    e_prev = e_c
                    vt_done.append(e_c)
                a1_q_done[seq] = Evt("qs", P.counts["qs"], "qs")
                P.flush()

            deps_kv = tuple(kv_done[-1:]) + tuple(vt_done[-1:]) + \
                tuple(vt_done[:1])
            a2c = {"q": [None, None, None], "qld": [None, None, None],
                   "qsp": [None, None, None],
                   "exp": [None, None, None, None],
                   "sc": [None, None, None],
                   "ao": [None, None], "att": None, "es": None, "eib": None,
                   "eic": None, "wo": [None, None], "xa": [None, None],
                   "rows": None}
            for qt in range(NT if not (stop_after.startswith("p0") or stop_after in ("a1", "a1x")) else 0):
                t0 = qt * TQ
                with ExitStack() as stk:
                    P.wait_fence(P.fence())
                    tag = f"a2{seq}_{qt}"
                    atth = stk.enter_context(
                        nc.sbuf_tensor(f"{tag}ah", [128, DC * TQ], F32R))
                    attl = stk.enter_context(
                        nc.sbuf_tensor(f"{tag}al", [128, DC * TQ], F32R))
                    wo = [stk.enter_context(
                        nc.sbuf_tensor(f"{tag}wo{i}", [128, D // 2], F32R))
                        for i in range(4)]
                    osb = stk.enter_context(
                        nc.sbuf_tensor(f"{tag}osb", [128, DC * TQ], F32))
                    ao_ps = [stk.enter_context(
                        nc.psum_tensor(f"{tag}ao{i}", [128, TQ], F32))
                        for i in range(2)]

                    att_split_ev = [None] * DC
                    ao_slot_ev = a2c["ao"]
                    with ExitStack() as s1:
                        qh = [s1.enter_context(
                            nc.sbuf_tensor(f"{tag}q{i}", [128, TQ], F32))
                            for i in range(3)]
                        qhh = [s1.enter_context(
                            nc.sbuf_tensor(f"{tag}qh{i}", [128, TQ], F32R))
                            for i in range(3)]
                        qhl = [s1.enter_context(
                            nc.sbuf_tensor(f"{tag}ql{i}", [128, TQ], F32R))
                            for i in range(3)]
                        expb = [s1.enter_context(
                            nc.sbuf_tensor(f"{tag}e{i}", [128, TQ], F32))
                            for i in range(4)]
                        exph = [s1.enter_context(
                            nc.sbuf_tensor(f"{tag}eh{i}", [128, TQ], F32R))
                            for i in range(4)]
                        expl = [s1.enter_context(
                            nc.sbuf_tensor(f"{tag}el{i}", [128, TQ], F32R))
                            for i in range(4)]
                        sacc = s1.enter_context(
                            nc.sbuf_tensor(f"{tag}sacc", [128, TQ], F32))
                        srow = s1.enter_context(
                            nc.sbuf_tensor(f"{tag}srow", [1, TQ], F32))
                        srh = s1.enter_context(
                            nc.sbuf_tensor(f"{tag}srh", [1, TQ], F32R))
                        srl = s1.enter_context(
                            nc.sbuf_tensor(f"{tag}srl", [1, TQ], F32R))
                        isb = s1.enter_context(
                            nc.sbuf_tensor(f"{tag}isb", [128, TQ], F32))
                        att_t = s1.enter_context(
                            nc.sbuf_tensor(f"{tag}att", [128, TQ], F32))
                        sc_ps = [s1.enter_context(
                            nc.psum_tensor(f"{tag}sc{i}", [128, TQ], F32))
                            for i in range(3)]
                        s_ps = s1.enter_context(
                            nc.psum_tensor(f"{tag}sps", [128, TQ], F32))
                        is_ps = s1.enter_context(
                            nc.psum_tensor(f"{tag}isps", [128, TQ], F32))
                        q_slot_ev = a2c["q"]
                        exp_slot_ev = a2c["exp"]
                        sc_slot_ev = a2c["sc"]
                        att_tmp_ev = a2c["att"]
                        e_s_prev = a2c["es"]
                        e_ib_prev = a2c["eib"]
                        e_ic_prev = a2c["eic"]
                        qld_ev = a2c["qld"]
                        qsp_ev = a2c["qsp"]

                        def q_load(hh):
                            sl = hh % 3
                            sw = (qsp_ev[sl][1],) if qsp_ev[sl] else ()
                            qld_ev[sl] = dma(
                                P, "sync", ("qlA", "qlB", "qlC")[sl],
                                qh[sl][:],
                                q_s[seq, hh * 128:(hh + 1) * 128,
                                    t0:t0 + TQ],
                                waits=sw + (a1_q_done[seq],))

                        def q_split(hh):
                            sl = hh % 3
                            qw2 = (q_slot_ev[sl],) if q_slot_ev[sl] else ()
                            e_qh2 = act(P, qhh[sl][:], qh[sl][:], AF.Copy,
                                        waits=(qld_ev[sl],) + qw2)
                            e_ql2 = vop(P, lambda E, o=qhl[sl][:],
                                        a=qh[sl][:], b=qhh[sl][:]:
                                        E.tensor_sub(o, a, b),
                                        waits=(e_qh2,))
                            qsp_ev[sl] = (e_qh2, e_ql2)

                        q_load(0)
                        q_load(1)
                        q_split(0)
                        st = {"es": e_s_prev, "eib": e_ib_prev,
                              "eic": e_ic_prev, "att": att_tmp_ev,
                              "er": None}
                        SKEW = 2
                        pend = None

                        def schain_a(pd):
                            ph = pd["h"]
                            psl = ph % 3
                            ssp = (st["es"],) if st["es"] else ()
                            rsp = (st["er"],) if st["er"] else ()
                            e_kh = act(P, qhh[psl][:], sacc[:], AF.Copy,
                                       waits=(pd["e_sacc"], pd["e_ao"]) + ssp)
                            e_kl = vop(P, lambda E, o=qhl[psl][:], a=sacc[:],
                                       b=qhh[psl][:]: E.tensor_sub(o, a, b),
                                       waits=(e_kh,))
                            mm(P, s_ps[0:1, :], cst_ones_r[:], qhh[psl][:],
                               start=True, stop=False,
                               waits=(e_kh, e_onesr) + ssp + rsp)
                            e_s = mm(P, s_ps[0:1, :], cst_ones_r[:],
                                     qhl[psl][:], start=False, stop=True,
                                     waits=(e_kl,), inc=1)
                            st["es"] = e_s
                            q_slot_ev[psl] = e_s
                            pd["e_s"] = e_s

                        def schain_b(pd):
                            pd["e_r"] = vop(P, lambda E, o=srow[:],
                                            i=s_ps[0:1, :]:
                                            E.reciprocal(o, i),
                                            waits=(pd["e_s"],) +
                                                  ((st["eib"],)
                                                   if st["eib"] else ()))
                            st["er"] = pd["e_r"]

                        def schain_c(pd):
                            ph = pd["h"]
                            e_ib = bcast_r(is_ps[:], srow[:], srh[:], srl[:],
                                           waits=(pd["e_r"],) +
                                                 ((st["eic"],)
                                                  if st["eic"] else ()))
                            st["eib"] = e_ib
                            e_ic = vop(P, lambda E, o=isb[:], i=is_ps[:]:
                                       E.tensor_copy(o, i), waits=(e_ib,))
                            st["eic"] = e_ic
                            e_at = vop(P, lambda E, o=att_t[:],
                                       a=ao_ps[ph % 2][:], b=isb[:]:
                                       E.tensor_mul(o, a, b),
                                       waits=(pd["e_ao"], e_ic) +
                                             ((st["att"],) if st["att"]
                                              else ()))
                            ao_slot_ev[ph % 2] = e_at
                            hv = atth[:, ph * TQ:(ph + 1) * TQ]
                            lv = attl[:, ph * TQ:(ph + 1) * TQ]
                            hw_ = (e_at,)
                            if ph < 2:
                                hw_ = hw_ + tuple(e for e in a2c["wo"] if e)
                            e_h = act(P, hv, att_t[:], AF.Copy, waits=hw_)
                            e_l = vop(P, lambda E, o=lv, a=att_t[:], b=hv:
                                      E.tensor_sub(o, a, b), waits=(e_h,))
                            st["att"] = e_l
                            att_split_ev[ph] = e_l

                        for h in range(HEADS):
                            qsl = h % 3
                            if h + 2 < HEADS:
                                q_load(h + 2)
                            if conv_deferred:
                                for _ in range(6):
                                    conv_tick()
                            if h + 1 < HEADS:
                                q_split(h + 1)
                            e_qh, e_ql = qsp_ev[qsl]
                            e_sacc = None
                            e_ao_last = None
                            split_ev = [None] * KC
                            for kc in range(KC + SKEW):
                                if kc < KC:
                                    scsl = kc % 3
                                    esl = kc % 4
                                    scw = (sc_slot_ev[scsl],) \
                                        if sc_slot_ev[scsl] else ()
                                    kh = K_hi[:, kc * 128:(kc + 1) * 128]
                                    kl = K_lo[:, kc * 128:(kc + 1) * 128]
                                    mm(P, sc_ps[scsl][:], kh, qhh[qsl][:],
                                       start=True, stop=False,
                                       waits=(e_qh,) + deps_kv + scw)
                                    mm(P, sc_ps[scsl][:], kh, qhl[qsl][:],
                                       start=False, stop=False, waits=(e_ql,))
                                    e_sc = mm(P, sc_ps[scsl][:], kl,
                                              qhh[qsl][:], start=False,
                                              stop=True, inc=1)
                                    if kc == 0 and pend is not None:
                                        schain_a(pend)
                                    ew = exp_slot_ev[esl] or ()
                                    e_e = act(P, expb[esl][:], sc_ps[scsl][:],
                                              AF.Exp, bias=cst_stab[:],
                                              scale=cst_scl[:],
                                              waits=(e_sc, e_stab, e_scl) +
                                                    tuple(ew))
                                    sc_slot_ev[scsl] = e_e
                                    e_eh = act(P, exph[esl][:], expb[esl][:],
                                               AF.Copy, waits=(e_e,))
                                    e_el = vop(P, lambda E, o=expl[esl][:],
                                               a=expb[esl][:], b=exph[esl][:]:
                                               E.tensor_sub(o, a, b),
                                               waits=(e_eh,))
                                    split_ev[kc] = (e_eh, e_el)
                                    if kc == 0:
                                        sw = (e_e,) + ((st["es"],)
                                                       if st["es"] else ())
                                        e_sacc = vop(P, lambda E, o=sacc[:],
                                                     i=expb[esl][:]:
                                                     E.tensor_copy(o, i),
                                                     waits=sw)
                                    else:
                                        e_sacc = vop(P, lambda E, o=sacc[:],
                                                     a=sacc[:], b=expb[esl][:]:
                                                     E.tensor_add(o, a, b),
                                                     waits=(e_e,))
                                    if kc == 2 and pend is not None:
                                        schain_b(pend)
                                    if kc == 6 and pend is not None:
                                        schain_c(pend)
                                        pend = None
                                if kc >= SKEW:
                                    kv = kc - SKEW
                                    vsl = kv % 4
                                    vh = V_hi[:, kv * 128:(kv + 1) * 128]
                                    vl = V_lo[:, kv * 128:(kv + 1) * 128]
                                    e_eh2, e_el2 = split_ev[kv]
                                    aw = [e_eh2]
                                    if kv == 0 and ao_slot_ev[h % 2] is not None:
                                        aw.append(ao_slot_ev[h % 2])
                                    mm(P, ao_ps[h % 2][:], vh, exph[vsl][:],
                                       start=(kv == 0), stop=False,
                                       waits=tuple(aw))
                                    mm(P, ao_ps[h % 2][:], vh, expl[vsl][:],
                                       start=False, stop=False,
                                       waits=(e_el2,))
                                    e_ao_last = mm(P, ao_ps[h % 2][:], vl,
                                                   exph[vsl][:], start=False,
                                                   stop=(kv == KC - 1),
                                                   inc=1 if kv == KC - 1
                                                   else 0)
                                    exp_slot_ev[vsl] = (e_ao_last, e_sacc)
                            pend = {"h": h, "e_sacc": e_sacc,
                                    "e_ao": e_ao_last}
                        schain_a(pend)
                        schain_b(pend)
                        schain_c(pend)
                        pend = None
                        e_s_prev = st["es"]
                        e_ib_prev = st["eib"]
                        e_ic_prev = st["eic"]
                        att_tmp_ev = st["att"]

                    a2c["att"] = att_tmp_ev
                    a2c["es"] = e_s_prev
                    a2c["eib"] = e_ib_prev
                    a2c["eic"] = e_ic_prev
                    with ExitStack() as s2:
                        lb2 = mk_ln_bufs(s2, tag + "ln")
                        lb2["prev_rows"] = a2c["rows"] or ()
                        xa = [s2.enter_context(
                            nc.sbuf_tensor(f"{tag}x{i}", [128, TQ], F32))
                            for i in range(2)]
                        xguard = tuple(P.fence())
                        wslot_ev = a2c["wo"]
                        oproj_ev = [None] * DC
                        for oc in range(DC):
                            ps = ao_ps[oc % 2]
                            pw = (ao_slot_ev[oc % 2],) \
                                if ao_slot_ev[oc % 2] else ()
                            e_mm = None
                            for half in range(2):
                                wh, wl = wo[2 * half], wo[2 * half + 1]
                                wch = "wA" if half == 0 else "wB"
                                ww = ((wslot_ev[half],) if wslot_ev[half]
                                      else ()) + p0_done
                                e_wh = dma(P, "sync", wch, wh[:],
                                           oTr_hi[oc, :, half * (D // 2):
                                                  (half + 1) * (D // 2)],
                                           waits=ww)
                                e_wl = dma(P, "sync", wch, wl[:],
                                           oTr_lo[oc, :, half * (D // 2):
                                                  (half + 1) * (D // 2)],
                                           waits=ww)
                                for k in range(DC // 2):
                                    c = half * (DC // 2) + k
                                    whc = wh[:, k * 128:(k + 1) * 128]
                                    wlc = wl[:, k * 128:(k + 1) * 128]
                                    rh = atth[:, c * TQ:(c + 1) * TQ]
                                    rl = attl[:, c * TQ:(c + 1) * TQ]
                                    first = (c == 0)
                                    last = (c == DC - 1)
                                    mm(P, ps[:], whc, rh, start=first,
                                       stop=False,
                                       waits=(e_wh, att_split_ev[c]) +
                                             (pw if first else ()))
                                    mm(P, ps[:], whc, rl, start=False,
                                       stop=False)
                                    e_mm = mm(P, ps[:], wlc, rh, start=False,
                                              stop=last, waits=(e_wl,),
                                              inc=1 if last or k == DC // 2 - 1
                                              else 0)
                                wslot_ev[half] = e_mm
                            oevw = (e_mm,)
                            if oc < 2 and a2c["xa"][oc % 2]:
                                oevw = (e_mm, a2c["xa"][oc % 2])
                            ee = act(P, osb[:, oc * TQ:(oc + 1) * TQ], ps[:],
                                     AF.Copy, waits=oevw)
                            ao_slot_ev[oc % 2] = ee
                            oproj_ev[oc] = ee
                        o_tiles = [osb[:, c * TQ:(c + 1) * TQ]
                                   for c in range(DC)]
                        emb, erb = ln_rows_emit(P, o_tiles, oproj_ev, lb2, 1e-5)
                        xa_ev = a2c["xa"]
                        for c in range(DC):
                            xw = list(xguard)
                            if xa_ev[c % 2]:
                                xw.append(xa_ev[c % 2])
                            e_x = dma(P, "scalar", "xin", xa[c % 2][:],
                                      x_in[seq, c * 128:(c + 1) * 128,
                                           t0:t0 + TQ], waits=tuple(xw))
                            vop(P, lambda E, o=o_tiles[c], a=o_tiles[c],
                                b=lb2["mb_ps"][:]: E.tensor_sub(o, a, b),
                                waits=(oproj_ev[c], emb, erb))
                            vop(P, lambda E, o=o_tiles[c], a=o_tiles[c],
                                b=lb2["rb_ps"][:]: E.tensor_mul(o, a, b))
                            e3 = vop(P, lambda E, o=o_tiles[c], a=o_tiles[c],
                                     b=xa[c % 2][:]: E.tensor_add(o, a, b),
                                     waits=(e_x,))
                            xa_ev[c % 2] = e3
                            x1_store_last = dma(
                                P, "scalar", "x1s",
                                x1_s[seq, c * 128:(c + 1) * 128, t0:t0 + TQ],
                                o_tiles[c], waits=(e3,))
                        a2c["rows"] = lb2.get("prev_rows")
                    P.flush()
            if conv_deferred and seq == SEQS - 1:
                while cvs["js"] < cvs["N"]:
                    conv_tick()

    out_store_last = None
    for tt in range(SEQS * T // TQ if stop_after == "ffn" else 0):
        seq, t0 = divmod(tt * TQ, T)
        with ExitStack() as stk:
            P.wait_fence(P.fence())
            tag = f"f{tt}"
            ph = stk.enter_context(
                nc.sbuf_tensor(f"{tag}ph", [128, HC * TQ], BF16))
            pl = stk.enter_context(
                nc.sbuf_tensor(f"{tag}pl", [128, HC * TQ], BF16))
            lb = mk_ln_bufs(stk, tag + "ln")
            s23 = stk.enter_context(ExitStack())
            x1 = [s23.enter_context(
                nc.sbuf_tensor(f"{tag}x{c}", [128, TQ], F32))
                for c in range(DC)]
            xi = s23.enter_context(
                nc.sbuf_tensor(f"{tag}xi", [128, DC * TQ], BF16))
            sb_b = s23.enter_context(
                nc.sbuf_tensor(f"{tag}sb", [128, TQ], F32))
            isv_b = s23.enter_context(
                nc.sbuf_tensor(f"{tag}isv", [128, TQ], F32))

            with ExitStack() as s2:
                _l0 = s2.enter_context(
                    nc.sbuf_tensor(f"{tag}l0", [128, TQ], F32))
                ln3t = [_l0, _l0]
                tmax = s2.enter_context(
                    nc.sbuf_tensor(f"{tag}tm", [128, TQ], F32))
                t_sx = lb["sq0"]
                t_c = lb["sq1"]
                t_r = lb["xh0"]
                t_sg = lb["xl0"]
                mx4 = s2.enter_context(
                    nc.sbuf_tensor(f"{tag}mx4", [128, 4], F32))
                amax_r = s2.enter_context(
                    nc.sbuf_tensor(f"{tag}am", [1, TQ], F32))
                srow2 = s2.enter_context(
                    nc.sbuf_tensor(f"{tag}sr", [1, TQ], F32))
                fbh = lb["mh"]
                fbl = lb["ml"]
                tr_ps = s2.enter_context(
                    nc.psum_tensor(f"{tag}trp", [128, TQ], F32))

                sb_ps = s2.enter_context(
                    nc.psum_tensor(f"{tag}sbp", [128, TQ], F32))
                iv_ps = s2.enter_context(
                    nc.psum_tensor(f"{tag}ivp", [128, TQ], F32))

                xw = (x1_store_last,) if x1_store_last else ()
                e_x = [dma(P, "scalar", "xin", x1[c][:],
                           x1_s[seq, c * 128:(c + 1) * 128, t0:t0 + TQ],
                           waits=xw)
                       for c in range(DC)]
                e_x = [e_x[-1]] * DC
                emb, erb = ln_rows_emit(P, [x1[c][:] for c in range(DC)],
                                        e_x, lb, 1e-8)
                e_mx = None
                ln_last = [None, None]
                for c in range(DC):
                    lt = ln3t[c % 2]
                    w1 = [e_x[c], emb, erb]
                    if ln_last[c % 2]:
                        w1.append(ln_last[c % 2])
                    vop(P, lambda E, o=lt[:], a=x1[c][:],
                        b=lb["mb_ps"][:]: E.tensor_sub(o, a, b),
                        waits=tuple(w1))
                    vop(P, lambda E, o=lt[:], a=lt[:],
                        b=lb["rb_ps"][:]: E.tensor_mul(o, a, b))
                    if c == 0:
                        e_mx = vop(P, lambda E, o=tmax[:], a=lt[:],
                                   b=lt[:]: E.scalar_tensor_tensor(
                                       o, a, -1.0, b, ALU.mult, ALU.max))
                    else:
                        e_a = vop(P, lambda E, o=t_sx[:], a=lt[:],
                                  b=lt[:]: E.scalar_tensor_tensor(
                                      o, a, -1.0, b, ALU.mult, ALU.max))
                        e_mx = vop(P, lambda E, o=tmax[:], a=tmax[:],
                                   b=t_sx[:]: E.tensor_max(o, a, b))
                    ln_last[c % 2] = e_mx
                e_red = None
                for j in range(4):
                    e_t = tr(P, tr_ps[:, 0:128], tmax[:, j * 128:(j + 1) * 128],
                             cst_id[:], waits=(e_mx, e_id) +
                             ((e_red,) if e_red else ()))
                    e_red = vop(P, lambda E, o=mx4[:, j:j + 1], i=tr_ps[:, 0:128]:
                                E.reduce_max(o, i, AXX), waits=(e_t,))
                e_t4 = tr(P, tr_ps[0:4, 0:128], mx4[:], cst_id[:], waits=(e_red,))
                e_cp = vop(P, lambda E, o=tmax[:, 0:128], i=tr_ps[:, 0:128]:
                           E.tensor_copy(o, i), waits=(e_t4,))
                e_am = None
                for j in range(4):
                    e_am = dma(P, "sync", "mxs", amax_r[:, j * 128:(j + 1) * 128],
                               tmax[j:j + 1, 0:128], waits=(e_cp,))
                e_rec = vop(P, lambda E, o=srow2[:], i=amax_r[:]:
                            E.reciprocal(o, i), waits=(e_am,))
                e_sr = vop(P, lambda E, o=srow2[:]: E.tensor_scalar_mul(
                    o, o, 127.0), waits=(e_rec,))
                e_ir = vop(P, lambda E, o=amax_r[:]: E.tensor_scalar_mul(
                    o, o, 1.0 / 127.0), waits=(e_rec,))
                e_sb = bcast_r(sb_ps[:], srow2[:], fbh[:], fbl[:],
                               waits=(e_sr,))
                e_sbc = vop(P, lambda E, o=sb_b[:], i=sb_ps[:]:
                            E.tensor_copy(o, i), waits=(e_sb,))
                e_iv = bcast_r(iv_ps[:], amax_r[:], fbh[:], fbl[:],
                               waits=(e_ir, e_sbc))
                e_ivc = vop(P, lambda E, o=isv_b[:], i=iv_ps[:]:
                            E.tensor_copy(o, i), waits=(e_iv,))
                e_xi = [None] * DC
                e_sg_prev = None
                for c in range(DC):
                    xv = xi[:, c * TQ:(c + 1) * TQ]
                    lt = ln3t[c % 2]
                    vop(P, lambda E, o=lt[:], a=x1[c][:], b=lb["mb_ps"][:]:
                        E.tensor_sub(o, a, b))
                    vop(P, lambda E, o=lt[:], a=lt[:], b=lb["rb_ps"][:]:
                        E.tensor_mul(o, a, b))
                    e1 = vop(P, lambda E, o=t_sx[:], a=lt[:], b=sb_b[:]:
                             E.tensor_mul(o, a, b), waits=(e_sbc,))
                    e_sg = act(P, t_sg[:], t_sx[:], AF.Sign,
                               waits=(e1,) + ((e_sg_prev,)
                                              if e_sg_prev else ()))
                    e2 = vop(P, lambda E, o=t_c[:], a=t_sx[:], b=t_sx[:]:
                             E.scalar_tensor_tensor(o, a, -1.0, b, ALU.mult,
                                                    ALU.max))
                    e3 = vop(P, lambda E, o=t_r[:], i=t_c[:]:
                             E.tensor_single_scalar(o, i, 63.5, ALU.is_ge))
                    e3 = vop(P, lambda E, o=t_r[:], i=t_r[:]:
                             E.tensor_scalar_mul(o, i, 64.0))
                    for b in (32, 16, 8, 4, 2, 1):
                        e3 = vop(P, lambda E, o=t_sx[:], a=t_r[:],
                                 bb=float(b), c2=t_c[:]:
                                 E.scalar_tensor_tensor(
                                     o, a, bb - 0.5, c2, ALU.add, ALU.is_le))
                        e3 = vop(P, lambda E, o=t_r[:], a=t_sx[:],
                                 bb=float(b), c2=t_r[:]:
                                 E.scalar_tensor_tensor(
                                     o, a, bb, c2, ALU.mult, ALU.add))
                    e_xi[c] = vop(P, lambda E, o=xv, a=t_r[:], b=t_sg[:]:
                                  E.tensor_mul(o, a, b), waits=(e3, e_sg))
                    e_sg_prev = e_xi[c]

            with ExitStack() as s3:
                wg_b = [s3.enter_context(
                    nc.sbuf_tensor(f"{tag}wg{i}", [128, D // 2], BF16))
                    for i in range(2)]
                wu_b = [s3.enter_context(
                    nc.sbuf_tensor(f"{tag}wu{i}", [128, D // 2], BF16))
                    for i in range(2)]
                gt = lb["sq0"]
                ut = lb["sq1"]
                pt = gt
                g_ps = [s3.enter_context(
                    nc.psum_tensor(f"{tag}gp{i}", [128, TQ], F32))
                    for i in range(2)]
                u_ps = [s3.enter_context(
                    nc.psum_tensor(f"{tag}up{i}", [128, TQ], F32))
                    for i in range(2)]
                wg_ev = [None, None]
                wu_ev = [None, None]
                gp_ev = [None, None]
                up_ev = [None, None]
                tmp_ev = None
                p_evts = []
                s3_guard = tuple(P.fence())
                HD2 = D // 2
                for hc in range(HC):
                    e_g = e_u = None
                    for half in range(2):
                        sl = half
                        ww = ((wg_ev[sl],) if wg_ev[sl] else ()) + \
                            p0_done + \
                            (s3_guard if wg_ev[sl] is None else ())
                        e_wg = dma(P, "sync", "wA" if sl == 0 else "wB",
                                   wg_b[sl][:],
                                   gTs[hc, :, half * HD2:(half + 1) * HD2],
                                   waits=ww)
                        ww = ((wu_ev[sl],) if wu_ev[sl] else ()) + \
                            p0_done + \
                            (s3_guard if wu_ev[sl] is None else ())
                        e_wu = dma(P, "sync", "wC" if sl == 0 else "wD",
                                   wu_b[sl][:],
                                   uTs[hc, :, half * HD2:(half + 1) * HD2],
                                   waits=ww)
                        for k in range(DC // 2):
                            c = half * (DC // 2) + k
                            first, last = (c == 0), (c == DC - 1)
                            gw = [e_wg, e_xi[c]]
                            if first and gp_ev[hc % 2]:
                                gw.append(gp_ev[hc % 2])
                            e_g = mm(P, g_ps[hc % 2][:],
                                     wg_b[sl][:, k * 128:(k + 1) * 128],
                                     xi[:, c * TQ:(c + 1) * TQ], start=first,
                                     stop=last, waits=tuple(gw),
                                     inc=1 if last or k == DC // 2 - 1 else 0)
                        wg_ev[sl] = e_g
                        for k in range(DC // 2):
                            c = half * (DC // 2) + k
                            first, last = (c == 0), (c == DC - 1)
                            uw = [e_wu, e_xi[c]]
                            if first and up_ev[hc % 2]:
                                uw.append(up_ev[hc % 2])
                            e_u = mm(P, u_ps[hc % 2][:],
                                     wu_b[sl][:, k * 128:(k + 1) * 128],
                                     xi[:, c * TQ:(c + 1) * TQ], start=first,
                                     stop=last, waits=tuple(uw),
                                     inc=1 if last or k == DC // 2 - 1 else 0)
                        wu_ev[sl] = e_u
                    psl = hc % 2
                    e1 = vop(P, lambda E, o=gt[:], a=g_ps[psl][:], b=isv_b[:]:
                             E.tensor_mul(o, a, b),
                             waits=(e_g, e_ivc) + ((tmp_ev,) if tmp_ev
                                                   else ()))
                    gp_ev[psl] = e1
                    e2 = act(P, gt[:], gt[:], AF.Silu, waits=(e1,))
                    e3 = vop(P, lambda E, o=ut[:], a=u_ps[psl][:], b=isv_b[:]:
                             E.tensor_mul(o, a, b), waits=(e_u,))
                    up_ev[psl] = e3
                    e4 = vop(P, lambda E, o=pt[:], a=gt[:], b=ut[:]:
                             E.tensor_mul(o, a, b), waits=(e2, e3))
                    hv = ph[:, hc * TQ:(hc + 1) * TQ]
                    lv = pl[:, hc * TQ:(hc + 1) * TQ]
                    e5 = act(P, hv, pt[:], AF.Copy, waits=(e4,))
                    e6 = vop(P, lambda E, o=lv, a=pt[:], b=hv:
                             E.tensor_sub(o, a, b), waits=(e5,))
                    tmp_ev = e6
                    p_evts.append(e6)

            s23.close()
            with ExitStack() as s4:
                QW = HID // 4
                wd_b = [s4.enter_context(
                    nc.sbuf_tensor(f"{tag}wd{i}", [128, QW], BF16))
                    for i in range(2)]
                x1b = [s4.enter_context(
                    nc.sbuf_tensor(f"{tag}xb{i}", [128, TQ], F32))
                    for i in range(2)]
                dsb = s4.enter_context(
                    nc.sbuf_tensor(f"{tag}dsb", [128, DC * TQ], F32))
                d_ps = [s4.enter_context(
                    nc.psum_tensor(f"{tag}dp{i}", [128, TQ], F32))
                    for i in range(2)]
                s4_guard = tuple(P.fence())
                wd_ev = [None, None]
                dp_ev = [None, None]
                d_ev = [None] * DC
                for oc in range(DC):
                    e_d = None
                    for qr in range(4):
                        sl = qr % 2
                        ww = list(p0_done)
                        ww += list(s4_guard)
                        if wd_ev[sl]:
                            ww.append(wd_ev[sl])
                        e_wd = dma(P, "sync", "wE" if sl == 0 else "wF",
                                   wd_b[sl][:],
                                   dTs[oc, :, qr * QW:(qr + 1) * QW],
                                   waits=tuple(ww))
                        for k in range(HC // 4):
                            hc = qr * (HC // 4) + k
                            first = (hc == 0)
                            last = (hc == HC - 1)
                            dw = [e_wd]
                            if first:
                                dw += p_evts
                                if dp_ev[oc % 2]:
                                    dw.append(dp_ev[oc % 2])
                            mm(P, d_ps[oc % 2][:],
                               wd_b[sl][:, k * 128:(k + 1) * 128],
                               ph[:, hc * TQ:(hc + 1) * TQ], start=first,
                               stop=False, waits=tuple(dw))
                            e_d = mm(P, d_ps[oc % 2][:],
                                     wd_b[sl][:, k * 128:(k + 1) * 128],
                                     pl[:, hc * TQ:(hc + 1) * TQ],
                                     start=False, stop=last,
                                     inc=1 if last or k == HC // 4 - 1 else 0)
                        wd_ev[sl] = e_d
                    ee = act(P, dsb[:, oc * TQ:(oc + 1) * TQ], d_ps[oc % 2][:],
                             AF.Copy, waits=(e_d,))
                    dp_ev[oc % 2] = ee
                    d_ev[oc] = ee
                d_tiles = [dsb[:, c * TQ:(c + 1) * TQ] for c in range(DC)]
                lb["_smm"] = {}
                emb, erb = ln_rows_emit(P, d_tiles, d_ev, lb, 1e-5)
                xb_ev = [None, None]
                for c in range(DC):
                    xw = [s for s in s4_guard]
                    if xb_ev[c % 2]:
                        xw.append(xb_ev[c % 2])
                    e_xb = dma(P, "scalar", "xin", x1b[c % 2][:],
                               x1_s[seq, c * 128:(c + 1) * 128, t0:t0 + TQ],
                               waits=tuple(xw))
                    vop(P, lambda E, o=d_tiles[c], a=d_tiles[c],
                        b=lb["mb_ps"][:]: E.tensor_sub(o, a, b),
                        waits=(d_ev[c], emb, erb))
                    vop(P, lambda E, o=d_tiles[c], a=d_tiles[c],
                        b=lb["rb_ps"][:]: E.tensor_mul(o, a, b))
                    e3 = vop(P, lambda E, o=d_tiles[c], a=d_tiles[c],
                             b=x1b[c % 2][:]: E.tensor_add(o, a, b),
                             waits=(e_xb,))
                    xb_ev[c % 2] = e3
                    out_store_last = dma(
                        P, "scalar", "outs",
                        out_hbm[seq, c * 128:(c + 1) * 128, t0:t0 + TQ],
                        d_tiles[c], waits=(e3,))
            P.flush()

    for key in ("outs", "x1s", "qs", "p0sa", "p0sb", "p0sc", "p0sd"):
        if P.counts[key]:
            P.emit("scalar",
                   lambda E, s=P.sems[key], v=P.counts[key]: E.wait_ge(s, v))
    P.flush(final=True)
    return nc


def _layout_inputs(x, q_w, k_w, v_w, o_w, W_g, W_u, W_d):
    def slabify(W, ocs, ics):
        a = W.reshape(ocs, 128, ics, 128)
        a = a.transpose(0, 3, 2, 1)
        return np.ascontiguousarray(a.reshape(ocs, 128, ics * 128))

    qT = slabify(q_w, DC, DC)
    kT = slabify(k_w, 1, DC)
    vT = slabify(v_w, 1, DC)
    oT = slabify(o_w, DC, DC)
    gT = slabify(W_g, HC, DC)
    uT = slabify(W_u, HC, DC)
    dT = slabify(W_d, DC, HC)
    in_maps = []
    for core in range(NCORES):
        xs = x[core * SEQS:(core + 1) * SEQS]
        x_fm = np.ascontiguousarray(xs.transpose(0, 2, 1))
        in_maps.append({"x_fm": x_fm, "qT": qT, "kT": kT, "vT": vT, "oT": oT,
                        "gT": gT, "uT": uT, "dT": dT})
    return in_maps


def kernel(L0_attn_x, L0_attn_q_w, L0_attn_k_w, L0_attn_v_w, L0_attn_o_w,
           L0_ffn_W_g, L0_ffn_W_u, L0_ffn_W_d):
    in_maps = _layout_inputs(
        np.asarray(L0_attn_x, dtype=np.float32),
        np.asarray(L0_attn_q_w, dtype=np.float32),
        np.asarray(L0_attn_k_w, dtype=np.float32),
        np.asarray(L0_attn_v_w, dtype=np.float32),
        np.asarray(L0_attn_o_w, dtype=np.float32),
        np.asarray(L0_ffn_W_g, dtype=np.float32),
        np.asarray(L0_ffn_W_u, dtype=np.float32),
        np.asarray(L0_ffn_W_d, dtype=np.float32))
    nc = build_nc()
    res = run_bass_kernel_spmd(nc, in_maps, core_ids=list(range(NCORES)))
    out = np.empty((NCORES * SEQS, T, D), dtype=np.float32)
    for core in range(NCORES):
        o = res.results[core]["out_fm"]
        out[core * SEQS:(core + 1) * SEQS] = o.transpose(0, 2, 1)
    return out


# version 04 (final)
# speedup vs baseline: 1.2599x; 1.0631x over previous
from contextlib import ExitStack

import numpy as np
import concourse.bass as bass
import concourse.mybir as mybir
from concourse.bass_utils import run_bass_kernel_spmd

F32 = mybir.dt.float32
F32R = mybir.dt.float32r
BF16 = mybir.dt.bfloat16
AF = mybir.ActivationFunctionType
ALU = mybir.AluOpType
AXX = mybir.AxisListType.X

NCORES = 8
SEQS = 2
T = 2048
D = 2048
HID = 8192
HEADS = 16
DC = D // 128
HC = HID // 128
KC = T // 128
TQ = 512
NT = T // TQ
SCALE = 1.0 / float(np.sqrt(128))
STAB = 12.0


class Evt:
    __slots__ = ("sem", "val", "eng")

    def __init__(self, sem, val, eng):
        self.sem, self.val, self.eng = sem, val, eng


class Plan:

    ENGS = ("sync", "gpsimd", "scalar", "vector", "tensor")
    DMАS = ()
    SEMS = ENGS + ("cst", "p0a", "p0b", "p0c", "p0d", "p0e", "p0f",
                   "p0sa", "p0sb", "p0sc", "p0sd", "xin", "xinB",
                   "wA", "wB", "wC", "wD", "wE", "wF",
                   "qs", "qlA", "qlB", "qlC", "x1s", "outs", "mxs")

    def __init__(self, nc):
        self.nc = nc
        self.streams = {e: [] for e in self.ENGS}
        self.sems = {s: nc.alloc_semaphore("sem_" + s) for s in self.SEMS}
        self.counts = {s: 0 for s in self.SEMS}
        self.marks = {e: {} for e in self.ENGS}

    def _wait(self, eng, evt, force=False):
        if evt is None:
            return
        if evt.eng == eng and not force:
            return
        m = self.marks[eng]
        if m.get(evt.sem, 0) >= evt.val:
            return
        m[evt.sem] = evt.val
        sem_h, val = self.sems[evt.sem], evt.val
        self.streams[eng].append(lambda E, s=sem_h, v=val: E.wait_ge(s, v))

    def emit(self, eng, fn, waits=(), inc=0, sem=None):
        is_dma = sem is not None and sem != eng
        for w in waits:
            self._wait(eng, w, force=is_dma)
        if inc:
            key = sem or eng
            self.counts[key] += inc
            sem_h, val = self.sems[key], self.counts[key]

            def run(E, fn=fn, s=sem_h, k=inc):
                fn(E).then_inc(s, k)
            self.streams[eng].append(run)
            return Evt(key, val, key if key != eng else eng)
        self.streams[eng].append(fn)
        return None

    def fence(self):
        return [Evt(s, self.counts[s], s if s.endswith("_dma") else s)
                for s in self.SEMS if self.counts[s] > 0]

    def wait_fence(self, fevts, exclude=("tensor",)):
        for e in self.ENGS:
            if e in exclude:
                continue
            for ev in fevts:
                self._wait(e, ev)

    def flush(self, final=False):
        if not final:
            return
        used = [e for e in self.ENGS if self.streams[e]]
        if not used:
            return
        with self.nc.Block() as block:
            for e in used:
                fns = self.streams[e]

                def body(E, fns=fns):
                    for f in fns:
                        f(E)
                getattr(block, e)(body)
        self.streams = {e: [] for e in self.ENGS}


def dma(P, qeng, ch, out_ap, in_ap, waits=()):
    return P.emit(qeng, lambda E: E.dma_start(out=out_ap, in_=in_ap),
                  waits=waits, inc=16, sem=ch)


def mm(P, out_ap, lhsT, rhs, start, stop, waits=(), inc=0):
    return P.emit("tensor",
                  lambda E: E.matmul(out_ap, lhsT, rhs, start=start, stop=stop),
                  waits=waits, inc=inc)


def tr(P, out_ap, in_ap, ident, waits=()):
    return P.emit("tensor", lambda E: E.transpose(out_ap, in_ap, ident),
                  waits=waits, inc=1)


def act(P, out_ap, in_ap, func, bias=0.0, scale=1.0, waits=()):
    return P.emit("scalar",
                  lambda E: E.activation(out_ap, in_ap, func, bias=bias,
                                         scale=scale),
                  waits=waits, inc=1)


def vop(P, fn, waits=()):
    return P.emit("vector", fn, waits=waits, inc=1)


def build_nc(stop_after="ffn", dbg=False, paranoid=False):
    nc = bass.Bass("TRN2", target_bir_lowering=False,
                   dynamic_dma_scratch_size=8192,
                   detect_race_conditions=False)

    x_in = nc.dram_tensor("x_fm", [SEQS, D, T], F32, kind="ExternalInput").ap()
    qT_in = nc.dram_tensor("qT", [DC, 128, D], F32, kind="ExternalInput").ap()
    kT_in = nc.dram_tensor("kT", [1, 128, D], F32, kind="ExternalInput").ap()
    vT_in = nc.dram_tensor("vT", [1, 128, D], F32, kind="ExternalInput").ap()
    oT_in = nc.dram_tensor("oT", [DC, 128, D], F32, kind="ExternalInput").ap()
    gT_in = nc.dram_tensor("gT", [HC, 128, D], F32, kind="ExternalInput").ap()
    uT_in = nc.dram_tensor("uT", [HC, 128, D], F32, kind="ExternalInput").ap()
    dT_in = nc.dram_tensor("dT", [DC, 128, HID], F32, kind="ExternalInput").ap()
    out_hbm = nc.dram_tensor("out_fm", [SEQS, D, T], F32,
                             kind="ExternalOutput").ap()

    _dk = "ExternalOutput" if dbg else "Internal"
    dbg_rows = nc.dram_tensor("dbg_rows", [12, TQ], F32, kind=_dk).ap() if dbg \
        else None
    qTr_hi = nc.dram_tensor("qTr_hi", [DC, 128, D], F32R, kind=_dk).ap()
    qTr_lo = nc.dram_tensor("qTr_lo", [DC, 128, D], F32R, kind=_dk).ap()
    oTr_hi = nc.dram_tensor("oTr_hi", [DC, 128, D], F32R).ap()
    oTr_lo = nc.dram_tensor("oTr_lo", [DC, 128, D], F32R).ap()
    kTr_hi = nc.dram_tensor("kTr_hi", [1, 128, D], F32R).ap()
    kTr_lo = nc.dram_tensor("kTr_lo", [1, 128, D], F32R).ap()
    vTr_hi = nc.dram_tensor("vTr_hi", [1, 128, D], F32R).ap()
    vTr_lo = nc.dram_tensor("vTr_lo", [1, 128, D], F32R).ap()
    gTs = nc.dram_tensor("gTs", [HC, 128, D], BF16).ap()
    uTs = nc.dram_tensor("uTs", [HC, 128, D], BF16).ap()
    dTs = nc.dram_tensor("dTs", [DC, 128, HID], BF16).ap()
    dbg_kind = "ExternalOutput" if dbg else "Internal"
    q_s = nc.dram_tensor("q_s", [SEQS, D, T], F32, kind=dbg_kind).ap()
    x1_s = nc.dram_tensor("x1_s", [SEQS, D, T], F32, kind=dbg_kind).ap()

    ones128_h = nc.inline_tensor(np.ones((128, 1), np.float32), "ones128")
    onescol_h = nc.inline_tensor(np.ones((1, 128), np.float32), "onescol")
    ident_h = nc.inline_tensor(np.eye(128, dtype=np.float32), "ident128")
    stab_h = nc.inline_tensor(np.full((128, 1), -STAB, np.float32), "stab")
    scl_h = nc.inline_tensor(np.full((128, 1), SCALE, np.float32), "scl")

    P = Plan(nc)

    cst = nc.alloc_sbuf_tensor("cst_ones128", [128, 1], F32)
    cst_col = nc.alloc_sbuf_tensor("cst_onescol", [1, 128], F32)
    cst_id = nc.alloc_sbuf_tensor("cst_ident", [128, 128], F32)
    cst_stab = nc.alloc_sbuf_tensor("cst_stab", [128, 1], F32)
    cst_scl = nc.alloc_sbuf_tensor("cst_scl", [128, 1], F32)
    cst_col_r = nc.alloc_sbuf_tensor("cst_colr", [1, 128], F32R)
    cst_ones_r = nc.alloc_sbuf_tensor("cst_onesr", [128, 1], F32R)
    e_ones = dma(P, "sync", "cst", cst[:], ones128_h.ap())
    e_col = dma(P, "sync", "cst", cst_col[:], onescol_h.ap())
    e_id = dma(P, "sync", "cst", cst_id[:], ident_h.ap())
    e_stab = dma(P, "sync", "cst", cst_stab[:], stab_h.ap())
    e_scl = dma(P, "sync", "cst", cst_scl[:], scl_h.ap())
    e_colr = act(P, cst_col_r[:], cst_col[:], AF.Copy, waits=(e_col,))
    e_onesr = act(P, cst_ones_r[:], cst[:], AF.Copy, waits=(e_ones,))

    def bcast_r(ps_ap, row_ap, hi_ap, lo_ap, waits=()):
        eh = act(P, hi_ap, row_ap, AF.Copy, waits=waits)
        el = vop(P, lambda E, o=lo_ap, a=row_ap, b=hi_ap: E.tensor_sub(o, a, b),
                 waits=(eh,))
        mm(P, ps_ap, cst_col_r[:], hi_ap, start=True, stop=False,
           waits=(eh, e_colr))
        return mm(P, ps_ap, cst_col_r[:], lo_ap, start=False, stop=True,
                  waits=(el,), inc=1)

    def ln_rows_emit(P, x_tiles, x_evts, lb, eps):
        sq = (lb["sq0"], lb["sq1"])
        xh = (lb["xh0"], lb["xh1"])
        xl = (lb["xl0"], lb["xl1"])
        prev_rows = lb.get("prev_rows") or ()
        mmev = lb["_smm"]
        e_m = None
        for c in range(DC):
            w = [x_evts[c]]
            if ("m", c - 2) in mmev:
                w.append(mmev[("m", c - 2)])
            elif c < 2:
                w.extend(prev_rows)
            eh = act(P, xh[c % 2][:], x_tiles[c], AF.Copy, waits=tuple(w))
            el = vop(P, lambda E, o=xl[c % 2][:], a=x_tiles[c], b=xh[c % 2][:]:
                     E.tensor_sub(o, a, b), waits=(eh,))
            mm(P, lb["mean_ps"][0:1, :], cst_ones_r[:], xh[c % 2][:],
               start=(c == 0), stop=False,
               waits=(eh, e_onesr) + (tuple(prev_rows) if c == 0 else ()))
            e_m = mm(P, lb["mean_ps"][0:1, :], cst_ones_r[:], xl[c % 2][:],
                     start=False, stop=(c == DC - 1), waits=(el,), inc=1)
            mmev[("m", c)] = e_m
        e_s = None
        for c in range(DC):
            w = [x_evts[c]]
            if ("s", c - 2) in mmev:
                w.append(mmev[("s", c - 2)])
            elif c < 2:
                w.append(mmev[("m", DC - 2 + c)])
            e_sq = act(P, sq[c % 2][:], x_tiles[c], AF.Square, waits=tuple(w))
            eh = act(P, xh[c % 2][:], sq[c % 2][:], AF.Copy, waits=(e_sq,))
            el = vop(P, lambda E, o=xl[c % 2][:], a=sq[c % 2][:],
                     b=xh[c % 2][:]: E.tensor_sub(o, a, b), waits=(eh,))
            mm(P, lb["sq_ps"][0:1, :], cst_ones_r[:], xh[c % 2][:],
               start=(c == 0), stop=False,
               waits=(eh,) + (tuple(prev_rows) if c == 0 else ()))
            e_s = mm(P, lb["sq_ps"][0:1, :], cst_ones_r[:], xl[c % 2][:],
                     start=False, stop=(c == DC - 1), waits=(el,), inc=1)
            mmev[("s", c)] = e_s
        m_row, v_row, mm_row = lb["m_row"], lb["v_row"], lb["mm_row"]
        em = vop(P, lambda E: E.tensor_scalar_mul(m_row[:], lb["mean_ps"][0:1, :],
                                                  1.0 / D), waits=(e_m,))
        es = vop(P, lambda E: E.tensor_scalar_mul(v_row[:], lb["sq_ps"][0:1, :],
                                                  1.0 / D), waits=(e_s,))
        e1 = vop(P, lambda E: E.tensor_mul(mm_row[:], m_row[:], m_row[:]),
                 waits=(em,))
        e2 = vop(P, lambda E: E.tensor_sub(v_row[:], v_row[:], mm_row[:]),
                 waits=(es, e1))
        e2b = vop(P, lambda E: E.tensor_scalar_add(v_row[:], v_row[:],
                                                   float(eps)), waits=(e2,))
        e2c = vop(P, lambda E: E.reciprocal(v_row[:], v_row[:]), waits=(e2b,))
        er = act(P, v_row[:], v_row[:], AF.Sqrt, waits=(e2c,))
        emb = bcast_r(lb["mb_ps"][:], m_row[:], lb["mh"][:], lb["ml"][:],
                      waits=(em,))
        erb = bcast_r(lb["rb_ps"][:], v_row[:], lb["vh"][:], lb["vl"][:],
                      waits=(er,))
        lb["prev_rows"] = (em, es)
        return emb, erb

    def mk_ln_bufs(stk, tag):
        return {
            "sq0": stk.enter_context(nc.sbuf_tensor(tag + "_sq0", [128, TQ], F32)),
            "sq1": stk.enter_context(nc.sbuf_tensor(tag + "_sq1", [128, TQ], F32)),
            "m_row": stk.enter_context(nc.sbuf_tensor(tag + "_m", [1, TQ], F32)),
            "v_row": stk.enter_context(nc.sbuf_tensor(tag + "_v", [1, TQ], F32)),
            "mm_row": stk.enter_context(nc.sbuf_tensor(tag + "_mm", [1, TQ], F32)),
            "xh0": stk.enter_context(nc.sbuf_tensor(tag + "_xh0", [128, TQ], F32R)),
            "xh1": stk.enter_context(nc.sbuf_tensor(tag + "_xh1", [128, TQ], F32R)),
            "xl0": stk.enter_context(nc.sbuf_tensor(tag + "_xl0", [128, TQ], F32R)),
            "xl1": stk.enter_context(nc.sbuf_tensor(tag + "_xl1", [128, TQ], F32R)),
            "mh": stk.enter_context(nc.sbuf_tensor(tag + "_mh", [1, TQ], F32R)),
            "ml": stk.enter_context(nc.sbuf_tensor(tag + "_ml", [1, TQ], F32R)),
            "vh": stk.enter_context(nc.sbuf_tensor(tag + "_vh", [1, TQ], F32R)),
            "vl": stk.enter_context(nc.sbuf_tensor(tag + "_vl", [1, TQ], F32R)),
            "mean_ps": stk.enter_context(nc.psum_tensor(tag + "_mps", [128, TQ], F32)),
            "sq_ps": stk.enter_context(nc.psum_tensor(tag + "_sps", [128, TQ], F32)),
            "mb_ps": stk.enter_context(nc.psum_tensor(tag + "_mb", [128, TQ], F32)),
            "rb_ps": stk.enter_context(nc.psum_tensor(tag + "_rb", [128, TQ], F32)),
            "_smm": {},
        }

    with (
        nc.sbuf_tensor("p0_in", [128, 4 * D], F32) as p0_in,
        nc.sbuf_tensor("p0_hi", [128, 4 * D], F32R) as p0_hi,
        nc.sbuf_tensor("p0_lo", [128, 4 * D], F32R) as p0_lo,
    ):
        attn_jobs = []
        for oc in range(DC):
            attn_jobs.append((qT_in[oc], qTr_hi[oc], qTr_lo[oc], D))
            attn_jobs.append((oT_in[oc], oTr_hi[oc], oTr_lo[oc], D))
        attn_jobs.append((kT_in[0], kTr_hi[0], kTr_lo[0], D))
        attn_jobs.append((vT_in[0], vTr_hi[0], vTr_lo[0], D))
        if stop_after == "p0s":
            attn_jobs = attn_jobs[:4]
        elif stop_after.startswith(("p0f", "p0d")):
            attn_jobs = []
        LOADQ = [("scalar", "p0a"), ("scalar", "p0b"),
                 ("sync", "p0c"), ("sync", "p0d")]
        STORQ = [("sync", "p0sa"), ("sync", "p0sb"),
                 ("scalar", "p0sc"), ("scalar", "p0sd")]
        slot_ev = [None] * 4
        for j, (src, dst_hi, dst_lo, w) in enumerate(attn_jobs):
            ln = j % 4
            off = ln * D
            iv, hv, lv = (p0_in[:, off:off + w], p0_hi[:, off:off + w],
                          p0_lo[:, off:off + w])
            lq, lch = LOADQ[ln]
            sq, sch = STORQ[ln]
            eld = dma(P, lq, lch, iv, src,
                      waits=(slot_ev[ln],) if slot_ev[ln] else ())
            ehi = act(P, hv, iv, AF.Copy, waits=(eld,))
            elo = vop(P, lambda E, o=lv, a=iv, b=hv: E.tensor_sub(o, a, b),
                      waits=(ehi,))
            dma(P, sq, sch, dst_hi, hv, waits=(ehi,))
            e2 = dma(P, sq, sch, dst_lo, lv, waits=(elo,))
            slot_ev[ln] = e2
        p0_done = tuple(Evt(ch, P.counts[ch], ch)
                        for ch in ("p0sa", "p0sb", "p0sc", "p0sd"))
        P.flush()

    ffn_jobs = []
    for hc in range(HC):
        ffn_jobs.append((gT_in[hc], gTs[hc], D))
        ffn_jobs.append((uT_in[hc], uTs[hc], D))
    for oc in range(DC):
        ffn_jobs.append((dT_in[oc], dTs[oc], HID))
    if stop_after == "p0s":
        ffn_jobs = ffn_jobs[:4]
    elif stop_after == "p0a":
        ffn_jobs = []
    elif stop_after == "p0d":
        ffn_jobs = ffn_jobs[-DC:]
    elif stop_after == "p0f":
        ffn_jobs = ffn_jobs[:-DC]
    elif stop_after.startswith("p0f"):
        ffn_jobs = ffn_jobs[:int(stop_after[3:])]
    CW, CSLOTS, CK = 512, 5, 4
    conv_jobs = []
    for (src, dst, w) in ffn_jobs:
        for o in range(0, w, CW):
            conv_jobs.append((src[:, o:o + CW], dst[:, o:o + CW]))
    cvb = {}
    cvs = {"jl": 0, "js": 0, "ld_ev": [None] * CSLOTS,
           "sg_ev": [None] * CSLOTS, "st_ev": [None, None],
           "N": len(conv_jobs)}
    CLOADQ = [("scalar", "p0e"), ("sync", "p0f")]
    CSTORQ = [("sync", "p0sc"), ("scalar", "p0sd")]

    def conv_tick():
        if cvs["jl"] < cvs["N"]:
            j = cvs["jl"]
            sl = j % CSLOTS
            lq, lch = CLOADQ[j % 2]
            iv = cvb["in"][:, sl * CW:(sl + 1) * CW]
            sg0 = cvs["sg_ev"][sl]
            w = sg0 if isinstance(sg0, tuple) else (sg0,) if sg0 else ()
            cvs["ld_ev"][sl] = dma(P, lq, lch, iv, conv_jobs[j][0], waits=w)
            cvs["jl"] += 1
        if cvs["js"] < cvs["N"] and \
                (cvs["js"] <= cvs["jl"] - CK or cvs["jl"] >= cvs["N"]):
            j = cvs["js"]
            sl = j % CSLOTS
            ln = j % 2
            iv = cvb["in"][:, sl * CW:(sl + 1) * CW]
            sgv = cvb["sg"][:, 0:CW]
            t1v = cvb["t1"][:, 0:CW]
            t2v = cvb["t2"][:, 0:CW]
            w0 = (cvs["ld_ev"][sl],) + ((cvs["st_ev"][0],)
                                        if cvs["st_ev"][0] else ())
            e3 = act(P, sgv, iv, AF.Sign, waits=w0)
            cvs["sg_ev"][sl] = e3
            sq, sch = CSTORQ[ln]
            cvs["st_ev"][0] = dma(P, sq, sch, conv_jobs[j][1], sgv,
                                  waits=(e3,))
            cvs["js"] += 1

    conv_deferred = (stop_after == "ffn")

    with (
        nc.sbuf_tensor("K_hi", [128, T], F32R) as K_hi,
        nc.sbuf_tensor("K_lo", [128, T], F32R) as K_lo,
        nc.sbuf_tensor("V_hi", [128, T], F32R) as V_hi,
        nc.sbuf_tensor("V_lo", [128, T], F32R) as V_lo,
        nc.sbuf_tensor("cv_in", [128, CSLOTS * CW], F32) as cv_in,
        nc.sbuf_tensor("cv_sg", [128, CW], BF16) as cv_sg,
        nc.sbuf_tensor("cv_t1", [128, CW], BF16) as cv_t1,
        nc.sbuf_tensor("cv_t2", [128, CW], BF16) as cv_t2,
    ):
        cvb["in"], cvb["sg"], cvb["t1"], cvb["t2"] = cv_in, cv_sg, cv_t1, cv_t2
        if not conv_deferred:
            cvs["sg_ev"] = [p0_done] * CSLOTS
            while cvs["js"] < cvs["N"]:
                conv_tick()
        a1_q_done = {}
        x1_store_last = None

        for seq in ((1,) if stop_after == "a1x" else range(SEQS if not stop_after.startswith("p0") else 0)):
            with ExitStack() as stk:
                xa = [stk.enter_context(
                    nc.sbuf_tensor(f"a1x{seq}_{c}", [128, TQ], F32))
                    for c in range(DC)]
                lnh = stk.enter_context(
                    nc.sbuf_tensor(f"a1lnh{seq}", [128, DC * TQ], F32R))
                lnl = stk.enter_context(
                    nc.sbuf_tensor(f"a1lnl{seq}", [128, DC * TQ], F32R))
                lnt = [stk.enter_context(
                    nc.sbuf_tensor(f"a1lnt{seq}_{i}", [128, TQ], F32))
                    for i in range(2)]
                lb = mk_ln_bufs(stk, f"a1{seq}")
                wq = [stk.enter_context(
                    nc.sbuf_tensor(f"a1wq{seq}_{i}", [128, D], F32R))
                    for i in range(4)]
                ev_out = stk.enter_context(
                    nc.sbuf_tensor(f"a1ev{seq}", [128, TQ], F32))
                V_fm = stk.enter_context(
                    nc.sbuf_tensor(f"a1vfm{seq}", [128, T], F32))
                proj_ps = [stk.enter_context(
                    nc.psum_tensor(f"a1p{seq}_{i}", [128, TQ], F32))
                    for i in range(2)]
                tr_ps = stk.enter_context(
                    nc.psum_tensor(f"a1tr{seq}", [128, TQ], F32))

                wslot_ev = [None, None]
                psum_ev = [None, None]
                ev_store_ev = None
                kv_done = []
                for it in range(NT):
                    P.wait_fence(P.fence())
                    lb["_smm"] = {}
                    t0 = it * TQ
                    e_x = [dma(P, ("scalar", "sync")[c % 2],
                               ("xin", "xinB")[c % 2], xa[c][:],
                               x_in[seq, c * 128:(c + 1) * 128, t0:t0 + TQ])
                           for c in range(DC)]
                    emb, erb = ln_rows_emit(
                        P, [xa[c][:] for c in range(DC)], e_x, lb, 1e-5)
                    if paranoid:
                        P.wait_fence(P.fence())
                    e_lh = [None] * DC
                    e_ll = [None] * DC
                    for c in range(DC):
                        hv = lnh[:, c * TQ:(c + 1) * TQ]
                        lv = lnl[:, c * TQ:(c + 1) * TQ]
                        w1 = [e_x[c], emb]
                        if c >= 2:
                            w1.append(e_lh[c - 2])
                        e1 = vop(P, lambda E, o=lnt[c % 2][:], a=xa[c][:],
                                 b=lb["mb_ps"][:]: E.tensor_sub(o, a, b),
                                 waits=tuple(w1))
                        e2 = vop(P, lambda E, o=lnt[c % 2][:], a=lnt[c % 2][:],
                                 b=lb["rb_ps"][:]: E.tensor_mul(o, a, b),
                                 waits=(erb,))
                        e_lh[c] = act(P, hv, lnt[c % 2][:], AF.Copy,
                                      waits=(e2,))
                        e_ll[c] = vop(P, lambda E, o=lv, a=lnt[c % 2][:], b=hv:
                                      E.tensor_sub(o, a, b), waits=(e_lh[c],))
                    if paranoid:
                        P.wait_fence(P.fence())
                    if dbg and it == 0:
                        base = seq * 6
                        dma(P, "scalar", "mxs", dbg_rows[base + 0:base + 1, :],
                            lb["m_row"][:], waits=(emb,))
                        dma(P, "scalar", "mxs", dbg_rows[base + 1:base + 2, :],
                            lb["v_row"][:], waits=(erb,))
                        dma(P, "scalar", "mxs", dbg_rows[base + 2:base + 3, :].bitcast(F32R),
                            lnh[0:1, 0:TQ], waits=(e_lh[0],))
                        dma(P, "scalar", "mxs", dbg_rows[base + 3:base + 4, :].bitcast(F32R),
                            lnl[0:1, 0:TQ], waits=(e_ll[0],))
                        e_d1 = vop(P, lambda E, o=lnt[0][:], i=lb["mb_ps"][:]:
                                   E.tensor_copy(o, i), waits=(emb,))
                        dma(P, "scalar", "mxs", dbg_rows[base + 4:base + 5, :],
                            lnt[0][0:1, :], waits=(e_d1,))
                        e_d2 = vop(P, lambda E, o=lnt[0][:], i=lb["rb_ps"][:]:
                                   E.tensor_copy(o, i), waits=(erb,))
                        dma(P, "scalar", "mxs", dbg_rows[base + 5:base + 6, :],
                            lnt[0][0:1, :], waits=(e_d2,))
                    jobs = [("q", oc) for oc in range(DC)] + [("k", 0), ("v", 0)]
                    for jn, (kind, oc) in enumerate(jobs):
                        if paranoid:
                            P.wait_fence(P.fence())
                        wsl = jn % 2
                        psl = jn % 2
                        if kind == "q":
                            src_h, src_l = qTr_hi[oc], qTr_lo[oc]
                        elif kind == "k":
                            src_h, src_l = kTr_hi[0], kTr_lo[0]
                        else:
                            src_h, src_l = vTr_hi[0], vTr_lo[0]
                        wh, wl = wq[2 * wsl], wq[2 * wsl + 1]
                        wch = "wA" if wsl == 0 else "wB"
                        ww = ((wslot_ev[wsl],) + p0_done) if wslot_ev[wsl] \
                            else p0_done
                        e_wh = dma(P, "sync", wch, wh[:], src_h, waits=ww)
                        e_wl = dma(P, "sync", wch, wl[:], src_l, waits=ww)
                        ps = proj_ps[psl]
                        pw = (psum_ev[psl],) if psum_ev[psl] else ()
                        e_mm = None
                        for c in range(DC):
                            whc = wh[:, c * 128:(c + 1) * 128]
                            wlc = wl[:, c * 128:(c + 1) * 128]
                            rh = lnh[:, c * TQ:(c + 1) * TQ]
                            rl = lnl[:, c * TQ:(c + 1) * TQ]
                            first, last = (c == 0), (c == DC - 1)
                            mm(P, ps[:], whc, rh, start=first, stop=False,
                               waits=(e_wh, e_lh[c]) + (pw if first else ()))
                            mm(P, ps[:], whc, rl, start=False, stop=False,
                               waits=(e_ll[c],))
                            e_mm = mm(P, ps[:], wlc, rh, start=False, stop=last,
                                      waits=(e_wl,), inc=1 if last else 0)
                        wslot_ev[wsl] = e_mm
                        if kind == "q":
                            ee = vop(P, lambda E, o=ev_out[:], i=ps[:]:
                                     E.tensor_copy(o, i),
                                     waits=(e_mm,) + ((ev_store_ev,)
                                                      if ev_store_ev else ()))
                            ev_store_ev = dma(
                                P, "scalar", "qs",
                                q_s[seq, oc * 128:(oc + 1) * 128, t0:t0 + TQ],
                                ev_out[:], waits=(ee,))
                            psum_ev[psl] = ee
                        elif kind == "k":
                            eh = act(P, K_hi[:, t0:t0 + TQ], ps[:], AF.Copy,
                                     waits=(e_mm,))
                            ee = vop(P, lambda E, o=K_lo[:, t0:t0 + TQ],
                                     a=ps[:], b=K_hi[:, t0:t0 + TQ]:
                                     E.tensor_sub(o, a, b), waits=(eh,))
                            psum_ev[psl] = ee
                            kv_done.append(ee)
                        else:
                            ee = vop(P, lambda E, o=V_fm[:, t0:t0 + TQ], i=ps[:]:
                                     E.tensor_copy(o, i), waits=(e_mm,))
                            psum_ev[psl] = ee
                            kv_done.append(ee)
                vt_done = []
                e_prev = None
                for kc in range(KC):
                    e_t = tr(P, tr_ps[:, 0:128], V_fm[:, kc * 128:(kc + 1) * 128],
                             cst_id[:],
                             waits=(kv_done[-1], e_id) +
                                   ((e_prev,) if e_prev else ()))
                    eh = act(P, V_hi[:, kc * 128:(kc + 1) * 128],
                             tr_ps[:, 0:128], AF.Copy, waits=(e_t,))
                    e_c = vop(P, lambda E, o=V_lo[:, kc * 128:(kc + 1) * 128],
                              a=tr_ps[:, 0:128],
                              b=V_hi[:, kc * 128:(kc + 1) * 128]:
                              E.tensor_sub(o, a, b), waits=(eh,))
                    e_prev = e_c
                    vt_done.append(e_c)
                a1_q_done[seq] = Evt("qs", P.counts["qs"], "qs")
                P.flush()

            deps_kv = tuple(kv_done[-1:]) + tuple(vt_done[-1:]) + \
                tuple(vt_done[:1])
            a2c = {"q": [None, None, None], "qld": [None, None, None],
                   "qsp": [None, None, None],
                   "exp": [None, None, None, None],
                   "sc": [None, None, None],
                   "ao": [None, None], "att": None, "es": None, "eib": None,
                   "eic": None, "wo": [None, None], "xa": [None, None],
                   "rows": None}
            for qt in range(NT if not (stop_after.startswith("p0") or stop_after in ("a1", "a1x")) else 0):
                t0 = qt * TQ
                with ExitStack() as stk:
                    P.wait_fence(P.fence())
                    tag = f"a2{seq}_{qt}"
                    atth = stk.enter_context(
                        nc.sbuf_tensor(f"{tag}ah", [128, DC * TQ], F32R))
                    attl = stk.enter_context(
                        nc.sbuf_tensor(f"{tag}al", [128, DC * TQ], F32R))
                    wo = [stk.enter_context(
                        nc.sbuf_tensor(f"{tag}wo{i}", [128, D // 2], F32R))
                        for i in range(4)]
                    osb = stk.enter_context(
                        nc.sbuf_tensor(f"{tag}osb", [128, DC * TQ], F32))
                    ao_ps = [stk.enter_context(
                        nc.psum_tensor(f"{tag}ao{i}", [128, TQ], F32))
                        for i in range(2)]

                    att_split_ev = [None] * DC
                    ao_slot_ev = a2c["ao"]
                    with ExitStack() as s1:
                        qh = [s1.enter_context(
                            nc.sbuf_tensor(f"{tag}q{i}", [128, TQ], F32))
                            for i in range(3)]
                        qhh = [s1.enter_context(
                            nc.sbuf_tensor(f"{tag}qh{i}", [128, TQ], F32R))
                            for i in range(3)]
                        qhl = [s1.enter_context(
                            nc.sbuf_tensor(f"{tag}ql{i}", [128, TQ], F32R))
                            for i in range(3)]
                        expb = [s1.enter_context(
                            nc.sbuf_tensor(f"{tag}e{i}", [128, TQ], F32))
                            for i in range(4)]
                        exph = [s1.enter_context(
                            nc.sbuf_tensor(f"{tag}eh{i}", [128, TQ], F32R))
                            for i in range(4)]
                        expl = [s1.enter_context(
                            nc.sbuf_tensor(f"{tag}el{i}", [128, TQ], F32R))
                            for i in range(4)]
                        sacc = s1.enter_context(
                            nc.sbuf_tensor(f"{tag}sacc", [128, TQ], F32))
                        srow = s1.enter_context(
                            nc.sbuf_tensor(f"{tag}srow", [1, TQ], F32))
                        srh = s1.enter_context(
                            nc.sbuf_tensor(f"{tag}srh", [1, TQ], F32R))
                        srl = s1.enter_context(
                            nc.sbuf_tensor(f"{tag}srl", [1, TQ], F32R))
                        isb = s1.enter_context(
                            nc.sbuf_tensor(f"{tag}isb", [128, TQ], F32))
                        att_t = s1.enter_context(
                            nc.sbuf_tensor(f"{tag}att", [128, TQ], F32))
                        sc_ps = [s1.enter_context(
                            nc.psum_tensor(f"{tag}sc{i}", [128, TQ], F32))
                            for i in range(3)]
                        s_ps = s1.enter_context(
                            nc.psum_tensor(f"{tag}sps", [128, TQ], F32))
                        is_ps = s1.enter_context(
                            nc.psum_tensor(f"{tag}isps", [128, TQ], F32))
                        q_slot_ev = a2c["q"]
                        exp_slot_ev = a2c["exp"]
                        sc_slot_ev = a2c["sc"]
                        att_tmp_ev = a2c["att"]
                        e_s_prev = a2c["es"]
                        e_ib_prev = a2c["eib"]
                        e_ic_prev = a2c["eic"]
                        qld_ev = a2c["qld"]
                        qsp_ev = a2c["qsp"]

                        def q_load(hh):
                            sl = hh % 3
                            sw = (qsp_ev[sl][1],) if qsp_ev[sl] else ()
                            qld_ev[sl] = dma(
                                P, "sync", ("qlA", "qlB", "qlC")[sl],
                                qh[sl][:],
                                q_s[seq, hh * 128:(hh + 1) * 128,
                                    t0:t0 + TQ],
                                waits=sw + (a1_q_done[seq],))

                        def q_split(hh):
                            sl = hh % 3
                            qw2 = (q_slot_ev[sl],) if q_slot_ev[sl] else ()
                            e_qh2 = act(P, qhh[sl][:], qh[sl][:], AF.Copy,
                                        waits=(qld_ev[sl],) + qw2)
                            e_ql2 = vop(P, lambda E, o=qhl[sl][:],
                                        a=qh[sl][:], b=qhh[sl][:]:
                                        E.tensor_sub(o, a, b),
                                        waits=(e_qh2,))
                            qsp_ev[sl] = (e_qh2, e_ql2)

                        q_load(0)
                        q_load(1)
                        q_split(0)
                        st = {"es": e_s_prev, "eib": e_ib_prev,
                              "eic": e_ic_prev, "att": att_tmp_ev,
                              "er": None}
                        SKEW = 2
                        pend = None

                        def schain_a(pd):
                            ph = pd["h"]
                            psl = ph % 3
                            ssp = (st["es"],) if st["es"] else ()
                            rsp = (st["er"],) if st["er"] else ()
                            e_kh = act(P, qhh[psl][:], sacc[:], AF.Copy,
                                       waits=(pd["e_sacc"], pd["e_ao"]) + ssp)
                            e_kl = vop(P, lambda E, o=qhl[psl][:], a=sacc[:],
                                       b=qhh[psl][:]: E.tensor_sub(o, a, b),
                                       waits=(e_kh,))
                            mm(P, s_ps[0:1, :], cst_ones_r[:], qhh[psl][:],
                               start=True, stop=False,
                               waits=(e_kh, e_onesr) + ssp + rsp)
                            e_s = mm(P, s_ps[0:1, :], cst_ones_r[:],
                                     qhl[psl][:], start=False, stop=True,
                                     waits=(e_kl,), inc=1)
                            st["es"] = e_s
                            q_slot_ev[psl] = e_s
                            pd["e_s"] = e_s

                        def schain_b(pd):
                            pd["e_r"] = vop(P, lambda E, o=srow[:],
                                            i=s_ps[0:1, :]:
                                            E.reciprocal(o, i),
                                            waits=(pd["e_s"],) +
                                                  ((st["eib"],)
                                                   if st["eib"] else ()))
                            st["er"] = pd["e_r"]

                        def schain_c(pd):
                            ph = pd["h"]
                            e_ib = bcast_r(is_ps[:], srow[:], srh[:], srl[:],
                                           waits=(pd["e_r"],) +
                                                 ((st["eic"],)
                                                  if st["eic"] else ()))
                            st["eib"] = e_ib
                            e_ic = vop(P, lambda E, o=isb[:], i=is_ps[:]:
                                       E.tensor_copy(o, i), waits=(e_ib,))
                            st["eic"] = e_ic
                            e_at = vop(P, lambda E, o=att_t[:],
                                       a=ao_ps[ph % 2][:], b=isb[:]:
                                       E.tensor_mul(o, a, b),
                                       waits=(pd["e_ao"], e_ic) +
                                             ((st["att"],) if st["att"]
                                              else ()))
                            ao_slot_ev[ph % 2] = e_at
                            hv = atth[:, ph * TQ:(ph + 1) * TQ]
                            lv = attl[:, ph * TQ:(ph + 1) * TQ]
                            hw_ = (e_at,)
                            if ph < 2:
                                hw_ = hw_ + tuple(e for e in a2c["wo"] if e)
                            e_h = act(P, hv, att_t[:], AF.Copy, waits=hw_)
                            e_l = vop(P, lambda E, o=lv, a=att_t[:], b=hv:
                                      E.tensor_sub(o, a, b), waits=(e_h,))
                            st["att"] = e_l
                            att_split_ev[ph] = e_l

                        for h in range(HEADS):
                            qsl = h % 3
                            if h + 2 < HEADS:
                                q_load(h + 2)
                            if conv_deferred:
                                for _ in range(6):
                                    conv_tick()
                            if h + 1 < HEADS:
                                q_split(h + 1)
                            e_qh, e_ql = qsp_ev[qsl]
                            e_sacc = None
                            e_ao_last = None
                            split_ev = [None] * KC
                            for kc in range(KC + SKEW):
                                if kc < KC:
                                    scsl = kc % 3
                                    esl = kc % 4
                                    scw = (sc_slot_ev[scsl],) \
                                        if sc_slot_ev[scsl] else ()
                                    kh = K_hi[:, kc * 128:(kc + 1) * 128]
                                    kl = K_lo[:, kc * 128:(kc + 1) * 128]
                                    mm(P, sc_ps[scsl][:], kh, qhh[qsl][:],
                                       start=True, stop=False,
                                       waits=(e_qh,) + deps_kv + scw)
                                    mm(P, sc_ps[scsl][:], kh, qhl[qsl][:],
                                       start=False, stop=False, waits=(e_ql,))
                                    e_sc = mm(P, sc_ps[scsl][:], kl,
                                              qhh[qsl][:], start=False,
                                              stop=True, inc=1)
                                    if kc == 0 and pend is not None:
                                        schain_a(pend)
                                    ew = exp_slot_ev[esl] or ()
                                    e_e = act(P, expb[esl][:], sc_ps[scsl][:],
                                              AF.Exp, bias=cst_stab[:],
                                              scale=cst_scl[:],
                                              waits=(e_sc, e_stab, e_scl) +
                                                    tuple(ew))
                                    sc_slot_ev[scsl] = e_e
                                    e_eh = act(P, exph[esl][:], expb[esl][:],
                                               AF.Copy, waits=(e_e,))
                                    e_el = vop(P, lambda E, o=expl[esl][:],
                                               a=expb[esl][:], b=exph[esl][:]:
                                               E.tensor_sub(o, a, b),
                                               waits=(e_eh,))
                                    split_ev[kc] = (e_eh, e_el)
                                    if kc == 0:
                                        sw = (e_e,) + ((st["es"],)
                                                       if st["es"] else ())
                                        e_sacc = vop(P, lambda E, o=sacc[:],
                                                     i=expb[esl][:]:
                                                     E.tensor_copy(o, i),
                                                     waits=sw)
                                    else:
                                        e_sacc = vop(P, lambda E, o=sacc[:],
                                                     a=sacc[:], b=expb[esl][:]:
                                                     E.tensor_add(o, a, b),
                                                     waits=(e_e,))
                                    if kc == 2 and pend is not None:
                                        schain_b(pend)
                                    if kc == 6 and pend is not None:
                                        schain_c(pend)
                                        pend = None
                                if kc >= SKEW:
                                    kv = kc - SKEW
                                    vsl = kv % 4
                                    vh = V_hi[:, kv * 128:(kv + 1) * 128]
                                    vl = V_lo[:, kv * 128:(kv + 1) * 128]
                                    e_eh2, e_el2 = split_ev[kv]
                                    aw = [e_eh2]
                                    if kv == 0 and ao_slot_ev[h % 2] is not None:
                                        aw.append(ao_slot_ev[h % 2])
                                    mm(P, ao_ps[h % 2][:], vh, exph[vsl][:],
                                       start=(kv == 0), stop=False,
                                       waits=tuple(aw))
                                    mm(P, ao_ps[h % 2][:], vh, expl[vsl][:],
                                       start=False, stop=False,
                                       waits=(e_el2,))
                                    e_ao_last = mm(P, ao_ps[h % 2][:], vl,
                                                   exph[vsl][:], start=False,
                                                   stop=(kv == KC - 1),
                                                   inc=1 if kv == KC - 1
                                                   else 0)
                                    exp_slot_ev[vsl] = (e_ao_last, e_sacc)
                            pend = {"h": h, "e_sacc": e_sacc,
                                    "e_ao": e_ao_last}
                        schain_a(pend)
                        schain_b(pend)
                        schain_c(pend)
                        pend = None
                        e_s_prev = st["es"]
                        e_ib_prev = st["eib"]
                        e_ic_prev = st["eic"]
                        att_tmp_ev = st["att"]

                    a2c["att"] = att_tmp_ev
                    a2c["es"] = e_s_prev
                    a2c["eib"] = e_ib_prev
                    a2c["eic"] = e_ic_prev
                    with ExitStack() as s2:
                        lb2 = mk_ln_bufs(s2, tag + "ln")
                        lb2["prev_rows"] = a2c["rows"] or ()
                        xa = [s2.enter_context(
                            nc.sbuf_tensor(f"{tag}x{i}", [128, TQ], F32))
                            for i in range(2)]
                        xguard = tuple(P.fence())
                        wslot_ev = a2c["wo"]
                        oproj_ev = [None] * DC
                        for oc in range(DC):
                            ps = ao_ps[oc % 2]
                            pw = (ao_slot_ev[oc % 2],) \
                                if ao_slot_ev[oc % 2] else ()
                            e_mm = None
                            for half in range(2):
                                wh, wl = wo[2 * half], wo[2 * half + 1]
                                wch = "wA" if half == 0 else "wB"
                                ww = ((wslot_ev[half],) if wslot_ev[half]
                                      else ()) + p0_done
                                e_wh = dma(P, "sync", wch, wh[:],
                                           oTr_hi[oc, :, half * (D // 2):
                                                  (half + 1) * (D // 2)],
                                           waits=ww)
                                e_wl = dma(P, "sync", wch, wl[:],
                                           oTr_lo[oc, :, half * (D // 2):
                                                  (half + 1) * (D // 2)],
                                           waits=ww)
                                for k in range(DC // 2):
                                    c = half * (DC // 2) + k
                                    whc = wh[:, k * 128:(k + 1) * 128]
                                    wlc = wl[:, k * 128:(k + 1) * 128]
                                    rh = atth[:, c * TQ:(c + 1) * TQ]
                                    rl = attl[:, c * TQ:(c + 1) * TQ]
                                    first = (c == 0)
                                    last = (c == DC - 1)
                                    mm(P, ps[:], whc, rh, start=first,
                                       stop=False,
                                       waits=(e_wh, att_split_ev[c]) +
                                             (pw if first else ()))
                                    mm(P, ps[:], whc, rl, start=False,
                                       stop=False)
                                    e_mm = mm(P, ps[:], wlc, rh, start=False,
                                              stop=last, waits=(e_wl,),
                                              inc=1 if last or k == DC // 2 - 1
                                              else 0)
                                wslot_ev[half] = e_mm
                            oevw = (e_mm,)
                            if oc < 2 and a2c["xa"][oc % 2]:
                                oevw = (e_mm, a2c["xa"][oc % 2])
                            ee = act(P, osb[:, oc * TQ:(oc + 1) * TQ], ps[:],
                                     AF.Copy, waits=oevw)
                            ao_slot_ev[oc % 2] = ee
                            oproj_ev[oc] = ee
                        o_tiles = [osb[:, c * TQ:(c + 1) * TQ]
                                   for c in range(DC)]
                        emb, erb = ln_rows_emit(P, o_tiles, oproj_ev, lb2, 1e-5)
                        xa_ev = a2c["xa"]
                        for c in range(DC):
                            xw = list(xguard)
                            if xa_ev[c % 2]:
                                xw.append(xa_ev[c % 2])
                            e_x = dma(P, "scalar", "xin", xa[c % 2][:],
                                      x_in[seq, c * 128:(c + 1) * 128,
                                           t0:t0 + TQ], waits=tuple(xw))
                            vop(P, lambda E, o=o_tiles[c], a=o_tiles[c],
                                b=lb2["mb_ps"][:]: E.tensor_sub(o, a, b),
                                waits=(oproj_ev[c], emb, erb))
                            vop(P, lambda E, o=o_tiles[c], a=o_tiles[c],
                                b=lb2["rb_ps"][:]: E.tensor_mul(o, a, b))
                            e3 = vop(P, lambda E, o=o_tiles[c], a=o_tiles[c],
                                     b=xa[c % 2][:]: E.tensor_add(o, a, b),
                                     waits=(e_x,))
                            xa_ev[c % 2] = e3
                            x1_store_last = dma(
                                P, "scalar", "x1s",
                                x1_s[seq, c * 128:(c + 1) * 128, t0:t0 + TQ],
                                o_tiles[c], waits=(e3,))
                        a2c["rows"] = lb2.get("prev_rows")
                    P.flush()
            if conv_deferred and seq == SEQS - 1:
                while cvs["js"] < cvs["N"]:
                    conv_tick()

    out_store_last = None
    for tt in range(SEQS * T // TQ if stop_after == "ffn" else 0):
        seq, t0 = divmod(tt * TQ, T)
        with ExitStack() as stk:
            P.wait_fence(P.fence())
            tag = f"f{tt}"
            ph = stk.enter_context(
                nc.sbuf_tensor(f"{tag}ph", [128, HC * TQ], BF16))
            pl = stk.enter_context(
                nc.sbuf_tensor(f"{tag}pl", [128, HC * TQ], BF16))
            lb = mk_ln_bufs(stk, tag + "ln")
            s23 = stk.enter_context(ExitStack())
            x1 = [s23.enter_context(
                nc.sbuf_tensor(f"{tag}x{c}", [128, TQ], F32))
                for c in range(DC)]
            xi = s23.enter_context(
                nc.sbuf_tensor(f"{tag}xi", [128, DC * TQ], BF16))
            sb_b = s23.enter_context(
                nc.sbuf_tensor(f"{tag}sb", [128, TQ], F32))
            isv_b = s23.enter_context(
                nc.sbuf_tensor(f"{tag}isv", [128, TQ], F32))

            with ExitStack() as s2:
                _l0 = s2.enter_context(
                    nc.sbuf_tensor(f"{tag}l0", [128, TQ], F32))
                ln3t = [_l0, _l0]
                tmax = s2.enter_context(
                    nc.sbuf_tensor(f"{tag}tm", [128, TQ], F32))
                t_sx = lb["sq0"]
                t_c = lb["sq1"]
                t_r = lb["xh0"]
                t_sg = lb["xl0"]
                mx4 = s2.enter_context(
                    nc.sbuf_tensor(f"{tag}mx4", [128, 4], F32))
                amax_r = s2.enter_context(
                    nc.sbuf_tensor(f"{tag}am", [1, TQ], F32))
                srow2 = s2.enter_context(
                    nc.sbuf_tensor(f"{tag}sr", [1, TQ], F32))
                fbh = lb["mh"]
                fbl = lb["ml"]
                tr_ps = s2.enter_context(
                    nc.psum_tensor(f"{tag}trp", [128, TQ], F32))

                sb_ps = s2.enter_context(
                    nc.psum_tensor(f"{tag}sbp", [128, TQ], F32))
                iv_ps = s2.enter_context(
                    nc.psum_tensor(f"{tag}ivp", [128, TQ], F32))

                xw = (x1_store_last,) if x1_store_last else ()
                e_x = [dma(P, ("scalar", "sync")[c % 2],
                           ("xin", "xinB")[c % 2], x1[c][:],
                           x1_s[seq, c * 128:(c + 1) * 128, t0:t0 + TQ],
                           waits=xw)
                       for c in range(DC)]
                emb, erb = ln_rows_emit(P, [x1[c][:] for c in range(DC)],
                                        e_x, lb, 1e-8)
                e_mx = None
                ln_last = [None, None]
                for c in range(DC):
                    lt = ln3t[c % 2]
                    w1 = [e_x[c], emb, erb]
                    if ln_last[c % 2]:
                        w1.append(ln_last[c % 2])
                    vop(P, lambda E, o=lt[:], a=x1[c][:],
                        b=lb["mb_ps"][:]: E.tensor_sub(o, a, b),
                        waits=tuple(w1))
                    vop(P, lambda E, o=lt[:], a=lt[:],
                        b=lb["rb_ps"][:]: E.tensor_mul(o, a, b))
                    if c == 0:
                        e_mx = vop(P, lambda E, o=tmax[:], a=lt[:],
                                   b=lt[:]: E.scalar_tensor_tensor(
                                       o, a, -1.0, b, ALU.mult, ALU.max))
                    else:
                        e_a = vop(P, lambda E, o=t_sx[:], a=lt[:],
                                  b=lt[:]: E.scalar_tensor_tensor(
                                      o, a, -1.0, b, ALU.mult, ALU.max))
                        e_mx = vop(P, lambda E, o=tmax[:], a=tmax[:],
                                   b=t_sx[:]: E.tensor_max(o, a, b))
                    ln_last[c % 2] = e_mx
                e_red = None
                for j in range(4):
                    e_t = tr(P, tr_ps[:, 0:128], tmax[:, j * 128:(j + 1) * 128],
                             cst_id[:], waits=(e_mx, e_id) +
                             ((e_red,) if e_red else ()))
                    e_red = vop(P, lambda E, o=mx4[:, j:j + 1], i=tr_ps[:, 0:128]:
                                E.reduce_max(o, i, AXX), waits=(e_t,))
                e_t4 = tr(P, tr_ps[0:4, 0:128], mx4[:], cst_id[:], waits=(e_red,))
                e_cp = vop(P, lambda E, o=tmax[:, 0:128], i=tr_ps[:, 0:128]:
                           E.tensor_copy(o, i), waits=(e_t4,))
                e_am = None
                for j in range(4):
                    e_am = dma(P, "sync", "mxs", amax_r[:, j * 128:(j + 1) * 128],
                               tmax[j:j + 1, 0:128], waits=(e_cp,))
                e_rec = vop(P, lambda E, o=srow2[:], i=amax_r[:]:
                            E.reciprocal(o, i), waits=(e_am,))
                e_sr = vop(P, lambda E, o=srow2[:]: E.tensor_scalar_mul(
                    o, o, 127.0), waits=(e_rec,))
                e_ir = vop(P, lambda E, o=amax_r[:]: E.tensor_scalar_mul(
                    o, o, 1.0 / 127.0), waits=(e_rec,))
                e_sb = bcast_r(sb_ps[:], srow2[:], fbh[:], fbl[:],
                               waits=(e_sr,))
                e_sbc = vop(P, lambda E, o=sb_b[:], i=sb_ps[:]:
                            E.tensor_copy(o, i), waits=(e_sb,))
                e_iv = bcast_r(iv_ps[:], amax_r[:], fbh[:], fbl[:],
                               waits=(e_ir, e_sbc))
                e_ivc = vop(P, lambda E, o=isv_b[:], i=iv_ps[:]:
                            E.tensor_copy(o, i), waits=(e_iv,))
                e_xi = [None] * DC
                e_sg_prev = None
                for c in range(DC):
                    xv = xi[:, c * TQ:(c + 1) * TQ]
                    lt = ln3t[c % 2]
                    vop(P, lambda E, o=lt[:], a=x1[c][:], b=lb["mb_ps"][:]:
                        E.tensor_sub(o, a, b))
                    vop(P, lambda E, o=lt[:], a=lt[:], b=lb["rb_ps"][:]:
                        E.tensor_mul(o, a, b))
                    e1 = vop(P, lambda E, o=t_sx[:], a=lt[:], b=sb_b[:]:
                             E.tensor_mul(o, a, b), waits=(e_sbc,))
                    e_sg = act(P, t_sg[:], t_sx[:], AF.Sign,
                               waits=(e1,) + ((e_sg_prev,)
                                              if e_sg_prev else ()))
                    e2 = vop(P, lambda E, o=t_c[:], a=t_sx[:], b=t_sx[:]:
                             E.scalar_tensor_tensor(o, a, -1.0, b, ALU.mult,
                                                    ALU.max))
                    e3 = vop(P, lambda E, o=t_r[:], i=t_c[:]:
                             E.tensor_single_scalar(o, i, 63.5, ALU.is_ge))
                    e3 = vop(P, lambda E, o=t_r[:], i=t_r[:]:
                             E.tensor_scalar_mul(o, i, 64.0))
                    for b in (32, 16, 8, 4, 2, 1):
                        e3 = vop(P, lambda E, o=t_sx[:], a=t_r[:],
                                 bb=float(b), c2=t_c[:]:
                                 E.scalar_tensor_tensor(
                                     o, a, bb - 0.5, c2, ALU.add, ALU.is_le))
                        e3 = vop(P, lambda E, o=t_r[:], a=t_sx[:],
                                 bb=float(b), c2=t_r[:]:
                                 E.scalar_tensor_tensor(
                                     o, a, bb, c2, ALU.mult, ALU.add))
                    e_xi[c] = vop(P, lambda E, o=xv, a=t_r[:], b=t_sg[:]:
                                  E.tensor_mul(o, a, b), waits=(e3, e_sg))
                    e_sg_prev = e_xi[c]

            with ExitStack() as s3:
                wg_b = [s3.enter_context(
                    nc.sbuf_tensor(f"{tag}wg{i}", [128, D // 2], BF16))
                    for i in range(2)]
                wu_b = [s3.enter_context(
                    nc.sbuf_tensor(f"{tag}wu{i}", [128, D // 2], BF16))
                    for i in range(2)]
                gt = lb["sq0"]
                ut = lb["sq1"]
                pt = gt
                g_ps = [s3.enter_context(
                    nc.psum_tensor(f"{tag}gp{i}", [128, TQ], F32))
                    for i in range(2)]
                u_ps = [s3.enter_context(
                    nc.psum_tensor(f"{tag}up{i}", [128, TQ], F32))
                    for i in range(2)]
                wg_ev = [None, None]
                wu_ev = [None, None]
                gp_ev = [None, None]
                up_ev = [None, None]
                tmp_ev = None
                p_evts = []
                s3_guard = tuple(P.fence())
                HD2 = D // 2
                for hc in range(HC):
                    e_g = e_u = None
                    for half in range(2):
                        sl = half
                        ww = ((wg_ev[sl],) if wg_ev[sl] else ()) + \
                            p0_done + \
                            (s3_guard if wg_ev[sl] is None else ())
                        e_wg = dma(P, "sync", "wA" if sl == 0 else "wB",
                                   wg_b[sl][:],
                                   gTs[hc, :, half * HD2:(half + 1) * HD2],
                                   waits=ww)
                        ww = ((wu_ev[sl],) if wu_ev[sl] else ()) + \
                            p0_done + \
                            (s3_guard if wu_ev[sl] is None else ())
                        e_wu = dma(P, "sync", "wC" if sl == 0 else "wD",
                                   wu_b[sl][:],
                                   uTs[hc, :, half * HD2:(half + 1) * HD2],
                                   waits=ww)
                        for k in range(DC // 2):
                            c = half * (DC // 2) + k
                            first, last = (c == 0), (c == DC - 1)
                            gw = [e_wg, e_xi[c]]
                            if first and gp_ev[hc % 2]:
                                gw.append(gp_ev[hc % 2])
                            e_g = mm(P, g_ps[hc % 2][:],
                                     wg_b[sl][:, k * 128:(k + 1) * 128],
                                     xi[:, c * TQ:(c + 1) * TQ], start=first,
                                     stop=last, waits=tuple(gw),
                                     inc=1 if last or k == DC // 2 - 1 else 0)
                        wg_ev[sl] = e_g
                        for k in range(DC // 2):
                            c = half * (DC // 2) + k
                            first, last = (c == 0), (c == DC - 1)
                            uw = [e_wu, e_xi[c]]
                            if first and up_ev[hc % 2]:
                                uw.append(up_ev[hc % 2])
                            e_u = mm(P, u_ps[hc % 2][:],
                                     wu_b[sl][:, k * 128:(k + 1) * 128],
                                     xi[:, c * TQ:(c + 1) * TQ], start=first,
                                     stop=last, waits=tuple(uw),
                                     inc=1 if last or k == DC // 2 - 1 else 0)
                        wu_ev[sl] = e_u
                    psl = hc % 2
                    e1 = vop(P, lambda E, o=gt[:], a=g_ps[psl][:], b=isv_b[:]:
                             E.tensor_mul(o, a, b),
                             waits=(e_g, e_ivc) + ((tmp_ev,) if tmp_ev
                                                   else ()))
                    gp_ev[psl] = e1
                    e2 = act(P, gt[:], gt[:], AF.Silu, waits=(e1,))
                    e3 = vop(P, lambda E, o=ut[:], a=u_ps[psl][:], b=isv_b[:]:
                             E.tensor_mul(o, a, b), waits=(e_u,))
                    up_ev[psl] = e3
                    e4 = vop(P, lambda E, o=pt[:], a=gt[:], b=ut[:]:
                             E.tensor_mul(o, a, b), waits=(e2, e3))
                    hv = ph[:, hc * TQ:(hc + 1) * TQ]
                    lv = pl[:, hc * TQ:(hc + 1) * TQ]
                    e5 = act(P, hv, pt[:], AF.Copy, waits=(e4,))
                    e6 = vop(P, lambda E, o=lv, a=pt[:], b=hv:
                             E.tensor_sub(o, a, b), waits=(e5,))
                    tmp_ev = e6
                    p_evts.append(e6)

            s23.close()
            with ExitStack() as s4:
                QW = HID // 4
                wd_b = [s4.enter_context(
                    nc.sbuf_tensor(f"{tag}wd{i}", [128, QW], BF16))
                    for i in range(2)]
                x1b = [s4.enter_context(
                    nc.sbuf_tensor(f"{tag}xb{i}", [128, TQ], F32))
                    for i in range(2)]
                dsb = s4.enter_context(
                    nc.sbuf_tensor(f"{tag}dsb", [128, DC * TQ], F32))
                d_ps = [s4.enter_context(
                    nc.psum_tensor(f"{tag}dp{i}", [128, TQ], F32))
                    for i in range(2)]
                s4_guard = tuple(P.fence())
                wd_ev = [None, None]
                dp_ev = [None, None]
                d_ev = [None] * DC
                for oc in range(DC):
                    e_d = None
                    for qr in range(4):
                        sl = qr % 2
                        ww = list(p0_done)
                        ww += list(s4_guard)
                        if wd_ev[sl]:
                            ww.append(wd_ev[sl])
                        e_wd = dma(P, "sync", "wE" if sl == 0 else "wF",
                                   wd_b[sl][:],
                                   dTs[oc, :, qr * QW:(qr + 1) * QW],
                                   waits=tuple(ww))
                        for k in range(HC // 4):
                            hc = qr * (HC // 4) + k
                            first = (hc == 0)
                            last = (hc == HC - 1)
                            dw = [e_wd]
                            if first:
                                dw += p_evts
                                if dp_ev[oc % 2]:
                                    dw.append(dp_ev[oc % 2])
                            mm(P, d_ps[oc % 2][:],
                               wd_b[sl][:, k * 128:(k + 1) * 128],
                               ph[:, hc * TQ:(hc + 1) * TQ], start=first,
                               stop=False, waits=tuple(dw))
                            e_d = mm(P, d_ps[oc % 2][:],
                                     wd_b[sl][:, k * 128:(k + 1) * 128],
                                     pl[:, hc * TQ:(hc + 1) * TQ],
                                     start=False, stop=last,
                                     inc=1 if last or k == HC // 4 - 1 else 0)
                        wd_ev[sl] = e_d
                    ee = act(P, dsb[:, oc * TQ:(oc + 1) * TQ], d_ps[oc % 2][:],
                             AF.Copy, waits=(e_d,))
                    dp_ev[oc % 2] = ee
                    d_ev[oc] = ee
                d_tiles = [dsb[:, c * TQ:(c + 1) * TQ] for c in range(DC)]
                lb["_smm"] = {}
                emb, erb = ln_rows_emit(P, d_tiles, d_ev, lb, 1e-5)
                xb_ev = [None, None]
                for c in range(DC):
                    xw = [s for s in s4_guard]
                    if xb_ev[c % 2]:
                        xw.append(xb_ev[c % 2])
                    e_xb = dma(P, "scalar", "xin", x1b[c % 2][:],
                               x1_s[seq, c * 128:(c + 1) * 128, t0:t0 + TQ],
                               waits=tuple(xw))
                    vop(P, lambda E, o=d_tiles[c], a=d_tiles[c],
                        b=lb["mb_ps"][:]: E.tensor_sub(o, a, b),
                        waits=(d_ev[c], emb, erb))
                    vop(P, lambda E, o=d_tiles[c], a=d_tiles[c],
                        b=lb["rb_ps"][:]: E.tensor_mul(o, a, b))
                    e3 = vop(P, lambda E, o=d_tiles[c], a=d_tiles[c],
                             b=x1b[c % 2][:]: E.tensor_add(o, a, b),
                             waits=(e_xb,))
                    xb_ev[c % 2] = e3
                    out_store_last = dma(
                        P, "scalar", "outs",
                        out_hbm[seq, c * 128:(c + 1) * 128, t0:t0 + TQ],
                        d_tiles[c], waits=(e3,))
            P.flush()

    for key in ("outs", "x1s", "qs", "p0sa", "p0sb", "p0sc", "p0sd"):
        if P.counts[key]:
            P.emit("scalar",
                   lambda E, s=P.sems[key], v=P.counts[key]: E.wait_ge(s, v))
    P.flush(final=True)
    return nc


def _layout_inputs(x, q_w, k_w, v_w, o_w, W_g, W_u, W_d):
    def slabify(W, ocs, ics):
        a = W.reshape(ocs, 128, ics, 128)
        a = a.transpose(0, 3, 2, 1)
        return np.ascontiguousarray(a.reshape(ocs, 128, ics * 128))

    qT = slabify(q_w, DC, DC)
    kT = slabify(k_w, 1, DC)
    vT = slabify(v_w, 1, DC)
    oT = slabify(o_w, DC, DC)
    gT = slabify(W_g, HC, DC)
    uT = slabify(W_u, HC, DC)
    dT = slabify(W_d, DC, HC)
    in_maps = []
    for core in range(NCORES):
        xs = x[core * SEQS:(core + 1) * SEQS]
        x_fm = np.ascontiguousarray(xs.transpose(0, 2, 1))
        in_maps.append({"x_fm": x_fm, "qT": qT, "kT": kT, "vT": vT, "oT": oT,
                        "gT": gT, "uT": uT, "dT": dT})
    return in_maps


def kernel(L0_attn_x, L0_attn_q_w, L0_attn_k_w, L0_attn_v_w, L0_attn_o_w,
           L0_ffn_W_g, L0_ffn_W_u, L0_ffn_W_d):
    in_maps = _layout_inputs(
        np.asarray(L0_attn_x, dtype=np.float32),
        np.asarray(L0_attn_q_w, dtype=np.float32),
        np.asarray(L0_attn_k_w, dtype=np.float32),
        np.asarray(L0_attn_v_w, dtype=np.float32),
        np.asarray(L0_attn_o_w, dtype=np.float32),
        np.asarray(L0_ffn_W_g, dtype=np.float32),
        np.asarray(L0_ffn_W_u, dtype=np.float32),
        np.asarray(L0_ffn_W_d, dtype=np.float32))
    nc = build_nc()
    res = run_bass_kernel_spmd(nc, in_maps, core_ids=list(range(NCORES)))
    out = np.empty((NCORES * SEQS, T, D), dtype=np.float32)
    for core in range(NCORES):
        o = res.results[core]["out_fm"]
        out[core * SEQS:(core + 1) * SEQS] = o.transpose(0, 2, 1)
    return out
